# version 04 (final)
# baseline (speedup 1.0000x reference)
import numpy as np
import ml_dtypes
from contextlib import ExitStack
import concourse.bass as bass
import concourse.mybir as mybir
from concourse.bass_utils import run_bass_kernel_spmd

F32 = mybir.dt.float32
BF16 = mybir.dt.bfloat16
U32 = mybir.dt.uint32
AF = mybir.ActivationFunctionType
ALU = mybir.AluOpType
AX = mybir.AxisListType

DBG = False
SKIP_B = False
SKIP_C = False
B_LEVEL = 4
B_NEXP = NEXP if False else 8
NCORES = 8
SEQ = 2048
NTOK = 2 * SEQ
D = 1024
DC = D // 128
NTT = NTOK // 128
TPS = SEQ // 128



class Ctx:
    SEMS = ["dl", "dh", "pe", "dve", "act", "po", "st"]

    def __init__(self, nc, es, block):
        self.nc = nc
        self.es = es
        self.block = block
        self.sem = {}
        self.cnt = {}
        self.epoch = 0
        for name in self.SEMS:
            self.sem[name] = es.enter_context(nc.semaphore("s0_" + name))
            self.cnt[name] = 0
        self.sb = {}

    def inc(self, instr, s, n=1):
        instr.then_inc(self.sem[s], n)
        self.cnt[s] += n
        return self.cnt[s]

    def wait(self, eng, s, val=None):
        if val is None:
            val = self.cnt[s]
        if val > 0:
            eng.wait_ge(self.sem[s], val)

    def sbuf(self, es, name, shape, dt=F32):
        h = es.enter_context(self.nc.sbuf_tensor("sb_" + name, list(shape), dt))
        self.sb[name] = h
        return h

    def fence(self, rotate=True):
        snap = dict(self.cnt)

        allsems = ("pe", "dve", "act", "po", "st", "dl", "dh")

        @self.block.gpsimd
        def _(g):
            for s in allsems:
                self.wait(g, s, snap[s])

        @self.block.sync
        def _(sy):
            for s in allsems:
                self.wait(sy, s, snap[s])

        @self.block.tensor
        def _(t):
            for s in allsems:
                self.wait(t, s, snap[s])

        @self.block.vector
        def _(v):
            for s in allsems:
                self.wait(v, s, snap[s])

        @self.block.scalar
        def _(s_):
            for s in allsems:
                self.wait(s_, s, snap[s])

        if rotate:
            self.epoch += 1
            for name in self.SEMS:
                self.sem[name] = self.es.enter_context(
                    self.nc.semaphore(f"s{self.epoch}_" + name))
                self.cnt[name] = 0



def make_consts():
    c = {}
    c["c_id_bf"] = np.eye(128, dtype=np.float32).astype(ml_dtypes.bfloat16)
    c["c_id_f32"] = np.eye(128, dtype=np.float32)
    c["c_ut_strict"] = np.triu(np.ones((128, 128), np.float32), k=1)
    c["c_ones_col_bf"] = np.ones((128, 1), np.float32).astype(ml_dtypes.bfloat16)
    c["c_ones_row"] = np.ones((1, 128), np.float32)
    i = np.arange(128)[:, None]
    j = np.arange(640)[None, :]
    c["c_mask_l0"] = np.where((j >= i + 1) & (j <= i + 512), 0.0,
                              -1e9).astype(np.float32)
    c["c_causal"] = np.where(np.arange(128)[None, :] <= np.arange(128)[:, None],
                             0.0, -1e9).astype(np.float32)
    sel = np.zeros((8, 128, 16), np.float32)
    for cc in range(8):
        for p in range(128):
            sel[cc, p, (cc * 128 + p) // 64] = 1.0
    c["c_sel64"] = sel.reshape(8 * 128, 16).astype(ml_dtypes.bfloat16)
    selb = np.transpose(sel, (0, 2, 1)).reshape(8 * 16, 128).copy()
    c["c_sel64b"] = selb.astype(ml_dtypes.bfloat16)
    c["c_iota8"] = np.broadcast_to(np.arange(8, dtype=np.float32), (128, 8)).copy()
    selE = np.zeros((64, 8), np.float32)
    for p in range(64):
        selE[p, p % 8] = 1.0
    c["c_selE"] = selE
    return c


CONSTS = make_consts()
CONST_SB = {
    "c_id_bf": ([128, 128], BF16),
    "c_id_f32": ([128, 128], F32),
    "c_ut_strict": ([128, 128], F32),
    "c_ones_col_bf": ([128, 1], BF16),
    "c_ones_row": ([1, 128], F32),
    "c_mask_l0": ([128, 640], F32),
    "c_causal": ([128, 128], F32),
    "c_sel64": ([128, 8, 16], BF16),
    "c_sel64b": ([16, 8, 128], BF16),
    "c_iota8": ([128, 8], F32),
    "c_selE": ([64, 8], F32),
}


def declare_consts(nc):
    aps = {}
    for k, v in CONSTS.items():
        dt = BF16 if v.dtype == ml_dtypes.bfloat16 else F32
        aps[k] = nc.dram_tensor(k, list(v.shape), dt, kind="ExternalInput").ap()
    return aps


def load_consts(ctx, es, caps):
    for name, (shape, dt) in CONST_SB.items():
        ctx.sbuf(es, name, shape, dt)

    @ctx.block.gpsimd
    def _(g):
        for name, (shape, dt) in CONST_SB.items():
            dst = ctx.sb[name]
            if len(shape) == 3:
                src_ap = caps[name].rearrange("(c p) h -> p c h", p=shape[0])
            else:
                src_ap = caps[name]
            ctx.inc(g.dma_start(out=dst[:], in_=src_ap), "dl", 16)



def dump_barrier(ctx):
    stv = ctx.cnt["st"]

    @ctx.block.vector
    def _(v):
        ctx.wait(v, "st", stv)

    @ctx.block.scalar
    def _(s):
        ctx.wait(s, "st", stv)

    @ctx.block.tensor
    def _(t):
        ctx.wait(t, "st", stv)


def emit_stream_load(ctx, es, src_ap):
    xT = ctx.sb["xT"]
    tok = ctx.sbuf(es, "ld_tok", [128, 2, D], BF16)
    slots = []
    for nm in ("psA", "psB", "psC", "psD"):
        slots.append(ctx.ps[nm][:].bitcast(BF16))
    slots.append(ctx.ps["psW0"][:].bitcast(BF16))
    slots.append(ctx.ps["psW1"][:].bitcast(BF16))
    def slot(k):
        k = k % 8
        if k < 4:
            return slots[k][:, 0:128]
        return slots[4 + (k - 4) // 2][:, ((k - 4) % 2) * 1024:((k - 4) % 2) * 1024 + 128]

    tr = {}
    ev = {}
    ld = {}
    for tt in range(NTT):
        buf = tt % 2

        @ctx.block.gpsimd
        def _(g, tt=tt, buf=buf):
            if tt >= 2:
                ctx.wait(g, "pe", tr[(tt - 2) * DC + DC - 1])
            ld[tt] = ctx.inc(g.dma_start(out=tok[:, buf, :],
                                         in_=src_ap[tt * 128:(tt + 1) * 128, :]),
                             "dl", 16)

        @ctx.block.tensor
        def _(t, tt=tt, buf=buf):
            ctx.wait(t, "dl", ld[tt])
            for c in range(DC):
                k = tt * DC + c
                if k >= 8:
                    ctx.wait(t, "dve", ev[k - 8])
                tr[k] = ctx.inc(t.transpose(slot(k), tok[:, buf, c * 128:(c + 1) * 128],
                                            ctx.sb["c_id_bf"][:]), "pe", 1)

        @ctx.block.vector
        def _(v, tt=tt):
            for c in range(DC):
                k = tt * DC + c
                ctx.wait(v, "pe", tr[k])
                ev[k] = ctx.inc(v.tensor_copy(out=xT[:, c, tt * 128:(tt + 1) * 128],
                                              in_=slot(k)), "dve", 1)


def emit_stream_store(ctx, es, dst_ap, src_name="xT"):
    xT = ctx.sb[src_name]
    tok = ctx.sbuf(es, "st_tok", [128, 2, D], BF16)
    slots = []
    for nm in ("psA", "psB", "psC", "psD"):
        slots.append(ctx.ps[nm][:].bitcast(BF16))
    slots.append(ctx.ps["psW0"][:].bitcast(BF16))
    slots.append(ctx.ps["psW1"][:].bitcast(BF16))
    def slot(k):
        k = k % 8
        if k < 4:
            return slots[k][:, 0:128]
        return slots[4 + (k - 4) // 2][:, ((k - 4) % 2) * 1024:((k - 4) % 2) * 1024 + 128]

    start_dve = ctx.cnt["dve"]
    start_act = ctx.cnt["act"]
    tr = {}
    ev = {}
    st = {}
    for tt in range(NTT):
        buf = tt % 2

        @ctx.block.tensor
        def _(t, tt=tt):
            if tt == 0:
                ctx.wait(t, "dve", start_dve)
                ctx.wait(t, "act", start_act)
            for c in range(DC):
                k = tt * DC + c
                if k >= 8:
                    ctx.wait(t, "dve", ev[k - 8])
                tr[k] = ctx.inc(t.transpose(slot(k), xT[:, c, tt * 128:(tt + 1) * 128],
                                            ctx.sb["c_id_bf"][:]), "pe", 1)

        @ctx.block.vector
        def _(v, tt=tt, buf=buf):
            if tt >= 2:
                ctx.wait(v, "st", st[tt - 2])
            for c in range(DC):
                k = tt * DC + c
                ctx.wait(v, "pe", tr[k])
                ev[k] = ctx.inc(v.tensor_copy(out=tok[:, buf, c * 128:(c + 1) * 128],
                                              in_=slot(k)), "dve", 1)

        @ctx.block.gpsimd
        def _(g, tt=tt, buf=buf):
            ctx.wait(g, "dve", ev[tt * DC + DC - 1])
            st[tt] = ctx.inc(g.dma_start(out=dst_ap[tt * 128:(tt + 1) * 128, :],
                                         in_=tok[:, buf, :]), "st", 16)

    @ctx.block.gpsimd
    def _(g):
        ctx.wait(g, "st")



def emit_ln_group(ctx, nw, z_of, out_of, z_ready_dve):
    zbf = ctx.sb["ln_zbf"]
    zsq = ctx.sb["ln_zsq"]
    stats = ctx.sb["ln_stats"]
    psW0 = ctx.ps["psW0"]
    psC = ctx.ps["psC"]
    psD = ctx.ps["psD"]

    zb = {}
    for c in range(DC):
        @ctx.block.vector
        def _(v, c=c):
            if c == 0:
                ctx.wait(v, "dve", z_ready_dve)
            zb[c] = ctx.inc(v.tensor_copy(out=zbf[:, c, 0:nw], in_=z_of(c)), "dve", 1)

    sq = {}
    sqmm = {}
    for c in range(DC):
        @ctx.block.scalar
        def _(s, c=c):
            ctx.wait(s, "dve", zb[c])
            if c >= 2:
                ctx.wait(s, "pe", sqmm[c - 2])
            sq[c] = ctx.inc(s.activation(out=zsq[:, c % 2, 0:nw], in_=zbf[:, c, 0:nw],
                                         func=AF.Square), "act", 1)

        @ctx.block.tensor
        def _(t, c=c):
            ctx.wait(t, "dve", zb[c])
            ctx.inc(t.matmul(psW0[0:1, 0:nw], ctx.sb["c_ones_col_bf"][:],
                             zbf[:, c, 0:nw], start=(c == 0), stop=(c == DC - 1)),
                    "pe", 1)
            ctx.wait(t, "act", sq[c])
            sqmm[c] = ctx.inc(t.matmul(psW0[0:1, 512:512 + nw],
                                       ctx.sb["c_ones_col_bf"][:],
                                       zsq[:, c % 2, 0:nw],
                                       start=(c == 0), stop=(c == DC - 1)),
                              "pe", 1)
    pe_stats = ctx.cnt["pe"]

    @ctx.block.vector
    def _(v):
        ctx.wait(v, "pe", pe_stats)
        ctx.inc(v.tensor_scalar(out=stats[0:1, 0, 0:nw], in0=psW0[0:1, 0:nw],
                                scalar1=1.0 / D, scalar2=None, op0=ALU.mult), "dve", 1)
        ctx.inc(v.tensor_scalar(out=stats[0:1, 1, 0:nw], in0=psW0[0:1, 512:512 + nw],
                                scalar1=1.0 / D, scalar2=None, op0=ALU.mult), "dve", 1)
        ctx.wait(v, "dve")
        ctx.inc(v.scalar_tensor_tensor(out=stats[0:1, 2, 0:nw],
                                       in0=stats[0:1, 0, 0:nw], scalar=-1.0,
                                       in1=stats[0:1, 0, 0:nw],
                                       op0=ALU.mult, op1=ALU.mult), "dve", 1)
        ctx.wait(v, "dve")
        ctx.inc(v.tensor_tensor(out=stats[0:1, 1, 0:nw], in0=stats[0:1, 1, 0:nw],
                                in1=stats[0:1, 2, 0:nw], op=ALU.add), "dve", 1)
        ctx.wait(v, "dve")
        ctx.inc(v.tensor_scalar(out=stats[0:1, 1, 0:nw], in0=stats[0:1, 1, 0:nw],
                                scalar1=1e-5, scalar2=None, op0=ALU.add), "dve", 1)
    dve_var = ctx.cnt["dve"]

    @ctx.block.scalar
    def _(s):
        ctx.wait(s, "dve", dve_var)
        ctx.inc(s.activation(out=stats[0:1, 1, 0:nw], in_=stats[0:1, 1, 0:nw],
                             func=AF.Ln), "act", 1)
        ctx.wait(s, "act")
        ctx.inc(s.activation(out=stats[0:1, 2, 0:nw], in_=stats[0:1, 1, 0:nw],
                             func=AF.Exp, scale=-0.5), "act", 1)
    dve_rstd = ctx.cnt["act"]

    @ctx.block.vector
    def _(v):
        ctx.wait(v, "act", dve_rstd)
    dve_rstd = ctx.cnt["dve"]

    act_rstd = ctx.cnt["act"]

    @ctx.block.tensor
    def _(t):
        ctx.wait(t, "act", act_rstd)
        ctx.inc(t.matmul(psC[:, 0:nw], ctx.sb["c_ones_row"][:],
                         stats[0:1, 0, 0:nw], start=True, stop=True), "pe", 1)
        ctx.inc(t.matmul(psD[:, 0:nw], ctx.sb["c_ones_row"][:],
                         stats[0:1, 2, 0:nw], start=True, stop=True), "pe", 1)
    pe_bc = ctx.cnt["pe"]

    for c in range(DC):
        @ctx.block.vector
        def _(v, c=c):
            if c == 0:
                ctx.wait(v, "pe", pe_bc)
            zc = z_of(c)
            ctx.inc(v.tensor_tensor(out=zc, in0=zc,
                                    in1=psC[:, 0:nw], op=ALU.subtract), "dve", 1)
            ctx.wait(v, "dve")
            ctx.inc(v.tensor_tensor(out=out_of(c), in0=zc,
                                    in1=psD[:, 0:nw], op=ALU.mult), "dve", 1)
    return ctx.cnt["dve"]



def emit_l0_attn(ctx, es, w):
    nc = ctx.nc
    xT = ctx.sb["xT"]
    qkv = ctx.sbuf(es, "l0_qkv", [128, DC, 3072], BF16)
    outk = ctx.sbuf(es, "l0_outk", [128, DC, 1024], BF16)
    qkbias = ctx.sbuf(es, "l0_qkb", [128, 16], F32)
    ln_z = ctx.sbuf(es, "ln_z", [128, DC, 512], F32)
    obias = ctx.sbuf(es, "l0_ob", [128, DC], F32)
    vbias_bc = ctx.sbuf(es, "l0_vbc", [128, 1024], BF16)
    kring = ctx.sbuf(es, "l0_kring", [128, 5, DC, 128], BF16)
    vring = ctx.sbuf(es, "l0_vring", [128, 5, 1024], BF16)
    qtT = ctx.sbuf(es, "l0_qt", [128, DC, 128], BF16)
    pbuf = [ctx.sbuf(es, f"l0_p{i}", [128, 640], BF16) for i in range(2)]
    ptb = [ctx.sbuf(es, f"l0_pt{i}", [128, 5, 128], BF16) for i in range(2)]
    atile = ctx.sbuf(es, "l0_atile", [128, DC, 128], BF16)
    m_max = [ctx.sbuf(es, f"l0_mmax{i}", [128, 1], F32) for i in range(2)]
    m_neg = [ctx.sbuf(es, f"l0_mneg{i}", [128, 1], F32) for i in range(2)]
    m_sum = [ctx.sbuf(es, f"l0_msum{i}", [128, 1], F32) for i in range(2)]
    m_rcp = [ctx.sbuf(es, f"l0_mrcp{i}", [128, 1], F32) for i in range(2)]

    psA, psB, psC, psD = (ctx.ps[n] for n in ("psA", "psB", "psC", "psD"))
    psW0, psW1 = ctx.ps["psW0"], ctx.ps["psW1"]
    psC_bf = psC[:].bitcast(BF16)
    psD_bf = psD[:].bitcast(BF16)

    @ctx.block.gpsimd
    def _(g):
        ctx.inc(g.dma_start(out=qkv[:],
                            in_=w["qkv_kernel"].rearrange("(c p) n -> p c n", p=128)),
                "dl", 16)
        ctx.inc(g.dma_start(out=outk[:],
                            in_=w["out_kernel"].rearrange("(c p) n -> p c n", p=128)),
                "dl", 16)
        ctx.inc(g.dma_start(out=qkbias[:],
                            in_=w["qkv_bias"][0:2048].rearrange("(c p) -> p c", p=128),
                            allow_slow_non_contiguous=True), "dl", 16)
        ctx.inc(g.dma_start(out=vbias_bc[:],
                            in_=bass.AP(w["qkv_bias"].tensor, 2048,
                                        [[0, 128], [1, 1024]])), "dl", 16)
        ctx.inc(g.dma_start(out=obias[:],
                            in_=w["out_bias"].rearrange("(c p) -> p c", p=128),
                            allow_slow_non_contiguous=True), "dl", 16)
    wl = ctx.cnt["dl"]

    last_ln = ctx.cnt["dve"]
    last_pv_ev = ctx.cnt["dve"]
    zgrp_ready = 0

    for tt in range(NTT):
        s, ti = tt // TPS, tt % TPS
        ts0 = s * SEQ
        t0 = tt * 128
        slot = ti % 5
        mm = {}

        @ctx.block.tensor
        def _(t, tt=tt, ti=ti, t0=t0, slot=slot):
            if tt == 0:
                ctx.wait(t, "dl", wl)
            ctx.wait(t, "dve", last_pv_ev)
            for mt in range(DC):
                ps = psC if mt < 4 else psD
                col = (mt % 4) * 128
                for kc in range(DC):
                    ctx.inc(t.matmul(ps[:, col:col + 128],
                                     qkv[:, kc, 1024 + mt * 128:1024 + (mt + 1) * 128],
                                     xT[:, kc, t0:t0 + 128],
                                     start=(kc == 0), stop=(kc == DC - 1)), "pe", 1)
            mm["k"] = ctx.cnt["pe"]
            for nt in range(2):
                for kc in range(DC):
                    ctx.inc(t.matmul(psW0[:, nt * 512:(nt + 1) * 512],
                                     xT[:, kc, t0:t0 + 128],
                                     qkv[:, kc, 2048 + nt * 512:2048 + (nt + 1) * 512],
                                     start=(kc == 0), stop=(kc == DC - 1)), "pe", 1)
            mm["v"] = ctx.cnt["pe"]

        @ctx.block.vector
        def _(v, tt=tt, slot=slot):
            ctx.wait(v, "pe", mm["k"])
            for mt in range(DC):
                ps = psC if mt < 4 else psD
                col = (mt % 4) * 128
                ctx.inc(v.tensor_scalar(out=kring[:, slot, mt, :],
                                        in0=ps[:, col:col + 128],
                                        scalar1=qkbias[:, 8 + mt:9 + mt], scalar2=None,
                                        op0=ALU.add), "dve", 1)
            mm["kev"] = ctx.cnt["dve"]
            ctx.wait(v, "pe", mm["v"])
            for nt in range(2):
                ctx.inc(v.tensor_tensor(out=vring[:, slot, nt * 512:(nt + 1) * 512],
                                        in0=psW0[:, nt * 512:(nt + 1) * 512],
                                        in1=vbias_bc[:, nt * 512:(nt + 1) * 512],
                                        op=ALU.add), "dve", 1)
            mm["vev"] = ctx.cnt["dve"]

        @ctx.block.tensor
        def _(t, tt=tt, t0=t0):
            ctx.wait(t, "dve", mm["kev"])
            for mt in range(DC):
                ps = psC if mt < 4 else psD
                col = (mt % 4) * 128
                for kc in range(DC):
                    ctx.inc(t.matmul(ps[:, col:col + 128],
                                     qkv[:, kc, mt * 128:(mt + 1) * 128],
                                     xT[:, kc, t0:t0 + 128],
                                     start=(kc == 0), stop=(kc == DC - 1)), "pe", 1)
            mm["q"] = ctx.cnt["pe"]

        @ctx.block.vector
        def _(v, tt=tt):
            ctx.wait(v, "pe", mm["q"])
            for mt in range(DC):
                ps = psC if mt < 4 else psD
                col = (mt % 4) * 128
                ctx.inc(v.tensor_scalar(out=qtT[:, mt, :], in0=ps[:, col:col + 128],
                                        scalar1=qkbias[:, mt:mt + 1], scalar2=None,
                                        op0=ALU.add), "dve", 1)
            mm["qev"] = ctx.cnt["dve"]

        if DBG and tt == 17:
            dbg = ctx.nc._dbg
            @ctx.block.gpsimd
            def _(g, slot=slot):
                ctx.wait(g, "dve", mm["qev"])
                ctx.inc(g.dma_start(out=dbg["dbg_q"], in_=qtT[:]), "st", 16)
                ctx.inc(g.dma_start(out=dbg["dbg_k"], in_=kring[:, slot]), "st", 16)
                ctx.inc(g.dma_start(out=dbg["dbg_v"], in_=vring[:, slot]), "st", 16)
            dump_barrier(ctx)

        kt_lo = max(0, ti - 4)
        nk = ti - kt_lo + 1
        mask0 = (4 - (ti - kt_lo)) * 128
        span = nk * 128
        SC = [psW0, psW1]
        PTp = [psC_bf, psD_bf]
        PVp = [psC, psD]
        ev = {}

        def emit_scores(h):
            pi = h % 2

            @ctx.block.tensor
            def _(t, h=h, pi=pi):
                if h < 2:
                    ctx.wait(t, "dve", mm["qev"])
                else:
                    ctx.wait(t, "act", ev[("exp", h - 2)])
                for j in range(nk):
                    kt = kt_lo + j
                    ctx.inc(t.matmul(SC[pi][:, j * 128:(j + 1) * 128],
                                     qtT[hb_of(h):hb_of(h) + 64, h // 2, :],
                                     kring[hb_of(h):hb_of(h) + 64, kt % 5, h // 2, :],
                                     start=True, stop=True), "pe", 1)
            ev[("sc", h)] = ctx.cnt["pe"]

        def hb_of(h):
            return (h % 2) * 64

        def emit_softmax(h):
            pi = h % 2

            @ctx.block.vector
            def _(v, h=h, pi=pi):
                ctx.wait(v, "pe", ev[("sc", h)])
                ctx.inc(v.tensor_tensor(out=SC[pi][:, 0:span], in0=SC[pi][:, 0:span],
                                        in1=ctx.sb["c_mask_l0"][:, mask0:mask0 + span],
                                        op=ALU.add), "dve", 1)
                ctx.wait(v, "dve")
                ctx.inc(v.reduce_max(m_max[pi][:], SC[pi][:, 0:span], axis=AX.X),
                        "dve", 1)
                ctx.wait(v, "dve")
                ctx.inc(v.tensor_scalar(out=m_neg[pi][:], in0=m_max[pi][:],
                                        scalar1=-0.125, scalar2=None, op0=ALU.mult),
                        "dve", 1)
            ev[("msk", h)] = ctx.cnt["dve"]

            @ctx.block.scalar
            def _(s, h=h, pi=pi):
                ctx.wait(s, "dve", ev[("msk", h)])
                ctx.inc(s.activation(out=pbuf[pi][:, 0:span], in_=SC[pi][:, 0:span],
                                     func=AF.Exp, bias=m_neg[pi][:], scale=0.125,
                                     accum_out=m_sum[pi][:]), "act", 1)
            ev[("exp", h)] = ctx.cnt["act"]

            @ctx.block.vector
            def _(v, h=h, pi=pi):
                ctx.wait(v, "act", ev[("exp", h)])
                ctx.inc(v.reciprocal(m_rcp[pi][:], m_sum[pi][:]), "dve", 1)
                ctx.wait(v, "dve")
                ctx.inc(v.tensor_scalar(out=pbuf[pi][:, 0:span], in0=pbuf[pi][:, 0:span],
                                        scalar1=m_rcp[pi][:], scalar2=None,
                                        op0=ALU.mult), "dve", 1)
            ev[("psc", h)] = ctx.cnt["dve"]

        def emit_ptpv(h):
            pi = h % 2

            @ctx.block.tensor
            def _(t, h=h, pi=pi):
                ctx.wait(t, "dve", ev[("psc", h)])
                if h >= 2:
                    ctx.wait(t, "dve", ev[("ptbev", h - 2)])
                    ctx.wait(t, "dve", ev[("ocev", h - 2)])
                for j in range(nk):
                    ctx.inc(t.transpose(PTp[pi][:, j * 128:(j + 1) * 128],
                                        pbuf[pi][:, j * 128:(j + 1) * 128],
                                        ctx.sb["c_id_bf"][:]), "pe", 1)
            ev[("pt", h)] = ctx.cnt["pe"]

            @ctx.block.vector
            def _(v, h=h, pi=pi):
                ctx.wait(v, "pe", ev[("pt", h)])
                ctx.inc(v.tensor_copy(
                    out=ptb[pi][:, 0:nk, :].rearrange("p a b -> p (a b)"),
                    in_=PTp[pi][:, 0:nk * 128]), "dve", 1)
            ev[("ptbev", h)] = ctx.cnt["dve"]

            @ctx.block.tensor
            def _(t, h=h, pi=pi):
                ctx.wait(t, "dve", ev[("ptbev", h)])
                for j in range(nk):
                    kt = kt_lo + j
                    ctx.inc(t.matmul(PVp[pi][0:64, 384:512],
                                     vring[:, kt % 5, h * 64:(h + 1) * 64],
                                     ptb[pi][:, j, :],
                                     start=(j == 0), stop=(j == nk - 1)), "pe", 1)
            ev[("pv", h)] = ctx.cnt["pe"]

            @ctx.block.vector
            def _(v, h=h, pi=pi):
                ctx.wait(v, "pe", ev[("pv", h)])
                ctx.inc(v.tensor_copy(out=atile[hb_of(h):hb_of(h) + 64, h // 2, :],
                                      in_=PVp[pi][0:64, 384:512]), "dve", 1)
            ev[("ocev", h)] = ctx.cnt["dve"]

        emit_scores(0)
        emit_softmax(0)
        for h in range(16):
            if h + 1 < 16:
                emit_scores(h + 1)
                emit_softmax(h + 1)
            emit_ptpv(h)
        last_pv_ev = ev[("ocev", 15)]
        heads_done = ctx.cnt["dve"]
        if DBG and tt == 17:
            dbg = ctx.nc._dbg
            @ctx.block.gpsimd
            def _(g):
                ctx.wait(g, "dve", heads_done)
                ctx.inc(g.dma_start(out=dbg["dbg_at"], in_=atile[:]), "st", 16)
            dump_barrier(ctx)

        @ctx.block.tensor
        def _(t, heads_done=heads_done):
            ctx.wait(t, "dve", heads_done)
            ctx.wait(t, "dve", last_ln)
            for mt in range(DC):
                ps = psA if mt < 4 else psB
                col = (mt % 4) * 128
                for kc in range(DC):
                    ctx.inc(t.matmul(ps[:, col:col + 128],
                                     outk[:, kc, mt * 128:(mt + 1) * 128],
                                     atile[:, kc, :],
                                     start=(kc == 0), stop=(kc == DC - 1)), "pe", 1)
        y_done = ctx.cnt["pe"]

        zc = tt % 4
        @ctx.block.vector
        def _(v, tt=tt, t0=t0, zc=zc, y_done=y_done):
            ctx.wait(v, "pe", y_done)
            z = ctx.sb["ln_z"]
            for c in range(DC):
                ps = psA if c < 4 else psB
                col = (c % 4) * 128
                ctx.inc(v.scalar_tensor_tensor(
                    out=z[:, c, zc * 128:(zc + 1) * 128], in0=ps[:, col:col + 128],
                    scalar=obias[:, c:c + 1], in1=xT[:, c, t0:t0 + 128],
                    op0=ALU.add, op1=ALU.add), "dve", 1)
        last_ln = ctx.cnt["dve"]
        if DBG and tt == 17:
            dbg = ctx.nc._dbg
            @ctx.block.sync
            def _(sy, zc=zc):
                ctx.wait(sy, "dve", last_ln)
                ctx.inc(sy.dma_start(out=dbg["dbg_z"],
                                     in_=ctx.sb["ln_z"][:, :, zc * 128:(zc + 1) * 128]),
                        "st", 16)
            dump_barrier(ctx)

        if zc == 3:
            g0 = tt * 128 - 384
            z = ctx.sb["ln_z"]
            last_ln = emit_ln_group(
                ctx, 512, lambda c: z[:, c, 0:512],
                lambda c: xT[:, c, g0:g0 + 512], last_ln)
            last_pv_ev = last_ln



def emit_l0_ffn(ctx, es, w):
    nc = ctx.nc
    xT = ctx.sb["xT"]
    psA, psB, psC, psD = (ctx.ps[n] for n in ("psA", "psB", "psC", "psD"))
    psW0, psW1 = ctx.ps["psW0"], ctx.ps["psW1"]
    psC_bf = psC[:].bitcast(BF16)

    gu_bf = nc.dram_tensor("gu_bf", [32, 1024, 1024], BF16).ap()
    dn_bf = nc.dram_tensor("dn_bf", [32, 1024, 512], BF16).ap()
    sg_bf = nc.dram_tensor("sg_bf", [1024, 1024], BF16).ap()
    su_bf = nc.dram_tensor("su_bf", [1024, 1024], BF16).ap()
    sd_bf = nc.dram_tensor("sd_bf", [1024, 1024], BF16).ap()
    comb = nc.dram_tensor("comb_dram", [32, NTOK], F32).ap()

    gwbf = ctx.sbuf(es, "f0_gwbf", [32, 1024], BF16)
    gwT = ctx.sbuf(es, "f0_gwT", [128, DC, 32], BF16)
    biasbc = ctx.sbuf(es, "f0_biasbc", [128, 32], F32)
    scores = ctx.sbuf(es, "f0_scores", [128, 32], F32)
    sfc = ctx.sbuf(es, "f0_sfc", [128, 32], F32)
    gs = ctx.sbuf(es, "f0_gs", [128, 8], F32)
    gwork = ctx.sbuf(es, "f0_gwork", [128, 8], F32)
    g01 = ctx.sbuf(es, "f0_g01", [128, 8], F32)
    smask = ctx.sbuf(es, "f0_smask", [128, 8], F32)
    masked = ctx.sbuf(es, "f0_masked", [128, 32], F32)
    mwork = ctx.sbuf(es, "f0_mwork", [128, 32], F32)
    m01 = ctx.sbuf(es, "f0_m01", [128, 32], F32)
    sel = ctx.sbuf(es, "f0_sel", [128, 32], F32)
    selsc = ctx.sbuf(es, "f0_selsc", [128, 32], F32)
    r_mg = ctx.sbuf(es, "f0_rmg", [128, 1], F32)
    r_sum = ctx.sbuf(es, "f0_rsum", [128, 1], F32)
    r_rcp = ctx.sbuf(es, "f0_rrcp", [128, 1], F32)
    r_fac = ctx.sbuf(es, "f0_rfac", [128, 1], F32)
    combT = ctx.sbuf(es, "f0_combT", [32, 2, 128], F32)
    guT = [ctx.sbuf(es, f"f0_guT{i}", [128, DC, 1024], BF16) for i in range(3)]
    dnT = [ctx.sbuf(es, f"f0_dnT{i}", [128, 4, 1024], BF16) for i in range(3)]
    silb = ctx.sbuf(es, "f0_sil", [128, 4, 256], BF16)
    tmpb = ctx.sbuf(es, "f0_tmp", [128, 4, 256], BF16)
    aT = [ctx.sbuf(es, f"f0_aT{i}", [128, 4, 256], BF16) for i in range(2)]
    cbb = [ctx.sbuf(es, f"f0_cb{i}", [128, 256], F32) for i in range(2)]
    acc = ctx.sbuf(es, "f0_acc", [128, DC, 768], F32)

    @ctx.block.gpsimd
    def _(g):
        for e in range(32):
            ctx.inc(g.dma_start(out=gu_bf[e], in_=w["experts_gate_up"][e]), "dl", 16)
            ctx.inc(g.dma_start(out=dn_bf[e], in_=w["experts_down"][e]), "dl", 16)
        ctx.inc(g.dma_start(out=sg_bf, in_=w["shared_gate_w"]), "dl", 16)
        ctx.inc(g.dma_start(out=su_bf, in_=w["shared_up_w"]), "dl", 16)
        ctx.inc(g.dma_start(out=sd_bf, in_=w["shared_down_w"]), "dl", 16)
    d2d_done = ctx.cnt["dl"]

    @ctx.block.gpsimd
    def _(g):
        ctx.inc(g.dma_start(out=gwbf[:], in_=w["gate_weight"]), "dl", 16)
        ctx.inc(g.dma_start(out=biasbc[:],
                            in_=bass.AP(w["e_score_correction_bias"].tensor, 0,
                                        [[0, 128], [1, 32]])), "dl", 16)
    gw_done = ctx.cnt["dl"]

    gw_ev = 0
    for c in range(DC):
        @ctx.block.tensor
        def _(t, c=c, gw_ev=gw_ev):
            if c == 0:
                ctx.wait(t, "dl", gw_done)
            else:
                ctx.wait(t, "dve", gw_ev)
            ctx.inc(t.transpose(psC_bf[:, 0:32], gwbf[0:32, c * 128:(c + 1) * 128],
                                ctx.sb["c_id_bf"][0:32, 0:32]), "pe", 1)
        pe_tr = ctx.cnt["pe"]

        @ctx.block.vector
        def _(v, c=c, pe_tr=pe_tr):
            ctx.wait(v, "pe", pe_tr)
            ctx.inc(v.tensor_copy(out=gwT[:, c, :], in_=psC_bf[:, 0:32]), "dve", 1)
        gw_ev = ctx.cnt["dve"]
    gwT_done = ctx.cnt["dve"]

    def W(v):
        ctx.wait(v, "dve")

    lg_ev = {None: gwT_done}
    st_prev = {0: 0, 1: 0}
    for t in range(NTT):
        par = t % 2
        ps = psA if par == 0 else psB

        @ctx.block.tensor
        def _(tb, t=t, ps=ps, par=par):
            if t == 0:
                ctx.wait(tb, "dve", gwT_done)
                ctx.wait(tb, "dl", gw_done)
            if t >= 2:
                ctx.wait(tb, "act", lg_ev[("sig", t - 2)])
            for kc in range(DC):
                ctx.inc(tb.matmul(ps[:, 0:32], xT[:, kc, t * 128:(t + 1) * 128],
                                  gwT[:, kc, :], start=(kc == 0), stop=(kc == DC - 1)),
                        "pe", 1)
        lg_mm = ctx.cnt["pe"]

        @ctx.block.scalar
        def _(s, t=t, ps=ps, lg_mm=lg_mm):
            ctx.wait(s, "pe", lg_mm)
            if t >= 1:
                ctx.wait(s, "dve", lg_ev[("comb", t - 1)])
            ctx.inc(s.activation(out=scores[:], in_=ps[:, 0:32], func=AF.Tanh,
                                 scale=0.5), "act", 1)
        lg_ev[("sig", t)] = ctx.cnt["act"]

        @ctx.block.vector
        def _(v, t=t):
            ctx.wait(v, "act", lg_ev[("sig", t)])
            ctx.inc(v.tensor_scalar(out=scores[:], in0=scores[:], scalar1=0.5,
                                    scalar2=0.5, op0=ALU.mult, op1=ALU.add), "dve", 1)
            W(v)
            ctx.inc(v.tensor_tensor(out=sfc[:], in0=scores[:], in1=biasbc[:],
                                    op=ALU.add), "dve", 1)
            lg_ev[("sfc", t)] = ctx.cnt["dve"]
            W(v)
            ctx.inc(v.reduce_max(gs[:], sfc.rearrange("p (g i) -> p g i", i=4),
                                 axis=AX.X), "dve", 1)
            W(v)
            ctx.inc(v.tensor_copy(out=gwork[:], in_=gs[:]), "dve", 1)
            for i in range(4):
                W(v)
                ctx.inc(v.reduce_max(r_mg[:], gwork[:], axis=AX.X), "dve", 1)
                if i < 3:
                    W(v)
                    ctx.inc(v.tensor_scalar(out=g01[:], in0=gwork[:], scalar1=r_mg[:],
                                            scalar2=None, op0=ALU.is_ge), "dve", 1)
                    W(v)
                    ctx.inc(v.scalar_tensor_tensor(out=gwork[:], in0=g01[:],
                                                   scalar=-20.0, in1=gwork[:],
                                                   op0=ALU.mult, op1=ALU.add), "dve", 1)
            W(v)
            ctx.inc(v.tensor_scalar(out=smask[:], in0=gs[:], scalar1=r_mg[:],
                                    scalar2=None, op0=ALU.is_ge), "dve", 1)
            W(v)
            ctx.inc(v.tensor_tensor(out=masked.rearrange("p (g i) -> p g i", i=4),
                                    in0=sfc.rearrange("p (g i) -> p g i", i=4),
                                    in1=smask[:].unsqueeze(2).broadcast_to([128, 8, 4]),
                                    op=ALU.mult), "dve", 1)
            W(v)
            ctx.inc(v.tensor_copy(out=mwork[:], in_=masked[:]), "dve", 1)
            for i in range(6):
                W(v)
                ctx.inc(v.reduce_max(r_mg[:], mwork[:], axis=AX.X), "dve", 1)
                if i < 5:
                    W(v)
                    ctx.inc(v.tensor_scalar(out=m01[:], in0=mwork[:], scalar1=r_mg[:],
                                            scalar2=None, op0=ALU.is_ge), "dve", 1)
                    W(v)
                    ctx.inc(v.scalar_tensor_tensor(out=mwork[:], in0=m01[:],
                                                   scalar=-20.0, in1=mwork[:],
                                                   op0=ALU.mult, op1=ALU.add), "dve", 1)
            W(v)
            ctx.inc(v.tensor_scalar(out=sel[:], in0=masked[:], scalar1=r_mg[:],
                                    scalar2=None, op0=ALU.is_ge), "dve", 1)
            W(v)
            ctx.inc(v.tensor_tensor(out=selsc[:], in0=sel[:], in1=scores[:],
                                    op=ALU.mult), "dve", 1)
            W(v)
            ctx.inc(v.reduce_sum(r_sum[:], selsc[:], axis=AX.X), "dve", 1)
            W(v)
            ctx.inc(v.tensor_scalar(out=r_sum[:], in0=r_sum[:], scalar1=1e-20,
                                    scalar2=None, op0=ALU.add), "dve", 1)
            W(v)
            ctx.inc(v.reciprocal(r_rcp[:], r_sum[:]), "dve", 1)
            W(v)
            ctx.inc(v.tensor_scalar(out=r_fac[:], in0=r_rcp[:], scalar1=2.5,
                                    scalar2=None, op0=ALU.mult), "dve", 1)
            W(v)
            ctx.inc(v.tensor_scalar(out=selsc[:], in0=selsc[:], scalar1=r_fac[:],
                                    scalar2=None, op0=ALU.mult), "dve", 1)
        comb_dve = ctx.cnt["dve"]
        lg_ev[("comb", t)] = comb_dve

        @ctx.block.tensor
        def _(tb, t=t, comb_dve=comb_dve, par=par):
            ctx.wait(tb, "dve", comb_dve)
            ctx.wait(tb, "st", st_prev[par])
            ctx.inc(tb.transpose(psD[0:32, 0:128], selsc[:], ctx.sb["c_id_f32"][:]),
                    "pe", 1)
        ct_mm = ctx.cnt["pe"]

        @ctx.block.vector
        def _(v, t=t, par=par, ct_mm=ct_mm):
            ctx.wait(v, "pe", ct_mm)
            ctx.inc(v.tensor_copy(out=combT[:, par, :], in_=psD[0:32, 0:128]), "dve", 1)
        ct_ev = ctx.cnt["dve"]

        @ctx.block.gpsimd
        def _(g, t=t, par=par, ct_ev=ct_ev):
            ctx.wait(g, "dve", ct_ev)
            ctx.inc(g.dma_start(out=comb[:, t * 128:(t + 1) * 128],
                                in_=combT[:, par, :]), "st", 16)
            st_prev[par] = ctx.cnt["st"]
    router_done_dve = ctx.cnt["dve"]
    router_done_st = ctx.cnt["st"]

    NU = 34
    groups = [(0, 768), (768, 768), (1536, 768), (2304, 768), (3072, 768),
              (3840, 256)]
    wload = {}
    slot_luse = {0: 0, 1: 0, 2: 0}

    def load_unit(gi, u):
        slot = u % 3

        @ctx.block.sync
        def _(sy, gi=gi, u=u, slot=slot):
            if gi == 0 and u < 3:
                ctx.wait(sy, "dl", d2d_done)
            ctx.wait(sy, "pe", slot_luse[slot])
            if u < 32:
                for c in range(DC):
                    ctx.inc(sy.dma_start(out=guT[slot][:, c, :],
                                         in_=gu_bf[u][:, c * 128:(c + 1) * 128],
                                         transpose=True), "dh", 16)
                for kc in range(4):
                    ctx.inc(sy.dma_start(out=dnT[slot][:, kc, :],
                                         in_=dn_bf[u][:, kc * 128:(kc + 1) * 128],
                                         transpose=True), "dh", 16)
            else:
                hh = (u - 32) * 512
                for c in range(DC):
                    ctx.inc(sy.dma_start(out=guT[slot][:, c, 0:512],
                                         in_=sg_bf[hh:hh + 512, c * 128:(c + 1) * 128],
                                         transpose=True), "dh", 16)
                    ctx.inc(sy.dma_start(out=guT[slot][:, c, 512:1024],
                                         in_=su_bf[hh:hh + 512, c * 128:(c + 1) * 128],
                                         transpose=True), "dh", 16)
                for kc in range(4):
                    ctx.inc(sy.dma_start(out=dnT[slot][:, kc, :],
                                         in_=sd_bf[:, hh + kc * 128:hh + (kc + 1) * 128],
                                         transpose=True), "dh", 16)
        wload[(gi, u)] = ctx.cnt["dh"]

    cb_prev = {0: router_done_dve, 1: router_done_dve}
    epi_prev = router_done_dve
    evac_prev = router_done_dve
    aT_prev = {0: 0, 1: 0}

    def ps_out(mt):
        ps = (psA, psB, psC, psD)[mt // 2]
        return ps[:, (mt % 2) * 256:(mt % 2) * 256 + 256]

    for gi, (g0, gn) in enumerate(groups):
        ntt = gn // 256
        if gi == 0:
            load_unit(0, 0)
            load_unit(0, 1)
            load_unit(0, 2)
        for p in range(NU // 2):
            u0, u1 = 2 * p, 2 * p + 1
            for tt in range(ntt):
                t0 = g0 + tt * 256
                first_pair = (p == 0)
                epi_ev = {}
                for ei, u in enumerate((u0, u1)):
                    if u < 32:
                        @ctx.block.gpsimd
                        def _(g, u=u, ei=ei, t0=t0):
                            ctx.wait(g, "st", router_done_st)
                            ctx.wait(g, "dve", cb_prev[ei])
                            ctx.inc(g.dma_start(
                                out=cbb[ei][:],
                                in_=bass.AP(comb.tensor, u * NTOK + t0,
                                            [[0, 128], [1, 256]])), "dl", 16)
                        cb_ld = ctx.cnt["dl"]
                    else:
                        cb_ld = 0

                    @ctx.block.tensor
                    def _(tb, u=u, ei=ei, t0=t0, gi=gi):
                        ctx.wait(tb, "dh", wload[(gi, u)])
                        ctx.wait(tb, "dve", epi_prev)
                        slot = u % 3
                        for mt in range(DC):
                            psh = psW0 if mt < 4 else psW1
                            col = (mt % 4) * 256
                            for kc in range(DC):
                                ctx.inc(tb.matmul(
                                    psh[:, col:col + 256],
                                    guT[slot][:, kc, mt * 128:(mt + 1) * 128],
                                    xT[:, kc, t0:t0 + 256],
                                    start=(kc == 0), stop=(kc == DC - 1)), "pe", 1)
                        slot_luse[slot] = max(slot_luse[slot], ctx.cnt["pe"])
                    gu_mm = ctx.cnt["pe"]

                    sil_act = {}
                    for it in range(4):
                        @ctx.block.scalar
                        def _(s, it=it, gu_mm=gu_mm):
                            if it == 0:
                                ctx.wait(s, "pe", gu_mm)
                                ctx.wait(s, "dve", epi_prev)
                            sil_act[it] = ctx.inc(
                                s.activation(out=silb[:, it, :],
                                             in_=psW0[:, it * 256:(it + 1) * 256],
                                             func=AF.Silu), "act", 1)

                    @ctx.block.vector
                    def _(v, u=u, ei=ei, cb_ld=cb_ld):
                        ctx.wait(v, "pe", aT_prev[ei])
                        for it in range(4):
                            ctx.wait(v, "act", sil_act[it])
                            ctx.inc(v.tensor_tensor(
                                out=tmpb[:, it, :], in0=silb[:, it, :],
                                in1=psW1[:, it * 256:(it + 1) * 256], op=ALU.mult),
                                "dve", 1)
                        if u < 32:
                            ctx.wait(v, "dl", cb_ld)
                            for it in range(4):
                                W(v)
                                ctx.inc(v.tensor_tensor(
                                    out=aT[ei][:, it, :], in0=tmpb[:, it, :],
                                    in1=cbb[ei][:], op=ALU.mult), "dve", 1)
                        else:
                            W(v)
                            for it in range(4):
                                ctx.inc(v.tensor_copy(out=aT[ei][:, it, :],
                                                      in_=tmpb[:, it, :]), "dve", 1)
                    epi_ev[ei] = ctx.cnt["dve"]
                    epi_prev = epi_ev[ei]
                    cb_prev[ei] = ctx.cnt["dve"]

                @ctx.block.tensor
                def _(tb, u0=u0, u1=u1, gi=gi):
                    ctx.wait(tb, "dve", epi_ev[1])
                    ctx.wait(tb, "dve", evac_prev)
                    for mt in range(DC):
                        first = True
                        for ei, u in enumerate((u0, u1)):
                            slot = u % 3
                            for kc in range(4):
                                ctx.inc(tb.matmul(
                                    ps_out(mt),
                                    dnT[slot][:, kc, mt * 128:(mt + 1) * 128],
                                    aT[ei][:, kc, :],
                                    start=first, stop=(ei == 1 and kc == 3)), "pe", 1)
                                first = False
                            slot_luse[slot] = max(slot_luse[slot], ctx.cnt["pe"])
                    aT_prev[0] = ctx.cnt["pe"]
                    aT_prev[1] = ctx.cnt["pe"]
                dn_mm = ctx.cnt["pe"]

                if tt == ntt - 1:
                    for un in (u0 + 2, u1 + 2):
                        if un < NU:
                            load_unit(gi, un)
                        elif gi + 1 < len(groups):
                            load_unit(gi + 1, un - NU)

                @ctx.block.vector
                def _(v, t0=t0, dn_mm=dn_mm, first_pair=first_pair):
                    ctx.wait(v, "pe", dn_mm)
                    for mt in range(DC):
                        if first_pair:
                            ctx.inc(v.tensor_tensor(
                                out=acc[:, mt, (t0 % 768):(t0 % 768) + 256],
                                in0=ps_out(mt), in1=xT[:, mt, t0:t0 + 256],
                                op=ALU.add), "dve", 1)
                        else:
                            ctx.inc(v.tensor_tensor(
                                out=acc[:, mt, (t0 % 768):(t0 % 768) + 256],
                                in0=ps_out(mt),
                                in1=acc[:, mt, (t0 % 768):(t0 % 768) + 256],
                                op=ALU.add), "dve", 1)
                evac_prev = ctx.cnt["dve"]

        z_ready = evac_prev
        off = 0
        while off < gn:
            nw = min(512, gn - off)
            ln_done = emit_ln_group(
                ctx, nw,
                (lambda c, off=off, nw=nw: acc[:, c, off:off + nw]),
                (lambda c, g0=g0, off=off, nw=nw: xT[:, c, g0 + off:g0 + off + nw]),
                z_ready)
            off += nw
        epi_prev = ctx.cnt["dve"]
        evac_prev = ctx.cnt["dve"]



def emit_l1_attn(ctx, es, w):
    nc = ctx.nc
    xT = ctx.sb["xT"]
    psA, psB, psC, psD = (ctx.ps[n] for n in ("psA", "psB", "psC", "psD"))
    psW0, psW1 = ctx.ps["psW0"], ctx.ps["psW1"]
    psC_bf = psC[:].bitcast(BF16)
    psD_bf = psD[:].bitcast(BF16)

    wq = ctx.sbuf(es, "a1_wq", [128, DC, 1024], BF16)
    wo = ctx.sbuf(es, "a1_wo", [128, DC, 1024], BF16)
    wkv = ctx.sbuf(es, "a1_wkv", [128, DC, 128], BF16)
    wkp = ctx.sbuf(es, "a1_wkp", [128, 1024], BF16)
    wv = ctx.sbuf(es, "a1_wv", [128, 512], BF16)
    qg = ctx.sbuf(es, "a1_qg", [128, 1], F32)
    kg = ctx.sbuf(es, "a1_kg", [128, 1], F32)
    u0 = ctx.sbuf(es, "a1_u0", [128, 2048], BF16)
    kn = ctx.sbuf(es, "a1_kn", [128, DC, SEQ], BF16)
    vtok = ctx.sbuf(es, "a1_vtok", [128, TPS, 512], BF16)
    sqb = ctx.sbuf(es, "a1_sqb", [128, 2, 512], BF16)
    rtmp = ctx.sbuf(es, "a1_rtmp", [16, 512], F32)
    rstd = ctx.sbuf(es, "a1_rstd", [16, 512], BF16)
    scm = ctx.sbuf(es, "a1_scm", [128, SEQ], F32)
    pbuf = [ctx.sbuf(es, f"a1_p{i}", [128, SEQ], BF16) for i in range(2)]
    ptb = ctx.sbuf(es, "a1_pt", [128, 8, 128], BF16)
    atile = ctx.sbuf(es, "a1_atile", [128, DC, 128], BF16)
    ln_z = ctx.sbuf(es, "a1_lnz", [128, DC, 128], F32)
    m_max = ctx.sbuf(es, "a1_mmax", [128, 1], F32)
    m_neg = ctx.sbuf(es, "a1_mneg", [128, 1], F32)
    m_sum = ctx.sbuf(es, "a1_msum", [128, 1], F32)
    m_rcp = ctx.sbuf(es, "a1_mrcp", [128, 1], F32)

    def qn_ap(mt):
        return u0[:, mt * 128:(mt + 1) * 128]

    def qb_ap(mt):
        return u0[:, 1024 + mt * 128:1024 + (mt + 1) * 128]

    @ctx.block.gpsimd
    def _(g):
        ctx.inc(g.dma_start(out=wq[:],
                            in_=w["Wq"].rearrange("(c p) n -> p c n", p=128)), "dl", 16)
        ctx.inc(g.dma_start(out=wo[:],
                            in_=w["Wo"].rearrange("(c p) n -> p c n", p=128)), "dl", 16)
        ctx.inc(g.dma_start(out=wkv[:],
                            in_=w["Wkv"].rearrange("(c p) n -> p c n", p=128)), "dl", 16)
        ctx.inc(g.dma_start(out=wkp[:], in_=w["wk_pad"]), "dl", 16)
        ctx.inc(g.dma_start(out=wv[:], in_=w["Wv"]), "dl", 16)
        ctx.inc(g.dma_start(out=qg[:], in_=w["qg_col"]), "dl", 16)
        ctx.inc(g.dma_start(out=kg[:], in_=w["kg_col"]), "dl", 16)
    wl = ctx.cnt["dl"]

    @ctx.block.vector
    def _(v):
        ctx.inc(v.memset(atile[:], 0.0), "dve", 1)
    cd_free = ctx.cnt["dve"]
    last_ln = ctx.cnt["dve"]
    w01_free = ctx.cnt["dve"]

    for s in range(2):
        ts0 = s * SEQ
        kv_ev = cd_free
        for nt in range(4):
            @ctx.block.tensor
            def _(t, nt=nt, ts0=ts0, kv_ev=kv_ev):
                if nt == 0 and ts0 == 0:
                    ctx.wait(t, "dl", wl)
                ctx.wait(t, "dve", kv_ev)
                for kc in range(DC):
                    ctx.inc(t.matmul(psC[:, 0:512], wkv[:, kc, :],
                                     xT[:, kc, ts0 + nt * 512:ts0 + (nt + 1) * 512],
                                     start=(kc == 0), stop=(kc == DC - 1)), "pe", 1)
            mmv = ctx.cnt["pe"]

            @ctx.block.vector
            def _(v, nt=nt, mmv=mmv):
                ctx.wait(v, "pe", mmv)
                ctx.inc(v.tensor_copy(out=u0[:, nt * 512:(nt + 1) * 512],
                                      in_=psC[:, 0:512]), "dve", 1)
            kv_ev = ctx.cnt["dve"]
        kv_done = kv_ev

        for nt in range(4):
            sq_act = {}
            smm = {}
            krev = {}
            for mt in range(DC):
                ps = psC if mt % 2 == 0 else psD

                @ctx.block.tensor
                def _(t, nt=nt, mt=mt, ps=ps):
                    if nt == 0 and mt == 0:
                        ctx.wait(t, "dve", kv_done)
                    if mt >= 2:
                        ctx.wait(t, "act", sq_act[mt - 2])
                        ctx.wait(t, "dve", krev[mt - 2])
                    ctx.inc(t.matmul(ps[:, 0:512], wkp[:, mt * 128:(mt + 1) * 128],
                                     u0[:, nt * 512:(nt + 1) * 512],
                                     start=True, stop=True), "pe", 1)
                kr_mm = ctx.cnt["pe"]

                @ctx.block.scalar
                def _(sc_, mt=mt, ps=ps, kr_mm=kr_mm):
                    ctx.wait(sc_, "pe", kr_mm)
                    if mt >= 2:
                        ctx.wait(sc_, "pe", smm[mt - 2])
                    ctx.inc(sc_.activation(out=sqb[:, mt % 2, :], in_=ps[:, 0:512],
                                           func=AF.Square), "act", 1)
                sq_act[mt] = ctx.cnt["act"]

                @ctx.block.tensor
                def _(t, mt=mt):
                    ctx.wait(t, "act", sq_act[mt])
                    ctx.inc(t.matmul(psW0[0:16, 0:512], ctx.sb["c_sel64"][:, mt, :],
                                     sqb[:, mt % 2, :],
                                     start=(mt == 0), stop=(mt == DC - 1)), "pe", 1)
                smm[mt] = ctx.cnt["pe"]

                @ctx.block.vector
                def _(v, nt=nt, mt=mt, ps=ps, kr_mm=kr_mm):
                    ctx.wait(v, "pe", kr_mm)
                    ctx.inc(v.tensor_copy(out=kn[:, mt, nt * 512:(nt + 1) * 512],
                                          in_=ps[:, 0:512]), "dve", 1)
                krev[mt] = ctx.cnt["dve"]
            cd_free = ctx.cnt["dve"]

            @ctx.block.vector
            def _(v, nt=nt):
                ctx.wait(v, "pe", smm[DC - 1])
                ctx.inc(v.tensor_scalar(out=rtmp[:, 0:512], in0=psW0[0:16, 0:512],
                                        scalar1=1.0 / 32, scalar2=1e-6,
                                        op0=ALU.mult, op1=ALU.add), "dve", 1)
            r1 = ctx.cnt["dve"]

            @ctx.block.scalar
            def _(sc_, r1=r1):
                ctx.wait(sc_, "dve", r1)
                ctx.inc(sc_.activation(out=rtmp[:, 0:512], in_=rtmp[:, 0:512],
                                       func=AF.Ln), "act", 1)
                ctx.wait(sc_, "act")
                with nc.allow_low_precision(reason="rms scale in bf16 like other operands"):
                    ctx.inc(sc_.activation(out=rstd[:, 0:512], in_=rtmp[:, 0:512],
                                           func=AF.Exp, scale=-0.5), "act", 1)
            r2 = ctx.cnt["act"]

            @ctx.block.vector
            def _(v, r2=r2):
                ctx.wait(v, "act", r2)
            r3 = ctx.cnt["dve"]

            nrm_ev = {}
            for c in range(DC):
                half = c % 2

                @ctx.block.tensor
                def _(t, c=c, half=half, r2=r2):
                    ctx.wait(t, "act", r2)
                    if c >= 2:
                        ctx.wait(t, "dve", nrm_ev[c - 2])
                    ctx.inc(t.matmul(psW1[:, half * 512:half * 512 + 512],
                                     ctx.sb["c_sel64b"][:, c, :], rstd[:, 0:512],
                                     start=True, stop=True), "pe", 1)
                bc_mm = ctx.cnt["pe"]

                @ctx.block.vector
                def _(v, c=c, nt=nt, half=half, bc_mm=bc_mm):
                    ctx.wait(v, "pe", bc_mm)
                    sl = kn[:, c, nt * 512:(nt + 1) * 512]
                    ctx.inc(v.scalar_tensor_tensor(
                        out=sl, in0=sl, scalar=kg[:],
                        in1=psW1[:, half * 512:half * 512 + 512],
                        op0=ALU.mult, op1=ALU.mult), "dve", 1)
                nrm_ev[c] = ctx.cnt["dve"]
            w01_free = ctx.cnt["dve"]

        vt_ev = {}
        for tk in range(TPS):
            ps = psC if tk % 2 == 0 else psD

            @ctx.block.tensor
            def _(t, tk=tk, ps=ps):
                if tk == 0:
                    ctx.wait(t, "dve", cd_free)
                if tk >= 2:
                    ctx.wait(t, "dve", vt_ev[tk - 2])
                ctx.inc(t.matmul(ps[:, 0:512], u0[:, tk * 128:(tk + 1) * 128],
                                 wv[:], start=True, stop=True), "pe", 1)
            vmm = ctx.cnt["pe"]

            @ctx.block.vector
            def _(v, tk=tk, ps=ps, vmm=vmm):
                ctx.wait(v, "pe", vmm)
                ctx.inc(v.tensor_copy(out=vtok[:, tk, :], in_=ps[:, 0:512]), "dve", 1)
            vt_ev[tk] = ctx.cnt["dve"]
        cd_free = ctx.cnt["dve"]

        for ti in range(TPS):
            t0 = ts0 + ti * 128
            @ctx.block.tensor
            def _(t, t0=t0, ti=ti):
                ctx.wait(t, "dve", cd_free)
                for mt in range(DC):
                    ps = psC if mt < 4 else psD
                    col = (mt % 4) * 128
                    for kc in range(DC):
                        ctx.inc(t.matmul(ps[:, col:col + 128],
                                         wq[:, kc, mt * 128:(mt + 1) * 128],
                                         xT[:, kc, t0:t0 + 128],
                                         start=(kc == 0), stop=(kc == DC - 1)), "pe", 1)
            q_mm = ctx.cnt["pe"]

            sq_act = {}
            smm = {}
            for mt in range(DC):
                ps = psC if mt < 4 else psD
                col = (mt % 4) * 128

                @ctx.block.scalar
                def _(sc_, mt=mt, ps=ps, col=col, q_mm=q_mm):
                    if mt == 0:
                        ctx.wait(sc_, "pe", q_mm)
                        ctx.wait(sc_, "dve", w01_free)
                    if mt >= 2:
                        ctx.wait(sc_, "pe", smm[mt - 2])
                    ctx.inc(sc_.activation(out=sqb[:, mt % 2, 0:128],
                                           in_=ps[:, col:col + 128],
                                           func=AF.Square), "act", 1)
                sq_act[mt] = ctx.cnt["act"]

                @ctx.block.tensor
                def _(t, mt=mt):
                    ctx.wait(t, "act", sq_act[mt])
                    if mt == 0:
                        ctx.wait(t, "dve", w01_free)
                    ctx.inc(t.matmul(psW0[0:16, 0:128], ctx.sb["c_sel64"][:, mt, :],
                                     sqb[:, mt % 2, 0:128],
                                     start=(mt == 0), stop=(mt == DC - 1)), "pe", 1)
                smm[mt] = ctx.cnt["pe"]

            qev = {}
            for mt in range(DC):
                ps = psC if mt < 4 else psD
                col = (mt % 4) * 128

                @ctx.block.vector
                def _(v, mt=mt, ps=ps, col=col, q_mm=q_mm):
                    if mt == 0:
                        ctx.wait(v, "pe", q_mm)
                    ctx.inc(v.tensor_copy(out=qb_ap(mt), in_=ps[:, col:col + 128]),
                            "dve", 1)
                qev[mt] = ctx.cnt["dve"]
            cd_free_q = ctx.cnt["dve"]

            @ctx.block.vector
            def _(v):
                ctx.wait(v, "pe", smm[DC - 1])
                ctx.inc(v.tensor_scalar(out=rtmp[:, 0:128], in0=psW0[0:16, 0:128],
                                        scalar1=1.0 / 64, scalar2=1e-6,
                                        op0=ALU.mult, op1=ALU.add), "dve", 1)
            r1 = ctx.cnt["dve"]

            @ctx.block.scalar
            def _(sc_, r1=r1):
                ctx.wait(sc_, "dve", r1)
                ctx.inc(sc_.activation(out=rtmp[:, 0:128], in_=rtmp[:, 0:128],
                                       func=AF.Ln), "act", 1)
                ctx.wait(sc_, "act")
                with nc.allow_low_precision(reason="rms scale in bf16 like other operands"):
                    ctx.inc(sc_.activation(out=rstd[:, 0:128], in_=rtmp[:, 0:128],
                                           func=AF.Exp, scale=-0.5), "act", 1)
            r2 = ctx.cnt["act"]

            @ctx.block.vector
            def _(v, r2=r2):
                ctx.wait(v, "act", r2)
            r3 = ctx.cnt["dve"]

            @ctx.block.tensor
            def _(t, r2=r2):
                ctx.wait(t, "act", r2)
                for c in range(DC):
                    ctx.inc(t.matmul(psW1[:, c * 128:(c + 1) * 128],
                                     ctx.sb["c_sel64b"][:, c, :], rstd[:, 0:128],
                                     start=True, stop=True), "pe", 1)
            bc_mm = ctx.cnt["pe"]

            @ctx.block.vector
            def _(v, bc_mm=bc_mm):
                ctx.wait(v, "pe", bc_mm)
                for mt in range(DC):
                    ctx.wait(v, "dve")
                    ctx.inc(v.scalar_tensor_tensor(
                        out=qn_ap(mt), in0=qb_ap(mt), scalar=qg[:],
                        in1=psW1[:, mt * 128:(mt + 1) * 128],
                        op0=ALU.mult, op1=ALU.mult), "dve", 1)
            qn_done = ctx.cnt["dve"]
            w01_free = qn_done

            nk = ti + 1
            kspan = nk * 128
            npieces = (kspan + 511) // 512
            slots = [psW0[:, 0:512], psW0[:, 512:1024],
                     psW1[:, 0:512], psW1[:, 512:1024]]
            ev = {"cpall": {-1: qn_done}, "exp": {}, "psc": {}, "ptbev": {},
                  "ocev": {-1: qn_done}, "scp": {}}

            def emit_scores_l1(h):
                hb = (h % 2) * 64
                hc = h // 2

                @ctx.block.tensor
                def _(t, h=h, hb=hb, hc=hc):
                    if h == 0:
                        ctx.wait(t, "dve", qn_done)
                    else:
                        ctx.wait(t, "dve", ev["cpall"][h - 1])
                    for j in range(npieces):
                        n0 = j * 512
                        nn = min(512, kspan - n0)
                        ctx.inc(t.matmul(slots[j % 4][:, 0:nn],
                                         u0[hb:hb + 32, hc * 128:(hc + 1) * 128],
                                         kn[hb:hb + 32, hc, n0:n0 + nn],
                                         start=True, stop=True), "pe", 1)
                        ev["scp"][(h, j)] = ctx.cnt["pe"]

            def emit_softmax_l1(h):
                pi = h % 2

                @ctx.block.vector
                def _(v, h=h):
                    if h >= 1:
                        ctx.wait(v, "act", ev["exp"][h - 1])
                    for j in range(npieces):
                        n0 = j * 512
                        nn = min(512, kspan - n0)
                        ctx.wait(v, "pe", ev["scp"][(h, j)])
                        if n0 + nn == kspan:
                            if nn > 128:
                                ctx.inc(v.tensor_scalar(
                                    out=scm[:, n0:n0 + nn - 128],
                                    in0=slots[j % 4][:, 0:nn - 128],
                                    scalar1=0.125, scalar2=None, op0=ALU.mult),
                                    "dve", 1)
                            ctx.inc(v.scalar_tensor_tensor(
                                out=scm[:, kspan - 128:kspan],
                                in0=slots[j % 4][:, nn - 128:nn], scalar=0.125,
                                in1=ctx.sb["c_causal"][:],
                                op0=ALU.mult, op1=ALU.add), "dve", 1)
                        else:
                            ctx.inc(v.tensor_scalar(
                                out=scm[:, n0:n0 + nn], in0=slots[j % 4][:, 0:nn],
                                scalar1=0.125, scalar2=None, op0=ALU.mult), "dve", 1)
                    ev["cpall"][h] = ctx.cnt["dve"]
                    ctx.wait(v, "dve")
                    ctx.inc(v.reduce_max(m_max[:], scm[:, 0:kspan], axis=AX.X),
                            "dve", 1)
                    ctx.wait(v, "dve")
                    ctx.inc(v.tensor_scalar(out=m_neg[:], in0=m_max[:], scalar1=-1.0,
                                            scalar2=None, op0=ALU.mult), "dve", 1)
                msk = ctx.cnt["dve"]

                @ctx.block.scalar
                def _(s, h=h, pi=pi, msk=msk):
                    ctx.wait(s, "dve", msk)
                    ctx.inc(s.activation(out=pbuf[pi][:, 0:kspan], in_=scm[:, 0:kspan],
                                         func=AF.Exp, bias=m_neg[:], scale=1.0,
                                         accum_out=m_sum[:]), "act", 1)
                ev["exp"][h] = ctx.cnt["act"]

                @ctx.block.vector
                def _(v, h=h, pi=pi):
                    ctx.wait(v, "act", ev["exp"][h])
                    ctx.inc(v.reciprocal(m_rcp[:], m_sum[:]), "dve", 1)
                    ctx.wait(v, "dve")
                    ctx.inc(v.tensor_scalar(out=pbuf[pi][:, 0:kspan],
                                            in0=pbuf[pi][:, 0:kspan],
                                            scalar1=m_rcp[:], scalar2=None,
                                            op0=ALU.mult), "dve", 1)
                ev["psc"][h] = ctx.cnt["dve"]

            def emit_ptpv_l1(h):
                pi = h % 2
                nhalf = (nk + 7) // 8
                for half in range(nhalf):
                    j0 = half * 8
                    jn = min(8, nk - j0)

                    @ctx.block.tensor
                    def _(t, h=h, pi=pi, half=half, j0=j0, jn=jn):
                        if half == 0:
                            ctx.wait(t, "dve", ev["psc"][h])
                            ctx.wait(t, "dve", ev["ptbev"].get(h - 1, qn_done))
                        else:
                            ctx.wait(t, "dve", ev["ptbev"][h])
                        for j in range(jn):
                            ctx.inc(t.transpose(psC_bf[:, j * 128:(j + 1) * 128],
                                                pbuf[pi][:, (j0 + j) * 128:
                                                          (j0 + j + 1) * 128],
                                                ctx.sb["c_id_bf"][:]), "pe", 1)
                    ptm = ctx.cnt["pe"]

                    @ctx.block.vector
                    def _(v, ptm=ptm, jn=jn):
                        ctx.wait(v, "pe", ptm)
                        ctx.inc(v.tensor_copy(
                            out=ptb[:, 0:jn, :].rearrange("p a b -> p (a b)"),
                            in_=psC_bf[:, 0:jn * 128]), "dve", 1)
                    ev["ptbev"][h] = ctx.cnt["dve"]

                    @ctx.block.tensor
                    def _(t, h=h, half=half, j0=j0, jn=jn):
                        ctx.wait(t, "dve", ev["ptbev"][h])
                        if half == 0:
                            ctx.wait(t, "dve", ev["ocev"][h - 1])
                        for j in range(jn):
                            ctx.inc(t.matmul(psD[0:32, 384:512],
                                             vtok[:, j0 + j, h * 32:(h + 1) * 32],
                                             ptb[:, j, :],
                                             start=(half == 0 and j == 0),
                                             stop=(half == nhalf - 1 and j == jn - 1)),
                                    "pe", 1)
                pv = ctx.cnt["pe"]

                @ctx.block.vector
                def _(v, h=h, pv=pv):
                    ctx.wait(v, "pe", pv)
                    ctx.inc(v.tensor_copy(out=atile[(h % 2) * 64:(h % 2) * 64 + 32,
                                                    h // 2, :],
                                          in_=psD[0:32, 384:512]), "dve", 1)
                ev["ocev"][h] = ctx.cnt["dve"]

            emit_scores_l1(0)
            emit_softmax_l1(0)
            for h in range(16):
                if h + 1 < 16:
                    emit_scores_l1(h + 1)
                    emit_softmax_l1(h + 1)
                emit_ptpv_l1(h)
            scm_prev = ev["ocev"][15]
            heads_done = ctx.cnt["dve"]
            w01_free = heads_done
            cd_free = heads_done

            @ctx.block.tensor
            def _(t, heads_done=heads_done):
                ctx.wait(t, "dve", heads_done)
                ctx.wait(t, "dve", last_ln)
                for mt in range(DC):
                    ps = psA if mt < 4 else psB
                    col = (mt % 4) * 128
                    for kc in range(DC):
                        ctx.inc(t.matmul(ps[:, col:col + 128],
                                         wo[:, kc, mt * 128:(mt + 1) * 128],
                                         atile[:, kc, :],
                                         start=(kc == 0), stop=(kc == DC - 1)), "pe", 1)
            y_done = ctx.cnt["pe"]

            @ctx.block.vector
            def _(v, t0=t0, y_done=y_done):
                ctx.wait(v, "pe", y_done)
                for c in range(DC):
                    ps = psA if c < 4 else psB
                    col = (c % 4) * 128
                    ctx.inc(v.tensor_tensor(
                        out=ln_z[:, c, 0:128],
                        in0=ps[:, col:col + 128], in1=xT[:, c, t0:t0 + 128],
                        op=ALU.add), "dve", 1)
            last_ln = ctx.cnt["dve"]

            ln_done = emit_ln_group(
                ctx, 128, (lambda c: ln_z[:, c, 0:128]),
                (lambda c, t0=t0: xT[:, c, t0:t0 + 128]), last_ln)
            last_ln = ln_done
            cd_free = ln_done
            w01_free = ln_done



CAP = 1536
GCAP = 10240
NEXP = 8


def emit_l1_ffn_a(ctx, es, w, dram, g_mode="consume", g_ap=None):
    nc = ctx.nc
    xT = ctx.sb["xT"]
    psA, psB, psC, psD = (ctx.ps[n] for n in ("psA", "psB", "psC", "psD"))
    psC_bf = psC[:].bitcast(BF16)
    slot0 = ctx.sb["slot0"]
    slot1 = ctx.sb["slot1"]

    g_locT = g_ap if g_mode == "produce" else None
    g_all = g_ap if g_mode == "consume" else None
    LCAP = GCAP // NCORES

    rg = ctx.sbuf(es, "f1_rg", [128, DC, 8], BF16)
    probs = ctx.sbuf(es, "f1_probs", [128, 8], F32)
    mk2 = ctx.sbuf(es, "f1_mk2", [128, 8], F32)
    oh1 = ctx.sbuf(es, "f1_oh1", [128, 8], F32)
    oh2 = ctx.sbuf(es, "f1_oh2", [128, 8], F32)
    tmp8 = ctx.sbuf(es, "f1_tmp8", [128, 8], F32)
    gt8 = ctx.sbuf(es, "f1_gt8", [128, 8], F32)
    m1 = ctx.sbuf(es, "f1_m1", [128, 1], F32)
    m2 = ctx.sbuf(es, "f1_m2", [128, 1], F32)
    mneg = ctx.sbuf(es, "f1_mneg", [128, 1], F32)
    msum = ctx.sbuf(es, "f1_msum", [128, 1], F32)
    mrcp = ctx.sbuf(es, "f1_mrcp", [128, 1], F32)
    g1 = ctx.sbuf(es, "f1_g1", [128, 1], F32)
    g2 = ctx.sbuf(es, "f1_g2", [128, 1], F32)
    ek = ctx.sbuf(es, "f1_ek", [128, 1], F32)
    rank = ctx.sbuf(es, "f1_rank", [128, 1], F32)
    ovf = ctx.sbuf(es, "f1_ovf", [128, 1], F32)
    keep = ctx.sbuf(es, "f1_keep", [128, 1], F32)
    mk = ctx.sbuf(es, "f1_mk", [128, 1], F32)
    mkn = ctx.sbuf(es, "f1_mkn", [128, 1], F32)
    thtok = ctx.sbuf(es, "f1_thtok", [128, 1], F32)
    slotf = ctx.sbuf(es, "f1_slotf", [128, 1], F32)
    base0 = ctx.sbuf(es, "f1_base0", [1, 8], F32)
    base1 = ctx.sbuf(es, "f1_base1", [1, 8], F32)
    n0row = ctx.sbuf(es, "f1_n0row", [1, 8], F32)
    n0bc = ctx.sbuf(es, "f1_n0bc", [128, 8], F32)
    thbc = ctx.sbuf(es, "f1_thbc", [128, 8], F32)
    onescol = ctx.sbuf(es, "f1_onescol", [128, 1], F32)
    oh1_all = ctx.sbuf(es, "f1_oh1a", [128, NTT, 8], F32)
    oh2_all = ctx.sbuf(es, "f1_oh2a", [128, NTT, 8], F32)
    g1_all = ctx.sbuf(es, "f1_g1a", [128, NTT], F32)
    g2_all = ctx.sbuf(es, "f1_g2a", [128, NTT], F32)
    grow = ctx.sbuf(es, "f1_grow", [8, 2, 128], F32)
    if g_mode == "consume":
        gsb = ctx.sbuf(es, "f1_gsb", [64, NTOK], F32)
        cmp_bf = ctx.sbuf(es, "f1_cmp", [64, NTOK], BF16)
    elif g_mode == "local":
        gsb = ctx.sbuf(es, "f1_gsb", [8, NTOK], F32)
        cmp_bf = ctx.sbuf(es, "f1_cmp", [8, NTOK], BF16)
        th8 = ctx.sbuf(es, "f1_th8", [8, 1], F32)
        lo8 = ctx.sbuf(es, "f1_lo8", [8, 1], F32)
        hi8 = ctx.sbuf(es, "f1_hi8", [8, 1], F32)
        cnt8 = ctx.sbuf(es, "f1_cnt8", [8, 1], F32)
        ov8 = ctx.sbuf(es, "f1_ov8", [8, 1], F32)
        ov8n = ctx.sbuf(es, "f1_ov8n", [8, 1], F32)
        sc8 = ctx.sbuf(es, "f1_sc8", [8, 1], F32)
        throw_ = ctx.sbuf(es, "f1_throw", [1, 8], F32)
    cnt_p = ctx.sbuf(es, "f1_cntp", [64, 1], F32)
    th = ctx.sbuf(es, "f1_th", [1, 8], F32)
    lo = ctx.sbuf(es, "f1_lo", [1, 8], F32)
    hi = ctx.sbuf(es, "f1_hi", [1, 8], F32)
    ovr = ctx.sbuf(es, "f1_ovr", [1, 8], F32)
    ovrn = ctx.sbuf(es, "f1_ovrn", [1, 8], F32)
    trow = ctx.sbuf(es, "f1_trow", [1, 8], F32)
    tokb = ctx.sbuf(es, "f1_tokb", [128, 2, D], BF16)
    xsc = ctx.sbuf(es, "f1_xsc", [128, 2, D], BF16)

    @ctx.block.gpsimd
    def _(g):
        ctx.inc(g.dma_start(out=rg[:],
                            in_=w["router_gate"].rearrange("(c p) n -> p c n", p=128)),
                "dl", 16)
    wl = ctx.cnt["dl"]

    @ctx.block.vector
    def _(v):
        ctx.inc(v.memset(base0[:], 0.0), "dve", 1)
        ctx.inc(v.memset(base1[:], 0.0), "dve", 1)
        ctx.inc(v.memset(tokb[:, 0, :], 0.0), "dve", 1)
        if g_mode == "consume":
            ctx.inc(v.memset(lo[:], -1.0), "dve", 1)
            ctx.inc(v.memset(hi[:], 1.0), "dve", 1)
            ctx.inc(v.memset(th[:], 0.0), "dve", 1)
        elif g_mode == "local":
            ctx.inc(v.memset(lo8[:], -1.0), "dve", 1)
            ctx.inc(v.memset(hi8[:], 1.0), "dve", 1)
        ctx.inc(v.tensor_copy(out=onescol[:], in_=ctx.sb["c_ones_col_bf"][:]), "dve", 1)
    init_dve = ctx.cnt["dve"]

    if g_mode in ("consume", "local"):
        @ctx.block.gpsimd
        def _(g):
            ctx.wait(g, "dve", init_dve)
            ctx.inc(g.dma_start(out=dram["ye"][NEXP * CAP:NEXP * CAP + 1, :],
                                in_=tokb[0:1, 0, :]), "st", 16)

    def W(v):
        ctx.wait(v, "dve")

    lg_free = {0: init_dve, 1: init_dve}
    gr_free = {0: init_dve, 1: init_dve}
    gst = {}
    for t in range(NTT):
        par = t % 2
        ps = psA if par == 0 else psB

        @ctx.block.tensor
        def _(tb, t=t, ps=ps, par=par):
            if t == 0:
                ctx.wait(tb, "dl", wl)
            ctx.wait(tb, "dve", lg_free[par])
            for kc in range(DC):
                ctx.inc(tb.matmul(ps[:, 0:8], xT[:, kc, t * 128:(t + 1) * 128],
                                  rg[:, kc, :], start=(kc == 0), stop=(kc == DC - 1)),
                        "pe", 1)
        lg_mm = ctx.cnt["pe"]

        @ctx.block.vector
        def _(v, t=t, ps=ps, lg_mm=lg_mm):
            ctx.wait(v, "pe", lg_mm)
            ctx.inc(v.reduce_max(m1[:], ps[:, 0:8], axis=AX.X), "dve", 1)
            W(v)
            ctx.inc(v.tensor_scalar(out=mneg[:], in0=m1[:], scalar1=-1.0,
                                    scalar2=None, op0=ALU.mult), "dve", 1)
        sm1 = ctx.cnt["dve"]

        @ctx.block.scalar
        def _(s, ps=ps, sm1=sm1):
            ctx.wait(s, "dve", sm1)
            ctx.inc(s.activation(out=probs[:], in_=ps[:, 0:8], func=AF.Exp,
                                 bias=mneg[:], scale=1.0, accum_out=msum[:]), "act", 1)
        sm2 = ctx.cnt["act"]

        @ctx.block.vector
        def _(v, t=t, sm2=sm2):
            ctx.wait(v, "act", sm2)
            ctx.inc(v.reciprocal(mrcp[:], msum[:]), "dve", 1)
            W(v)
            ctx.inc(v.tensor_scalar(out=probs[:], in0=probs[:], scalar1=mrcp[:],
                                    scalar2=None, op0=ALU.mult), "dve", 1)
            W(v)
            ctx.inc(v.reduce_max(m1[:], probs[:], axis=AX.X), "dve", 1)
            W(v)
            ctx.inc(v.tensor_scalar(out=oh1[:], in0=probs[:], scalar1=m1[:],
                                    scalar2=None, op0=ALU.is_ge), "dve", 1)
            W(v)
            ctx.inc(v.scalar_tensor_tensor(out=mk2[:], in0=oh1[:], scalar=-10.0,
                                           in1=probs[:], op0=ALU.mult, op1=ALU.add),
                    "dve", 1)
            W(v)
            ctx.inc(v.reduce_max(m2[:], mk2[:], axis=AX.X), "dve", 1)
            W(v)
            ctx.inc(v.tensor_scalar(out=oh2[:], in0=mk2[:], scalar1=m2[:],
                                    scalar2=None, op0=ALU.is_ge), "dve", 1)
            W(v)
            ctx.inc(v.tensor_copy(out=oh1_all[:, t, :], in_=oh1[:]), "dve", 1)
            ctx.inc(v.tensor_copy(out=oh2_all[:, t, :], in_=oh2[:]), "dve", 1)
            ctx.inc(v.tensor_tensor(out=g1[:], in0=m1[:], in1=m2[:], op=ALU.add),
                    "dve", 1)
            W(v)
            ctx.inc(v.tensor_scalar(out=g1[:], in0=g1[:], scalar1=1e-9, scalar2=None,
                                    op0=ALU.add), "dve", 1)
            W(v)
            ctx.inc(v.reciprocal(g2[:], g1[:]), "dve", 1)
            W(v)
            ctx.inc(v.tensor_tensor(out=g1[:], in0=m1[:], in1=g2[:], op=ALU.mult),
                    "dve", 1)
            W(v)
            ctx.inc(v.tensor_tensor(out=g2[:], in0=m2[:], in1=g2[:], op=ALU.mult),
                    "dve", 1)
            W(v)
            ctx.inc(v.tensor_copy(out=g1_all[:, t:t + 1], in_=g1[:]), "dve", 1)
            ctx.inc(v.tensor_copy(out=g2_all[:, t:t + 1], in_=g2[:]), "dve", 1)
            ctx.inc(v.tensor_scalar(out=tmp8[:], in0=oh1[:], scalar1=g1[:],
                                    scalar2=None, op0=ALU.mult), "dve", 1)
            ctx.inc(v.tensor_scalar(out=gt8[:], in0=oh2[:], scalar1=g2[:],
                                    scalar2=None, op0=ALU.mult), "dve", 1)
            W(v)
            ctx.inc(v.tensor_tensor(out=gt8[:], in0=gt8[:], in1=tmp8[:], op=ALU.add),
                    "dve", 1)
        gt_dve = ctx.cnt["dve"]
        lg_free[par] = gt_dve

        if g_mode == "produce":
            @ctx.block.tensor
            def _(tb, t=t, par=par, gt_dve=gt_dve):
                ctx.wait(tb, "dve", gt_dve)
                ctx.wait(tb, "st", gst.get(t - 2, 0))
                ctx.inc(tb.transpose(psD[0:8, 0:128], gt8[:], ctx.sb["c_id_f32"][:]),
                        "pe", 1)
            gt_mm = ctx.cnt["pe"]

            @ctx.block.vector
            def _(v, t=t, par=par, gt_mm=gt_mm):
                ctx.wait(v, "pe", gt_mm)
                ctx.inc(v.tensor_copy(out=grow[:, par, :], in_=psD[0:8, 0:128]),
                        "dve", 1)
            gr_ev = ctx.cnt["dve"]

            @ctx.block.gpsimd
            def _(g, t=t, par=par, gr_ev=gr_ev):
                ctx.wait(g, "dve", gr_ev)
                ctx.inc(g.dma_start(out=g_locT[:, t * 128:(t + 1) * 128],
                                    in_=grow[:, par, :]), "st", 16)
                gst[t] = ctx.cnt["st"]
        elif g_mode == "local":
            @ctx.block.tensor
            def _(tb, t=t, gt_dve=gt_dve):
                ctx.wait(tb, "dve", gt_dve)
                ctx.wait(tb, "dve", gst.get(t - 1, 0))
                ctx.inc(tb.transpose(psD[0:8, 0:128], gt8[:], ctx.sb["c_id_f32"][:]),
                        "pe", 1)
            gt_mm = ctx.cnt["pe"]

            @ctx.block.vector
            def _(v, t=t, gt_mm=gt_mm):
                ctx.wait(v, "pe", gt_mm)
                ctx.inc(v.tensor_copy(out=gsb[:, t * 128:(t + 1) * 128],
                                      in_=psD[0:8, 0:128]), "dve", 1)
                gst[t] = ctx.cnt["dve"]
    pass1_st = ctx.cnt["st"]
    if g_mode == "produce":
        @ctx.block.gpsimd
        def _(g):
            ctx.wait(g, "st", pass1_st)
        return

    if g_mode == "local":
        gates_ready = ctx.cnt["dve"]
        for it in range(30):
            @ctx.block.vector
            def _(v, it=it):
                if it == 0:
                    ctx.wait(v, "dve", gates_ready)
                ctx.inc(v.tensor_tensor(out=th8[:], in0=lo8[:], in1=hi8[:],
                                        op=ALU.add), "dve", 1)
                W(v)
                ctx.inc(v.tensor_scalar(out=th8[:], in0=th8[:], scalar1=0.5,
                                        scalar2=None, op0=ALU.mult), "dve", 1)
                W(v)
                ctx.inc(v.tensor_scalar(out=cmp_bf[:], in0=gsb[:], scalar1=th8[:],
                                        scalar2=None, op0=ALU.is_gt), "dve", 1)
                W(v)
                ctx.inc(v.reduce_sum(cnt8[:], cmp_bf[:], axis=AX.X), "dve", 1)
                W(v)
                ctx.inc(v.tensor_scalar(out=ov8[:], in0=cnt8[:],
                                        scalar1=float(LCAP) + 0.5, scalar2=None,
                                        op0=ALU.is_ge), "dve", 1)
                W(v)
                ctx.inc(v.tensor_scalar(out=ov8n[:], in0=ov8[:], scalar1=-1.0,
                                        scalar2=1.0, op0=ALU.mult, op1=ALU.add),
                        "dve", 1)
                W(v)
                ctx.inc(v.tensor_tensor(out=lo8[:], in0=lo8[:], in1=ov8n[:],
                                        op=ALU.mult), "dve", 1)
                ctx.inc(v.tensor_tensor(out=sc8[:], in0=th8[:], in1=ov8[:],
                                        op=ALU.mult), "dve", 1)
                W(v)
                ctx.inc(v.tensor_tensor(out=lo8[:], in0=lo8[:], in1=sc8[:],
                                        op=ALU.add), "dve", 1)
                ctx.inc(v.tensor_tensor(out=hi8[:], in0=hi8[:], in1=ov8[:],
                                        op=ALU.mult), "dve", 1)
                W(v)
                ctx.inc(v.tensor_tensor(out=sc8[:], in0=th8[:], in1=ov8n[:],
                                        op=ALU.mult), "dve", 1)
                W(v)
                ctx.inc(v.tensor_tensor(out=hi8[:], in0=hi8[:], in1=sc8[:],
                                        op=ALU.add), "dve", 1)
        bis8 = ctx.cnt["dve"]

        @ctx.block.tensor
        def _(tb, bis8=bis8):
            ctx.wait(tb, "dve", bis8)
            ctx.inc(tb.transpose(psD[0:1, 0:8], hi8[:], ctx.sb["c_id_f32"][0:8, 0:8]),
                    "pe", 1)
        tr8 = ctx.cnt["pe"]

        @ctx.block.vector
        def _(v, tr8=tr8):
            ctx.wait(v, "pe", tr8)
            ctx.inc(v.tensor_copy(out=throw_[:], in_=psD[0:1, 0:8]), "dve", 1)
        th_dve2 = ctx.cnt["dve"]

        @ctx.block.tensor
        def _(tb, th_dve2=th_dve2):
            ctx.wait(tb, "dve", th_dve2)
            ctx.inc(tb.matmul(psD[:, 0:8], ctx.sb["c_ones_row"][:], throw_[:],
                              start=True, stop=True), "pe", 1)
        thbc_mm2 = ctx.cnt["pe"]

        @ctx.block.vector
        def _(v, thbc_mm2=thbc_mm2):
            ctx.wait(v, "pe", thbc_mm2)
            ctx.inc(v.tensor_copy(out=thbc[:], in_=psD[:, 0:8]), "dve", 1)
    else:
        @ctx.block.gpsimd
        def _(g):
            ctx.inc(g.dma_start(out=gsb[:], in_=g_all), "dl", 16)
        ga_ld = ctx.cnt["dl"]

    if g_mode == "consume":
        NIT_BIS = 30
        for it in range(NIT_BIS):
            @ctx.block.vector
            def _(v, it=it):
                if it == 0:
                    ctx.wait(v, "dl", ga_ld)
                ctx.inc(v.tensor_tensor(out=trow[:], in0=lo[:], in1=hi[:], op=ALU.add),
                        "dve", 1)
                W(v)
                ctx.inc(v.tensor_scalar(out=trow[:], in0=trow[:], scalar1=0.5,
                                        scalar2=None, op0=ALU.mult), "dve", 1)
            th_dve = ctx.cnt["dve"]

            @ctx.block.tensor
            def _(tb, it=it, th_dve=th_dve):
                ctx.wait(tb, "dve", th_dve)
                ctx.inc(tb.matmul(psD[0:64, 0:8], ctx.sb["c_ones_row"][:, 0:64], trow[:],
                                  start=True, stop=True), "pe", 1)
            bc_mm = ctx.cnt["pe"]

            @ctx.block.vector
            def _(v, it=it, bc_mm=bc_mm):
                ctx.wait(v, "pe", bc_mm)
                ctx.inc(v.tensor_tensor(out=tmp8[0:64, :],
                                        in0=ctx.sb["c_selE"][:], in1=psD[0:64, 0:8],
                                        op=ALU.mult), "dve", 1)
                W(v)
                ctx.inc(v.reduce_sum(cnt_p[:], tmp8[0:64, :], axis=AX.X), "dve", 1)
                W(v)
                ctx.inc(v.tensor_scalar(out=cmp_bf[:], in0=gsb[:], scalar1=cnt_p[:],
                                        scalar2=None, op0=ALU.is_gt), "dve", 1)
                W(v)
                ctx.inc(v.reduce_sum(cnt_p[:], cmp_bf[:], axis=AX.X), "dve", 1)
            cnt_dve = ctx.cnt["dve"]

            @ctx.block.tensor
            def _(tb, it=it, cnt_dve=cnt_dve):
                ctx.wait(tb, "dve", cnt_dve)
                ctx.inc(tb.matmul(psD[0:1, 8:16], cnt_p[:], ctx.sb["c_selE"][:],
                                  start=True, stop=True), "pe", 1)
            cc_mm = ctx.cnt["pe"]

            @ctx.block.vector
            def _(v, it=it, cc_mm=cc_mm):
                ctx.wait(v, "pe", cc_mm)
                ctx.inc(v.tensor_scalar(out=ovr[:], in0=psD[0:1, 8:16],
                                        scalar1=float(GCAP) + 0.5, scalar2=None,
                                        op0=ALU.is_ge), "dve", 1)
                W(v)
                ctx.inc(v.tensor_scalar(out=ovrn[:], in0=ovr[:], scalar1=-1.0,
                                        scalar2=1.0, op0=ALU.mult, op1=ALU.add), "dve", 1)
                W(v)
                ctx.inc(v.tensor_tensor(out=lo[:], in0=lo[:], in1=ovrn[:], op=ALU.mult),
                        "dve", 1)
                W(v)
                ctx.inc(v.tensor_tensor(out=tmp8[0:1, :], in0=trow[:], in1=ovr[:],
                                        op=ALU.mult), "dve", 1)
                W(v)
                ctx.inc(v.tensor_tensor(out=lo[:], in0=lo[:], in1=tmp8[0:1, :],
                                        op=ALU.add), "dve", 1)
                ctx.inc(v.tensor_tensor(out=hi[:], in0=hi[:], in1=ovr[:], op=ALU.mult),
                        "dve", 1)
                W(v)
                ctx.inc(v.tensor_tensor(out=tmp8[0:1, :], in0=trow[:], in1=ovrn[:],
                                        op=ALU.mult), "dve", 1)
                W(v)
                ctx.inc(v.tensor_tensor(out=hi[:], in0=hi[:], in1=tmp8[0:1, :],
                                        op=ALU.add), "dve", 1)
        bis_dve = ctx.cnt["dve"]

        @ctx.block.tensor
        def _(tb, bis_dve=bis_dve):
            ctx.wait(tb, "dve", bis_dve)
            ctx.inc(tb.matmul(psD[:, 0:8], ctx.sb["c_ones_row"][:], hi[:],
                              start=True, stop=True), "pe", 1)
        thbc_mm = ctx.cnt["pe"]

        @ctx.block.vector
        def _(v, thbc_mm=thbc_mm):
            ctx.wait(v, "pe", thbc_mm)
            ctx.inc(v.tensor_copy(out=thbc[:], in_=psD[:, 0:8]), "dve", 1)
    thbc_dve = ctx.cnt["dve"]

    pfx_free = thbc_dve
    tr_ev_prev = [thbc_dve]
    st_done = {}
    n0_dve = 0

    for k in (0, 1):
        oh_all = oh1_all if k == 0 else oh2_all
        gk_all = g1_all if k == 0 else g2_all
        basek = base0 if k == 0 else base1
        slotk = slot0 if k == 0 else slot1
        for t in range(NTT):
            par = t % 2

            @ctx.block.vector
            def _(v, t=t, k=k, oh_all=oh_all, gk_all=gk_all):
                if t == 0 and k == 0:
                    ctx.wait(v, "dve", thbc_dve)
                if k == 1 and t == 0:
                    ctx.wait(v, "dve", n0_dve)
                ctx.inc(v.tensor_tensor(out=tmp8[:], in0=oh_all[:, t, :], in1=thbc[:],
                                        op=ALU.mult), "dve", 1)
                W(v)
                ctx.inc(v.reduce_sum(thtok[:], tmp8[:], axis=AX.X), "dve", 1)
                W(v)
                ctx.inc(v.tensor_tensor(out=keep[:], in0=gk_all[:, t:t + 1],
                                        in1=thtok[:], op=ALU.is_gt), "dve", 1)
                ctx.inc(v.tensor_tensor(out=tmp8[:], in0=oh_all[:, t, :],
                                        in1=ctx.sb["c_iota8"][:], op=ALU.mult),
                        "dve", 1)
                W(v)
                ctx.inc(v.reduce_sum(ek[:], tmp8[:], axis=AX.X), "dve", 1)
                if k == 1:
                    ctx.inc(v.tensor_tensor(out=tmp8[:], in0=oh_all[:, t, :],
                                            in1=n0bc[:], op=ALU.mult), "dve", 1)
                    W(v)
                    ctx.inc(v.reduce_sum(thtok[:], tmp8[:], axis=AX.X), "dve", 1)
                W(v)
                ctx.inc(v.tensor_scalar(out=gt8[:], in0=oh_all[:, t, :],
                                        scalar1=keep[:], scalar2=None,
                                        op0=ALU.mult), "dve", 1)
            oh_dve = ctx.cnt["dve"]

            @ctx.block.tensor
            def _(tb, t=t, k=k, basek=basek, oh_dve=oh_dve):
                ctx.wait(tb, "dve", oh_dve)
                ctx.wait(tb, "dve", pfx_free)
                ctx.inc(tb.matmul(psD[:, 16:24], ctx.sb["c_ut_strict"][:], gt8[:],
                                  start=True, stop=False), "pe", 1)
                ctx.inc(tb.matmul(psD[:, 16:24], ctx.sb["c_ones_row"][:], basek[:],
                                  start=False, stop=True), "pe", 1)
                ctx.inc(tb.matmul(psD[0:1, 24:32], onescol[:], gt8[:],
                                  start=True, stop=True), "pe", 1)
            pfx_mm = ctx.cnt["pe"]

            @ctx.block.vector
            def _(v, t=t, k=k, basek=basek, slotk=slotk, pfx_mm=pfx_mm):
                ctx.wait(v, "pe", pfx_mm)
                ctx.inc(v.tensor_tensor(out=tmp8[:], in0=psD[:, 16:24], in1=gt8[:],
                                        op=ALU.mult), "dve", 1)
                W(v)
                ctx.inc(v.reduce_sum(rank[:], tmp8[:], axis=AX.X), "dve", 1)
                if k == 1:
                    W(v)
                    ctx.inc(v.tensor_tensor(out=rank[:], in0=rank[:], in1=thtok[:],
                                            op=ALU.add), "dve", 1)
                ctx.inc(v.tensor_tensor(out=basek[:], in0=basek[:],
                                        in1=psD[0:1, 24:32], op=ALU.add), "dve", 1)
                W(v)
                ctx.inc(v.scalar_tensor_tensor(out=slotf[:], in0=ek[:],
                                               scalar=float(CAP), in1=rank[:],
                                               op0=ALU.mult, op1=ALU.add), "dve", 1)
                ctx.inc(v.tensor_scalar(out=ovf[:], in0=rank[:],
                                        scalar1=float(CAP) - 0.5, scalar2=None,
                                        op0=ALU.is_ge), "dve", 1)
                W(v)
                ctx.inc(v.tensor_scalar(out=ovf[:], in0=ovf[:], scalar1=-1.0,
                                        scalar2=1.0, op0=ALU.mult, op1=ALU.add),
                        "dve", 1)
                W(v)
                ctx.inc(v.tensor_tensor(out=mk[:], in0=keep[:], in1=ovf[:],
                                        op=ALU.mult), "dve", 1)
                W(v)
                ctx.inc(v.tensor_scalar(out=mkn[:], in0=mk[:], scalar1=-1.0,
                                        scalar2=1.0, op0=ALU.mult, op1=ALU.add),
                        "dve", 1)
                W(v)
                ctx.inc(v.tensor_tensor(out=slotf[:], in0=slotf[:], in1=mk[:],
                                        op=ALU.mult), "dve", 1)
                W(v)
                ctx.inc(v.scalar_tensor_tensor(out=slotf[:], in0=mkn[:],
                                               scalar=float(NEXP * CAP), in1=slotf[:],
                                               op0=ALU.mult, op1=ALU.add), "dve", 1)
                W(v)
                ctx.inc(v.tensor_copy(out=slotk[:, t:t + 1], in_=slotf[:]), "dve", 1)
            pfx_free = ctx.cnt["dve"]

            @ctx.block.tensor
            def _(tb, t=t, k=k, par=par):
                idx = k * NTT + t
                if idx >= 2:
                    ctx.wait(tb, "st", st_done[idx - 2])
                ctx.wait(tb, "dve", tr_ev_prev[0])
                for c in range(DC):
                    ctx.inc(tb.transpose(psC_bf[:, c * 128:(c + 1) * 128],
                                         xT[:, c, t * 128:(t + 1) * 128],
                                         ctx.sb["c_id_bf"][:]), "pe", 1)
            tr_mm = ctx.cnt["pe"]

            @ctx.block.vector
            def _(v, t=t, k=k, par=par, tr_mm=tr_mm, gk_all=gk_all):
                ctx.wait(v, "pe", tr_mm)
                ctx.inc(v.tensor_copy(out=tokb[:, par, :], in_=psC_bf[:, 0:1024]),
                        "dve", 1)
                tr_ev_prev[0] = ctx.cnt["dve"]
                W(v)
                ctx.inc(v.tensor_scalar(out=xsc[:, par, :], in0=tokb[:, par, :],
                                        scalar1=gk_all[:, t:t + 1], scalar2=None,
                                        op0=ALU.mult), "dve", 1)
            tr_ev = ctx.cnt["dve"]

            @ctx.block.gpsimd
            def _(g, t=t, k=k, par=par, tr_ev=tr_ev, slotk=slotk):
                ctx.wait(g, "dve", tr_ev)
                if k == 0:
                    ctx.inc(g.dma_start(out=dram["xstage"][t * 128:(t + 1) * 128, :],
                                        in_=tokb[:, par, :]), "st", 16)
                ctx.inc(g.indirect_dma_start(
                    out=dram["xe"],
                    out_offset=bass.IndirectOffsetOnAxis(slotk[:, t:t + 1], 0),
                    in_=tokb[:, par, :], in_offset=None), "st", 16)
                ctx.inc(g.indirect_dma_start(
                    out=dram["xte"],
                    out_offset=bass.IndirectOffsetOnAxis(slotk[:, t:t + 1], 0),
                    in_=xsc[:, par, :], in_offset=None), "st", 16)
                st_done[k * NTT + t] = ctx.cnt["st"]

        if k == 0:
            @ctx.block.vector
            def _(v):
                ctx.inc(v.tensor_copy(out=n0row[:], in_=base0[:]), "dve", 1)
            nr = ctx.cnt["dve"]

            @ctx.block.tensor
            def _(tb, nr=nr):
                ctx.wait(tb, "dve", nr)
                ctx.wait(tb, "dve", pfx_free)
                ctx.inc(tb.matmul(psD[:, 0:8], ctx.sb["c_ones_row"][:], n0row[:],
                                  start=True, stop=True), "pe", 1)
            nm = ctx.cnt["pe"]

            @ctx.block.vector
            def _(v, nm=nm):
                ctx.wait(v, "pe", nm)
                ctx.inc(v.tensor_copy(out=n0bc[:], in_=psD[:, 0:8]), "dve", 1)
            n0_dve = ctx.cnt["dve"]
            pfx_free = n0_dve

    @ctx.block.gpsimd
    def _(g):
        ctx.wait(g, "st")


def emit_l1_ffn_b(ctx, es, w, dram):
    nc = ctx.nc
    psA, psB, psC, psD = (ctx.ps[n] for n in ("psA", "psB", "psC", "psD"))
    psW0, psW1 = ctx.ps["psW0"], ctx.ps["psW1"]
    psA_bf = psA[:].bitcast(BF16)
    NCT = CAP // 128

    hT = ctx.sbuf(es, "f1_hT", [128, 32, CAP], BF16)
    xeT = ctx.sbuf(es, "f1_xeT", [128, DC, CAP], BF16)
    xteT = ctx.sbuf(es, "f1_xteT", [128, DC, CAP], BF16)
    tokl = ctx.sbuf(es, "f1_tokl", [128, 2, D], BF16)
    wg = ctx.sbuf(es, "f1_wg", [128, 2, DC, 128], BF16)
    wu = ctx.sbuf(es, "f1_wu", [128, 2, DC, 128], BF16)
    dw = ctx.sbuf(es, "f1_dw", [128, 2, DC, 128], BF16)
    silb = ctx.sbuf(es, "f1_silb", [128, 2, 512], BF16)
    yebf = ctx.sbuf(es, "f1_yebf", [128, DC, 512], BF16)
    ytok = ctx.sbuf(es, "f1_ytok", [128, 2, D], BF16)

    Cbs = [(0, 512), (512, 512), (1024, CAP - 1024)]

    ld_prev = {0: 0, 1: 0}
    ps_free = 0
    cd_free = 0
    st_prev = {0: 0, 1: 0}
    hT_free = 0

    for e in range(min(NEXP, B_NEXP)):
        if B_LEVEL < 1:
            break
        last_ev = cd_free
        for which, dst in (("xe", xeT), ("xte", xteT)):
            for ct in range(NCT):
                par = ct % 2

                @ctx.block.gpsimd
                def _(g, which=which, e=e, ct=ct, par=par):
                    ctx.wait(g, "pe", ld_prev[par])
                    ctx.inc(g.dma_start(
                        out=tokl[:, par, :],
                        in_=dram[which][e * CAP + ct * 128:e * CAP + (ct + 1) * 128, :]),
                        "dl", 16)
                ldd = ctx.cnt["dl"]

                @ctx.block.tensor
                def _(tb, which=which, ct=ct, par=par, ldd=ldd, e=e, last_ev=last_ev):
                    ctx.wait(tb, "dl", ldd)
                    ctx.wait(tb, "dve", last_ev)
                    for c in range(DC):
                        ctx.inc(tb.transpose(psA_bf[:, c * 128:(c + 1) * 128],
                                             tokl[:, par, c * 128:(c + 1) * 128],
                                             ctx.sb["c_id_bf"][:]), "pe", 1)
                ld_prev[par] = ctx.cnt["pe"]
                tr_mm = ctx.cnt["pe"]

                @ctx.block.vector
                def _(v, which=which, dst=dst, ct=ct, tr_mm=tr_mm):
                    ctx.wait(v, "pe", tr_mm)
                    for c in range(DC):
                        ctx.inc(v.tensor_copy(
                            out=dst[:, c, ct * 128:(ct + 1) * 128],
                            in_=psA_bf[:, c * 128:(c + 1) * 128]), "dve", 1)
                last_ev = ctx.cnt["dve"]
        xe_done = ctx.cnt["dve"]
        if B_LEVEL < 2:
            continue

        epi_ev = {}
        for It in range(32):
            par = It % 2

            @ctx.block.gpsimd
            def _(g, e=e, It=It, par=par):
                ctx.wait(g, "pe", ld_prev[("w", par)] if ("w", par) in ld_prev else 0)
                ctx.inc(g.dma_start(
                    out=wg[:, par], in_=w["gate_w"][e][:, It * 128:(It + 1) * 128]
                    .rearrange("(c p) n -> p c n", p=128)), "dl", 16)
                ctx.inc(g.dma_start(
                    out=wu[:, par], in_=w["up_w"][e][:, It * 128:(It + 1) * 128]
                    .rearrange("(c p) n -> p c n", p=128)), "dl", 16)
            wld = ctx.cnt["dl"]

            for cb, (c0, cn) in enumerate(Cbs):
                @ctx.block.tensor
                def _(tb, It=It, par=par, cb=cb, c0=c0, cn=cn, wld=wld, e=e):
                    if It == 0 and cb == 0:
                        ctx.wait(tb, "dve", xe_done)
                    ctx.wait(tb, "dl", wld)
                    if It >= 1 or cb >= 1:
                        ctx.wait(tb, "act", epi_ev.get(("sil", It, cb - 1),
                                                       epi_ev.get(("sil", It - 1, 2), 0)))
                        ctx.wait(tb, "dve", epi_ev.get(("ev", It, cb - 1),
                                                       epi_ev.get(("ev", It - 1, 2), 0)))
                    for kc in range(DC):
                        ctx.inc(tb.matmul(psC[:, 0:cn],
                                          wg[:, par, kc, :], xeT[:, kc, c0:c0 + cn],
                                          start=(kc == 0), stop=(kc == DC - 1)), "pe", 1)
                    for kc in range(DC):
                        ctx.inc(tb.matmul(psD[:, 0:cn],
                                          wu[:, par, kc, :], xteT[:, kc, c0:c0 + cn],
                                          start=(kc == 0), stop=(kc == DC - 1)), "pe", 1)
                    ld_prev[("w", par)] = ctx.cnt["pe"]
                gu_mm = ctx.cnt["pe"]

                @ctx.block.scalar
                def _(s, It=It, cb=cb, cn=cn, gu_mm=gu_mm):
                    ctx.wait(s, "pe", gu_mm)
                    ctx.inc(s.activation(out=silb[:, cb % 2, 0:cn], in_=psC[:, 0:cn],
                                         func=AF.Silu), "act", 1)
                epi_ev[("sil", It, cb)] = ctx.cnt["act"]

                @ctx.block.vector
                def _(v, It=It, cb=cb, c0=c0, cn=cn):
                    ctx.wait(v, "act", epi_ev[("sil", It, cb)])
                    ctx.inc(v.tensor_tensor(out=hT[:, It, c0:c0 + cn],
                                            in0=silb[:, cb % 2, 0:cn],
                                            in1=psD[:, 0:cn], op=ALU.mult), "dve", 1)
                epi_ev[("ev", It, cb)] = ctx.cnt["dve"]
        h_done = ctx.cnt["dve"]
        if B_LEVEL < 3:
            continue

        yslices = [psA[:, 0:512], psB[:, 0:512], psC[:, 0:512], psD[:, 0:512],
                   psW0[:, 0:512], psW0[:, 512:1024], psW1[:, 0:512],
                   psW1[:, 512:1024]]
        for cb, (c0, cn) in enumerate(Cbs):
            dn_wld = {}
            for It in range(32):
                par = It % 2

                @ctx.block.gpsimd
                def _(g, e=e, It=It, par=par, cb=cb):
                    ctx.wait(g, "pe", ld_prev.get(("d", par), 0))
                    ctx.inc(g.dma_start(
                        out=dw[:, par], in_=w["down_w"][e][It * 128:(It + 1) * 128, :]
                        .rearrange("p (c n) -> p c n", n=128)), "dl", 16)
                dn_wld[It] = ctx.cnt["dl"]

                @ctx.block.tensor
                def _(tb, It=It, par=par, cb=cb, c0=c0, cn=cn, e=e):
                    ctx.wait(tb, "dl", dn_wld[It])
                    if It == 0 and cb == 0:
                        ctx.wait(tb, "dve", h_done)
                        ctx.wait(tb, "dve", ps_free)
                    elif It == 0:
                        ctx.wait(tb, "dve", ps_free)
                    for mt in range(DC):
                        ctx.inc(tb.matmul(yslices[mt][:, 0:cn], dw[:, par, mt, :],
                                          hT[:, It, c0:c0 + cn],
                                          start=(It == 0), stop=(It == 31)), "pe", 1)
                    ld_prev[("d", par)] = ctx.cnt["pe"]
            dn_mm = ctx.cnt["pe"]
            hT_free = dn_mm

            @ctx.block.vector
            def _(v, cb=cb, cn=cn, dn_mm=dn_mm):
                ctx.wait(v, "pe", dn_mm)
                for mt in range(DC):
                    ctx.inc(v.tensor_copy(out=yebf[:, mt, 0:cn], in_=yslices[mt]
                                          if cn == 512 else yslices[mt][:, 0:cn]),
                            "dve", 1)
            ye_ev = ctx.cnt["dve"]

            ytt = cn // 128
            yev_prev = ye_ev
            for yt in range(ytt):
                par = yt % 2

                @ctx.block.tensor
                def _(tb, yt=yt, cb=cb, yev_prev=yev_prev, par=par):
                    ctx.wait(tb, "dve", yev_prev)
                    ctx.wait(tb, "st", st_prev[par])
                    for c in range(DC):
                        ctx.inc(tb.transpose(psA_bf[:, c * 128:(c + 1) * 128],
                                             yebf[:, c, yt * 128:(yt + 1) * 128],
                                             ctx.sb["c_id_bf"][:]), "pe", 1)
                ytr = ctx.cnt["pe"]

                @ctx.block.vector
                def _(v, yt=yt, par=par, ytr=ytr):
                    ctx.wait(v, "pe", ytr)
                    ctx.inc(v.tensor_copy(out=ytok[:, par, :], in_=psA_bf[:, 0:1024]),
                            "dve", 1)
                yev = ctx.cnt["dve"]
                yev_prev = yev

                @ctx.block.gpsimd
                def _(g, e=e, cb=cb, yt=yt, par=par, yev=yev):
                    ctx.wait(g, "dve", yev)
                    r0 = e * CAP + cb * 512 + yt * 128
                    ctx.inc(g.dma_start(out=dram["ye"][r0:r0 + 128, :],
                                        in_=ytok[:, par, :]), "st", 16)
                    st_prev[par] = ctx.cnt["st"]
            ps_free = ctx.cnt["dve"]
        cd_free = ctx.cnt["dve"]

    @ctx.block.gpsimd
    def _(g):
        ctx.wait(g, "st")


def emit_l1_ffn_c(ctx, es, dram, x_res_ap, out_ap, res_f32):
    nc = ctx.nc
    slot0 = ctx.sb["slot0"]
    slot1 = ctx.sb["slot1"]
    y0 = ctx.sbuf(es, "f1_y0", [128, 2, D], BF16)
    y1 = ctx.sbuf(es, "f1_y1", [128, 2, D], BF16)
    xres = ctx.sbuf(es, "f1_xres", [128, 2, D], F32 if res_f32 else BF16)
    zb = ctx.sbuf(es, "f1_zb", [128, 2, D], F32)
    ob = ctx.sbuf(es, "f1_ob", [128, 2, D], F32)
    zsq = ctx.sbuf(es, "f1_zsq", [128, 2, D], F32)
    smean = ctx.sbuf(es, "f1_smean", [128, 1], F32)
    svar = ctx.sbuf(es, "f1_svar", [128, 1], F32)
    srstd = ctx.sbuf(es, "f1_srstd", [128, 1], F32)

    gather_done = {}
    z_done = {}
    o_done = {}
    st_done = {}
    for t in range(NTT):
        par = t % 2

        @ctx.block.gpsimd
        def _(g, t=t, par=par):
            if t >= 2:
                ctx.wait(g, "dve", z_done[t - 2])
            ctx.inc(g.memset(y0[:, par, :], 0.0), "po", 1)
            ctx.inc(g.memset(y1[:, par, :], 0.0), "po", 1)
            g.wait_ge(ctx.sem["po"], ctx.cnt["po"])
            ctx.inc(g.indirect_dma_start(
                out=y0[:, par, :], out_offset=None, in_=dram["ye"],
                in_offset=bass.IndirectOffsetOnAxis(slot0[:, t:t + 1], 0)), "dl", 16)
            ctx.inc(g.indirect_dma_start(
                out=y1[:, par, :], out_offset=None, in_=dram["ye"],
                in_offset=bass.IndirectOffsetOnAxis(slot1[:, t:t + 1], 0)), "dl", 16)
            ctx.inc(g.dma_start(out=xres[:, par, :],
                                in_=x_res_ap[t * 128:(t + 1) * 128, :]), "dl", 16)
        g_ld = ctx.cnt["dl"]

        @ctx.block.vector
        def _(v, t=t, par=par, g_ld=g_ld):
            ctx.wait(v, "dl", g_ld)
            ctx.inc(v.tensor_tensor(out=zb[:, par, :], in0=y0[:, par, :],
                                    in1=y1[:, par, :], op=ALU.add), "dve", 1)
            ctx.wait(v, "dve")
            ctx.inc(v.tensor_tensor(out=zb[:, par, :], in0=zb[:, par, :],
                                    in1=xres[:, par, :], op=ALU.add), "dve", 1)
            ctx.wait(v, "dve")
            ctx.inc(v.reduce_sum(smean[:], zb[:, par, :], axis=AX.X), "dve", 1)
            ctx.wait(v, "dve")
            ctx.inc(v.tensor_scalar(out=smean[:], in0=smean[:], scalar1=1.0 / D,
                                    scalar2=None, op0=ALU.mult), "dve", 1)
        z1 = ctx.cnt["dve"]
        z_done[t] = z1

        @ctx.block.scalar
        def _(s, t=t, par=par, z1=z1):
            ctx.wait(s, "dve", z1)
            ctx.inc(s.activation(out=zsq[:, par, :], in_=zb[:, par, :],
                                 func=AF.Square), "act", 1)
        sq_a = ctx.cnt["act"]

        @ctx.block.vector
        def _(v, t=t, par=par, sq_a=sq_a):
            ctx.wait(v, "act", sq_a)
            ctx.inc(v.reduce_sum(svar[:], zsq[:, par, :], axis=AX.X), "dve", 1)
            ctx.wait(v, "dve")
            ctx.inc(v.tensor_scalar(out=svar[:], in0=svar[:], scalar1=1.0 / D,
                                    scalar2=None, op0=ALU.mult), "dve", 1)
            ctx.wait(v, "dve")
            ctx.inc(v.scalar_tensor_tensor(out=srstd[:], in0=smean[:], scalar=-1.0,
                                           in1=smean[:], op0=ALU.mult, op1=ALU.mult),
                    "dve", 1)
            ctx.wait(v, "dve")
            ctx.inc(v.tensor_tensor(out=svar[:], in0=svar[:], in1=srstd[:],
                                    op=ALU.add), "dve", 1)
            ctx.wait(v, "dve")
            ctx.inc(v.tensor_scalar(out=svar[:], in0=svar[:], scalar1=1e-5,
                                    scalar2=None, op0=ALU.add), "dve", 1)
        v1 = ctx.cnt["dve"]

        @ctx.block.scalar
        def _(s, v1=v1):
            ctx.wait(s, "dve", v1)
            ctx.inc(s.activation(out=svar[:], in_=svar[:], func=AF.Ln), "act", 1)
            ctx.wait(s, "act")
            ctx.inc(s.activation(out=srstd[:], in_=svar[:], func=AF.Exp,
                                 scale=-0.5), "act", 1)
        sq_b = ctx.cnt["act"]

        @ctx.block.vector
        def _(v, t=t, par=par, sq_b=sq_b):
            ctx.wait(v, "act", sq_b)
            ctx.inc(v.tensor_scalar(out=ob[:, par, :], in0=zb[:, par, :],
                                    scalar1=smean[:], scalar2=srstd[:],
                                    op0=ALU.subtract, op1=ALU.mult), "dve", 1)
        o_done[t] = ctx.cnt["dve"]

        @ctx.block.sync
        def _(sy, t=t, par=par):
            ctx.wait(sy, "dve", o_done[t])
            ctx.inc(sy.dma_start(out=out_ap[t * 128:(t + 1) * 128, :],
                                 in_=ob[:, par, :]), "st", 16)
            st_done[t] = ctx.cnt["st"]

    @ctx.block.sync
    def _(sy):
        ctx.wait(sy, "st")



def build_program(phases, weight_shapes):
    nc = bass.Bass("TRN2", target_bir_lowering=False)
    caps = declare_consts(nc)
    waps = {}
    for name, shape in weight_shapes.items():
        waps[name] = nc.dram_tensor(name, list(shape), F32, kind="ExternalInput").ap()
    x_in = nc.dram_tensor("x_in", [NTOK, D], F32, kind="ExternalInput").ap()
    x_out = nc.dram_tensor("x_out", [NTOK, D], F32, kind="ExternalOutput").ap()
    dbg = {}
    if DBG:
        for nm, shape in (("dbg_xT", [128, DC, 256]), ("dbg_q", [128, DC, 128]),
                          ("dbg_k", [128, DC, 128]), ("dbg_v", [128, 1024]),
                          ("dbg_at", [128, DC, 128]), ("dbg_scm", [128, 640]),
                          ("dbg_p", [128, 640]), ("dbg_z", [128, DC, 128]),
                          ("dbg_pt", [128, 5, 128]), ("dbg_st", [128, 4])):
            dbg[nm] = nc.dram_tensor(nm, shape, F32, kind="ExternalOutput").ap()
    nc._dbg = dbg

    root = ExitStack()
    with root:
        with nc.Block() as block:
            ctx = Ctx(nc, root, block)
            ctx.ps = {}
            for nm, shape in (("psA", [128, 512]), ("psB", [128, 512]),
                              ("psC", [128, 512]), ("psD", [128, 512]),
                              ("psW0", [128, 1024]), ("psW1", [128, 1024])):
                ctx.ps[nm] = root.enter_context(nc.psum_tensor(nm, shape, F32))
            ctx.sbuf(root, "ln_zbf", [128, DC, 512], BF16)
            ctx.sbuf(root, "ln_zsq", [128, 2, 512], BF16)
            ctx.sbuf(root, "ln_stats", [1, 3, 512], F32)
            ctx.sbuf(root, "slot0", [128, NTT], U32)
            ctx.sbuf(root, "slot1", [128, NTT], U32)
            load_consts(ctx, root, caps)

            l1f_any = ("l1_ffn" in phases or "l1_ffn_local" in phases
                       or "l1_ffn_a_only" in phases)
            if l1f_any or "l1_ffn_g" in phases:
                dram = {
                    "xe": nc.dram_tensor("xe_buf", [NEXP * CAP + 1, D], BF16).ap(),
                    "xte": nc.dram_tensor("xte_buf", [NEXP * CAP + 1, D], BF16).ap(),
                    "ye": nc.dram_tensor("ye_buf", [NEXP * CAP + 1, D], BF16).ap(),
                    "xstage": nc.dram_tensor("xstage", [NTOK, D], BF16).ap(),
                }

            with ExitStack() as xes:
                ctx.sbuf(xes, "xT", [128, DC, NTOK], BF16)

                with ExitStack() as pes:
                    emit_stream_load(ctx, pes, x_in)
                    ctx.fence()

                if "l0_attn" in phases:
                    with ExitStack() as pes:
                        emit_l0_attn(ctx, pes, {
                            "qkv_kernel": waps["L0_attn_qkv_kernel"],
                            "qkv_bias": waps["L0_attn_qkv_bias"],
                            "out_kernel": waps["L0_attn_out_kernel"],
                            "out_bias": waps["L0_attn_out_bias"]})
                        ctx.fence()

                if "l0_ffn" in phases:
                    with ExitStack() as pes:
                        emit_l0_ffn(ctx, pes, {
                            "gate_weight": waps["L0_ffn_gate_weight"],
                            "e_score_correction_bias": waps["L0_ffn_e_score_correction_bias"],
                            "experts_gate_up": waps["L0_ffn_experts_gate_up"],
                            "experts_down": waps["L0_ffn_experts_down"],
                            "shared_gate_w": waps["L0_ffn_shared_gate_w"],
                            "shared_up_w": waps["L0_ffn_shared_up_w"],
                            "shared_down_w": waps["L0_ffn_shared_down_w"]})
                        ctx.fence()

                if "l1_attn" in phases:
                    with ExitStack() as pes:
                        emit_l1_attn(ctx, pes, {
                            "Wq": waps["L1_attn_Wq"], "Wkv": waps["L1_attn_Wkv"],
                            "Wv": waps["L1_attn_Wv"], "Wo": waps["L1_attn_Wo"],
                            "wk_pad": waps["wk_pad"], "qg_col": waps["qg_col"],
                            "kg_col": waps["kg_col"]})
                        ctx.fence()

                if "l1_ffn_a_only" in phases:
                    g_all_in2 = nc.dram_tensor("g_all_in", [64, NTOK], F32,
                                               kind="ExternalInput").ap()
                    with ExitStack() as pes:
                        emit_l1_ffn_a(ctx, pes, {
                            "router_gate": waps["L1_ffn_router_gate"]}, dram,
                            g_mode="consume", g_ap=g_all_in2)
                        ctx.fence()

                if "l1_ffn_g" in phases:
                    g_out = nc.dram_tensor("g_loc_out", [8, NTOK], F32,
                                           kind="ExternalOutput").ap()
                    with ExitStack() as pes:
                        emit_l1_ffn_a(ctx, pes, {
                            "router_gate": waps["L1_ffn_router_gate"]}, None,
                            g_mode="produce", g_ap=g_out)
                        ctx.fence()

                if "l1_ffn_local" in phases:
                    with ExitStack() as pes:
                        emit_l1_ffn_a(ctx, pes, {
                            "router_gate": waps["L1_ffn_router_gate"]}, dram,
                            g_mode="local", g_ap=None)
                        ctx.fence()
                elif "l1_ffn" in phases:
                    g_all_in = nc.dram_tensor("g_all_in", [64, NTOK], F32,
                                              kind="ExternalInput").ap()
                    with ExitStack() as pes:
                        emit_l1_ffn_a(ctx, pes, {
                            "router_gate": waps["L1_ffn_router_gate"]}, dram,
                            g_mode="consume", g_ap=g_all_in)
                        ctx.fence()
                else:
                    with ExitStack() as pes:
                        emit_stream_store(ctx, pes, x_out)
                        ctx.fence()
            if "l1_ffn_a_only" in phases:
                s0d = nc.dram_tensor("slots0_d", [128, NTT], F32, kind="ExternalOutput").ap()
                s1d = nc.dram_tensor("slots1_d", [128, NTT], F32, kind="ExternalOutput").ap()
                sf0 = ctx.sbuf(root, "dbg_sf0", [128, NTT], F32)
                sf1 = ctx.sbuf(root, "dbg_sf1", [128, NTT], F32)

                @ctx.block.vector
                def _(v):
                    ctx.inc(v.tensor_copy(out=sf0[:], in_=ctx.sb["slot0"][:]), "dve", 1)
                    ctx.inc(v.tensor_copy(out=sf1[:], in_=ctx.sb["slot1"][:]), "dve", 1)
                dbg_dve = ctx.cnt["dve"]

                @ctx.block.sync
                def _(sy):
                    ctx.wait(sy, "dve", dbg_dve)
                    ctx.inc(sy.dma_start(out=s0d, in_=sf0[:]), "st", 16)
                    ctx.inc(sy.dma_start(out=s1d, in_=sf1[:]), "st", 16)
                    ctx.wait(sy, "st")

                @ctx.block.gpsimd
                def _(g):
                    ctx.wait(g, "dve", dbg_dve)
                    ctx.inc(g.dma_start(out=x_out[0:NTOK, :], in_=dram["xstage"]), "st", 16)
                    ctx.wait(g, "st")

            if "l1_ffn" in phases or "l1_ffn_local" in phases:
                if not SKIP_B:
                    with ExitStack() as pes:
                        emit_l1_ffn_b(ctx, pes, {
                            "gate_w": waps["L1_ffn_gate_w"],
                            "up_w": waps["L1_ffn_up_w"],
                            "down_w": waps["L1_ffn_down_w"]}, dram)
                        ctx.fence()
                if not SKIP_C:
                    with ExitStack() as pes:
                        if "fused" in phases:
                            emit_l1_ffn_c(ctx, pes, dram, dram["xstage"], x_out, False)
                        else:
                            emit_l1_ffn_c(ctx, pes, dram, x_in, x_out, True)
    return nc


WS_L0A = {
    "L0_attn_qkv_kernel": (1024, 3072),
    "L0_attn_qkv_bias": (3072,),
    "L0_attn_out_kernel": (1024, 1024),
    "L0_attn_out_bias": (1024,),
}
WS_L0F = {
    "L0_ffn_gate_weight": (32, 1024),
    "L0_ffn_e_score_correction_bias": (32,),
    "L0_ffn_experts_gate_up": (32, 1024, 1024),
    "L0_ffn_experts_down": (32, 1024, 512),
    "L0_ffn_shared_gate_w": (1024, 1024),
    "L0_ffn_shared_up_w": (1024, 1024),
    "L0_ffn_shared_down_w": (1024, 1024),
}
WS_L1A = {
    "L1_attn_Wq": (1024, 1024),
    "L1_attn_Wkv": (1024, 128),
    "L1_attn_Wv": (128, 512),
    "L1_attn_Wo": (1024, 1024),
    "wk_pad": (128, 1024),
    "qg_col": (128, 1),
    "kg_col": (128, 1),
    "L1_ffn_router_gate": (1024, 8),
}
WS_L1F = {
    "L1_ffn_router_gate": (1024, 8),
    "L1_ffn_gate_w": (8, 1024, 4096),
    "L1_ffn_up_w": (8, 1024, 4096),
    "L1_ffn_down_w": (8, 4096, 1024),
}

_PROG_CACHE = {}


def _get_prog(key, phases, wshapes):
    if key not in _PROG_CACHE:
        _PROG_CACHE[key] = build_program(phases, wshapes)
    return _PROG_CACHE[key]


def _derived_l1a(inputs):
    Wk = np.ascontiguousarray(inputs["L1_attn_Wk"], dtype=np.float32)
    wk_pad = np.zeros((128, 1024), np.float32)
    for h in range(16):
        wk_pad[:, h * 64:h * 64 + 32] = Wk[:, h * 32:(h + 1) * 32]
    qg_col = np.tile(np.ascontiguousarray(inputs["L1_attn_q_g"], np.float32),
                     2).reshape(128, 1)
    kg_col = np.tile(np.ascontiguousarray(inputs["L1_attn_k_g"], np.float32),
                     4).reshape(128, 1)
    return wk_pad, qg_col, kg_col


def _launch(nc, base_map, xs):
    in_maps = []
    for c in range(NCORES):
        m = dict(base_map)
        m["x_in"] = xs[c]
        in_maps.append(m)
    return run_bass_kernel_spmd(nc, in_maps, core_ids=list(range(NCORES)))


FUSED = True

WS_ALL = {}
WS_ALL.update(WS_L0A)
WS_ALL.update(WS_L0F)
WS_ALL.update(WS_L1A)
WS_ALL.update(WS_L1F)


def kernel(**inputs):
    if FUSED:
        return kernel_fused(**inputs)
    return kernel_4launch(**inputs)


def kernel_fused(**inputs):
    f32c = lambda a: np.ascontiguousarray(a, dtype=np.float32)
    x = f32c(inputs["L0_attn_x"])
    nc = _get_prog("fused", ["l0_attn", "l0_ffn", "l1_attn", "l1_ffn_local",
                             "fused"], WS_ALL)
    wk_pad, qg_col, kg_col = _derived_l1a(inputs)
    m = {k: f32c(inputs[k]) for k in WS_ALL
         if k not in ("wk_pad", "qg_col", "kg_col")}
    m["wk_pad"] = wk_pad
    m["qg_col"] = qg_col
    m["kg_col"] = kg_col
    m.update(CONSTS)
    xs = [np.ascontiguousarray(x[2 * c:2 * c + 2].reshape(NTOK, D))
          for c in range(NCORES)]
    r = _launch(nc, m, xs)
    out = np.empty((16, SEQ, D), np.float32)
    for c in range(NCORES):
        out[2 * c:2 * c + 2] = r.results[c]["x_out"].reshape(2, SEQ, D)
    return out


def kernel_4launch(**inputs):
    f32c = lambda a: np.ascontiguousarray(a, dtype=np.float32)
    x = f32c(inputs["L0_attn_x"])
    xs = [np.ascontiguousarray(x[2 * c:2 * c + 2].reshape(NTOK, D))
          for c in range(NCORES)]

    nc1 = _get_prog("l0a", ["l0_attn"], WS_L0A)
    m = {k: f32c(inputs[k]) for k in WS_L0A}
    m.update(CONSTS)
    r = _launch(nc1, m, xs)
    xs = [r.results[c]["x_out"] for c in range(NCORES)]

    nc2 = _get_prog("l0f", ["l0_ffn"], WS_L0F)
    m = {k: f32c(inputs[k]) for k in WS_L0F}
    m.update(CONSTS)
    r = _launch(nc2, m, xs)
    xs = [r.results[c]["x_out"] for c in range(NCORES)]

    nc3 = _get_prog("l1a", ["l1_attn", "l1_ffn_g"], WS_L1A)
    wk_pad, qg_col, kg_col = _derived_l1a(inputs)
    m = {"L1_attn_Wq": f32c(inputs["L1_attn_Wq"]),
         "L1_attn_Wkv": f32c(inputs["L1_attn_Wkv"]),
         "L1_attn_Wv": f32c(inputs["L1_attn_Wv"]),
         "L1_attn_Wo": f32c(inputs["L1_attn_Wo"]),
         "wk_pad": wk_pad, "qg_col": qg_col, "kg_col": kg_col,
         "L1_ffn_router_gate": f32c(inputs["L1_ffn_router_gate"])}
    m.update(CONSTS)
    r = _launch(nc3, m, xs)
    g_all = np.concatenate([r.results[c]["g_loc_out"] for c in range(NCORES)],
                           axis=0)
    xs = [r.results[c]["x_out"] for c in range(NCORES)]

    nc4 = _get_prog("l1f", ["l1_ffn"], WS_L1F)
    m = {k: f32c(inputs[k]) for k in WS_L1F}
    m["g_all_in"] = np.ascontiguousarray(g_all)
    m.update(CONSTS)
    r = _launch(nc4, m, xs)

    out = np.empty((16, SEQ, D), np.float32)
    for c in range(NCORES):
        out[2 * c:2 * c + 2] = r.results[c]["x_out"].reshape(2, SEQ, D)
    return out


# version 06
# speedup vs baseline: 1.2149x; 1.2149x over previous
import numpy as np
import ml_dtypes
from contextlib import ExitStack
import concourse.bass as bass
import concourse.mybir as mybir
from concourse.bass_utils import run_bass_kernel_spmd

F32 = mybir.dt.float32
BF16 = mybir.dt.bfloat16
U32 = mybir.dt.uint32
AF = mybir.ActivationFunctionType
ALU = mybir.AluOpType
AX = mybir.AxisListType

DBG = False
SKIP_B = False
SKIP_C = False
B_LEVEL = 4
B_NEXP = NEXP if False else 8
NCORES = 8
SEQ = 2048
NTOK = 2 * SEQ
D = 1024
DC = D // 128
NTT = NTOK // 128
TPS = SEQ // 128



class Ctx:
    SEMS = ["dl", "dh", "pe", "dve", "act", "po", "st"]

    def __init__(self, nc, es, block):
        self.nc = nc
        self.es = es
        self.block = block
        self.sem = {}
        self.cnt = {}
        self.epoch = 0
        for name in self.SEMS:
            self.sem[name] = es.enter_context(nc.semaphore("s0_" + name))
            self.cnt[name] = 0
        self.sb = {}

    def inc(self, instr, s, n=1):
        instr.then_inc(self.sem[s], n)
        self.cnt[s] += n
        return self.cnt[s]

    def wait(self, eng, s, val=None):
        if val is None:
            val = self.cnt[s]
        if val > 0:
            eng.wait_ge(self.sem[s], val)

    def sbuf(self, es, name, shape, dt=F32):
        h = es.enter_context(self.nc.sbuf_tensor("sb_" + name, list(shape), dt))
        self.sb[name] = h
        return h

    def fence(self, rotate=True):
        snap = dict(self.cnt)

        allsems = ("pe", "dve", "act", "po", "st", "dl", "dh")

        @self.block.gpsimd
        def _(g):
            for s in allsems:
                self.wait(g, s, snap[s])

        @self.block.sync
        def _(sy):
            for s in allsems:
                self.wait(sy, s, snap[s])

        @self.block.tensor
        def _(t):
            for s in allsems:
                self.wait(t, s, snap[s])

        @self.block.vector
        def _(v):
            for s in allsems:
                self.wait(v, s, snap[s])

        @self.block.scalar
        def _(s_):
            for s in allsems:
                self.wait(s_, s, snap[s])

        if rotate:
            self.epoch += 1
            for name in self.SEMS:
                self.sem[name] = self.es.enter_context(
                    self.nc.semaphore(f"s{self.epoch}_" + name))
                self.cnt[name] = 0



def make_consts():
    c = {}
    c["c_id_bf"] = np.eye(128, dtype=np.float32).astype(ml_dtypes.bfloat16)
    c["c_id_f32"] = np.eye(128, dtype=np.float32)
    c["c_ut_strict"] = np.triu(np.ones((128, 128), np.float32), k=1)
    c["c_ones_col_bf"] = np.ones((128, 1), np.float32).astype(ml_dtypes.bfloat16)
    c["c_ones_row"] = np.ones((1, 128), np.float32)
    i = np.arange(128)[:, None]
    j = np.arange(640)[None, :]
    c["c_mask_l0"] = np.where((j >= i + 1) & (j <= i + 512), 0.0,
                              -1e9).astype(np.float32)
    c["c_causal"] = np.where(np.arange(128)[None, :] <= np.arange(128)[:, None],
                             0.0, -1e9).astype(np.float32)
    sel = np.zeros((8, 128, 16), np.float32)
    for cc in range(8):
        for p in range(128):
            sel[cc, p, (cc * 128 + p) // 64] = 1.0
    c["c_sel64"] = sel.reshape(8 * 128, 16).astype(ml_dtypes.bfloat16)
    selb = np.transpose(sel, (0, 2, 1)).reshape(8 * 16, 128).copy()
    c["c_sel64b"] = selb.astype(ml_dtypes.bfloat16)
    c["c_iota8"] = np.broadcast_to(np.arange(8, dtype=np.float32), (128, 8)).copy()
    selE = np.zeros((64, 8), np.float32)
    for p in range(64):
        selE[p, p % 8] = 1.0
    c["c_selE"] = selE
    return c


CONSTS = make_consts()
CONST_SB = {
    "c_id_bf": ([128, 128], BF16),
    "c_id_f32": ([128, 128], F32),
    "c_ut_strict": ([128, 128], F32),
    "c_ones_col_bf": ([128, 1], BF16),
    "c_ones_row": ([1, 128], F32),
    "c_mask_l0": ([128, 640], F32),
    "c_causal": ([128, 128], F32),
    "c_sel64": ([128, 8, 16], BF16),
    "c_sel64b": ([16, 8, 128], BF16),
    "c_iota8": ([128, 8], F32),
    "c_selE": ([64, 8], F32),
}


def declare_consts(nc):
    aps = {}
    for k, v in CONSTS.items():
        dt = BF16 if v.dtype == ml_dtypes.bfloat16 else F32
        aps[k] = nc.dram_tensor(k, list(v.shape), dt, kind="ExternalInput").ap()
    return aps


def load_consts(ctx, es, caps):
    for name, (shape, dt) in CONST_SB.items():
        ctx.sbuf(es, name, shape, dt)

    @ctx.block.gpsimd
    def _(g):
        for name, (shape, dt) in CONST_SB.items():
            dst = ctx.sb[name]
            if len(shape) == 3:
                src_ap = caps[name].rearrange("(c p) h -> p c h", p=shape[0])
            else:
                src_ap = caps[name]
            ctx.inc(g.dma_start(out=dst[:], in_=src_ap), "dl", 16)



def dump_barrier(ctx):
    stv = ctx.cnt["st"]

    @ctx.block.vector
    def _(v):
        ctx.wait(v, "st", stv)

    @ctx.block.scalar
    def _(s):
        ctx.wait(s, "st", stv)

    @ctx.block.tensor
    def _(t):
        ctx.wait(t, "st", stv)


def emit_stream_load(ctx, es, src_ap):
    xT = ctx.sb["xT"]
    tok = ctx.sbuf(es, "ld_tok", [128, 2, D], BF16)
    slots = []
    for nm in ("psA", "psB", "psC", "psD"):
        slots.append(ctx.ps[nm][:].bitcast(BF16))
    slots.append(ctx.ps["psW0"][:].bitcast(BF16))
    slots.append(ctx.ps["psW1"][:].bitcast(BF16))
    def slot(k):
        k = k % 8
        if k < 4:
            return slots[k][:, 0:128]
        return slots[4 + (k - 4) // 2][:, ((k - 4) % 2) * 1024:((k - 4) % 2) * 1024 + 128]

    tr = {}
    ev = {}
    ld = {}
    for tt in range(NTT):
        buf = tt % 2

        @ctx.block.gpsimd
        def _(g, tt=tt, buf=buf):
            if tt >= 2:
                ctx.wait(g, "pe", tr[(tt - 2) * DC + DC - 1])
            ld[tt] = ctx.inc(g.dma_start(out=tok[:, buf, :],
                                         in_=src_ap[tt * 128:(tt + 1) * 128, :]),
                             "dl", 16)

        @ctx.block.tensor
        def _(t, tt=tt, buf=buf):
            ctx.wait(t, "dl", ld[tt])
            for c in range(DC):
                k = tt * DC + c
                if k >= 8:
                    ctx.wait(t, "dve", ev[k - 8])
                tr[k] = ctx.inc(t.transpose(slot(k), tok[:, buf, c * 128:(c + 1) * 128],
                                            ctx.sb["c_id_bf"][:]), "pe", 1)

        @ctx.block.vector
        def _(v, tt=tt):
            for c in range(DC):
                k = tt * DC + c
                ctx.wait(v, "pe", tr[k])
                ev[k] = ctx.inc(v.tensor_copy(out=xT[:, c, tt * 128:(tt + 1) * 128],
                                              in_=slot(k)), "dve", 1)


def emit_stream_store(ctx, es, dst_ap, src_name="xT"):
    xT = ctx.sb[src_name]
    tok = ctx.sbuf(es, "st_tok", [128, 2, D], BF16)
    slots = []
    for nm in ("psA", "psB", "psC", "psD"):
        slots.append(ctx.ps[nm][:].bitcast(BF16))
    slots.append(ctx.ps["psW0"][:].bitcast(BF16))
    slots.append(ctx.ps["psW1"][:].bitcast(BF16))
    def slot(k):
        k = k % 8
        if k < 4:
            return slots[k][:, 0:128]
        return slots[4 + (k - 4) // 2][:, ((k - 4) % 2) * 1024:((k - 4) % 2) * 1024 + 128]

    start_dve = ctx.cnt["dve"]
    start_act = ctx.cnt["act"]
    tr = {}
    ev = {}
    st = {}
    for tt in range(NTT):
        buf = tt % 2

        @ctx.block.tensor
        def _(t, tt=tt):
            if tt == 0:
                ctx.wait(t, "dve", start_dve)
                ctx.wait(t, "act", start_act)
            for c in range(DC):
                k = tt * DC + c
                if k >= 8:
                    ctx.wait(t, "dve", ev[k - 8])
                tr[k] = ctx.inc(t.transpose(slot(k), xT[:, c, tt * 128:(tt + 1) * 128],
                                            ctx.sb["c_id_bf"][:]), "pe", 1)

        @ctx.block.vector
        def _(v, tt=tt, buf=buf):
            if tt >= 2:
                ctx.wait(v, "st", st[tt - 2])
            for c in range(DC):
                k = tt * DC + c
                ctx.wait(v, "pe", tr[k])
                ev[k] = ctx.inc(v.tensor_copy(out=tok[:, buf, c * 128:(c + 1) * 128],
                                              in_=slot(k)), "dve", 1)

        @ctx.block.gpsimd
        def _(g, tt=tt, buf=buf):
            ctx.wait(g, "dve", ev[tt * DC + DC - 1])
            st[tt] = ctx.inc(g.dma_start(out=dst_ap[tt * 128:(tt + 1) * 128, :],
                                         in_=tok[:, buf, :]), "st", 16)

    @ctx.block.gpsimd
    def _(g):
        ctx.wait(g, "st")



def emit_ln_group(ctx, nw, z_of, out_of, z_ready_dve):
    zbf = ctx.sb["ln_zbf"]
    zsq = ctx.sb["ln_zsq"]
    stats = ctx.sb["ln_stats"]
    psW0 = ctx.ps["psW0"]
    psC = ctx.ps["psC"]
    psD = ctx.ps["psD"]

    zb = {}
    for c in range(DC):
        @ctx.block.vector
        def _(v, c=c):
            if c == 0:
                ctx.wait(v, "dve", z_ready_dve)
            zb[c] = ctx.inc(v.tensor_copy(out=zbf[:, c, 0:nw], in_=z_of(c)), "dve", 1)

    sq = {}
    sqmm = {}
    for c in range(DC):
        @ctx.block.scalar
        def _(s, c=c):
            ctx.wait(s, "dve", zb[c])
            if c >= 2:
                ctx.wait(s, "pe", sqmm[c - 2])
            sq[c] = ctx.inc(s.activation(out=zsq[:, c % 2, 0:nw], in_=zbf[:, c, 0:nw],
                                         func=AF.Square), "act", 1)

        @ctx.block.tensor
        def _(t, c=c):
            ctx.wait(t, "dve", zb[c])
            ctx.inc(t.matmul(psW0[0:1, 0:nw], ctx.sb["c_ones_col_bf"][:],
                             zbf[:, c, 0:nw], start=(c == 0), stop=(c == DC - 1)),
                    "pe", 1)
            ctx.wait(t, "act", sq[c])
            sqmm[c] = ctx.inc(t.matmul(psW0[0:1, 512:512 + nw],
                                       ctx.sb["c_ones_col_bf"][:],
                                       zsq[:, c % 2, 0:nw],
                                       start=(c == 0), stop=(c == DC - 1)),
                              "pe", 1)
    pe_stats = ctx.cnt["pe"]

    @ctx.block.vector
    def _(v):
        ctx.wait(v, "pe", pe_stats)
        ctx.inc(v.tensor_scalar(out=stats[0:1, 0, 0:nw], in0=psW0[0:1, 0:nw],
                                scalar1=1.0 / D, scalar2=None, op0=ALU.mult), "dve", 1)
        ctx.inc(v.tensor_scalar(out=stats[0:1, 1, 0:nw], in0=psW0[0:1, 512:512 + nw],
                                scalar1=1.0 / D, scalar2=None, op0=ALU.mult), "dve", 1)
        ctx.wait(v, "dve")
        ctx.inc(v.scalar_tensor_tensor(out=stats[0:1, 2, 0:nw],
                                       in0=stats[0:1, 0, 0:nw], scalar=-1.0,
                                       in1=stats[0:1, 0, 0:nw],
                                       op0=ALU.mult, op1=ALU.mult), "dve", 1)
        ctx.wait(v, "dve")
        ctx.inc(v.tensor_tensor(out=stats[0:1, 1, 0:nw], in0=stats[0:1, 1, 0:nw],
                                in1=stats[0:1, 2, 0:nw], op=ALU.add), "dve", 1)
        ctx.wait(v, "dve")
        ctx.inc(v.tensor_scalar(out=stats[0:1, 1, 0:nw], in0=stats[0:1, 1, 0:nw],
                                scalar1=1e-5, scalar2=None, op0=ALU.add), "dve", 1)
    dve_var = ctx.cnt["dve"]

    @ctx.block.scalar
    def _(s):
        ctx.wait(s, "dve", dve_var)
        ctx.inc(s.activation(out=stats[0:1, 1, 0:nw], in_=stats[0:1, 1, 0:nw],
                             func=AF.Ln), "act", 1)
        ctx.wait(s, "act")
        ctx.inc(s.activation(out=stats[0:1, 2, 0:nw], in_=stats[0:1, 1, 0:nw],
                             func=AF.Exp, scale=-0.5), "act", 1)
    dve_rstd = ctx.cnt["act"]

    @ctx.block.vector
    def _(v):
        ctx.wait(v, "act", dve_rstd)
    dve_rstd = ctx.cnt["dve"]

    act_rstd = ctx.cnt["act"]

    @ctx.block.tensor
    def _(t):
        ctx.wait(t, "act", act_rstd)
        ctx.inc(t.matmul(psC[:, 0:nw], ctx.sb["c_ones_row"][:],
                         stats[0:1, 0, 0:nw], start=True, stop=True), "pe", 1)
        ctx.inc(t.matmul(psD[:, 0:nw], ctx.sb["c_ones_row"][:],
                         stats[0:1, 2, 0:nw], start=True, stop=True), "pe", 1)
    pe_bc = ctx.cnt["pe"]

    for c in range(DC):
        @ctx.block.vector
        def _(v, c=c):
            if c == 0:
                ctx.wait(v, "pe", pe_bc)
            zc = z_of(c)
            ctx.inc(v.tensor_tensor(out=zc, in0=zc,
                                    in1=psC[:, 0:nw], op=ALU.subtract), "dve", 1)
            ctx.wait(v, "dve")
            ctx.inc(v.tensor_tensor(out=out_of(c), in0=zc,
                                    in1=psD[:, 0:nw], op=ALU.mult), "dve", 1)
    return ctx.cnt["dve"]



def emit_l0_attn(ctx, es, w):
    nc = ctx.nc
    xT = ctx.sb["xT"]
    qkv = ctx.sbuf(es, "l0_qkv", [128, DC, 3072], BF16)
    outk = ctx.sbuf(es, "l0_outk", [128, DC, 1024], BF16)
    qkbias = ctx.sbuf(es, "l0_qkb", [128, 16], F32)
    ln_z = ctx.sbuf(es, "ln_z", [128, DC, 512], F32)
    obias = ctx.sbuf(es, "l0_ob", [128, DC], F32)
    vbias_bc = ctx.sbuf(es, "l0_vbc", [128, 1024], BF16)
    kring = ctx.sbuf(es, "l0_kring", [128, 5, DC, 128], BF16)
    vring = ctx.sbuf(es, "l0_vring", [128, 5, 1024], BF16)
    qtT = ctx.sbuf(es, "l0_qt", [128, DC, 128], BF16)
    pbuf = [ctx.sbuf(es, f"l0_p{i}", [128, 640], BF16) for i in range(2)]
    ptb = [ctx.sbuf(es, f"l0_pt{i}", [128, 5, 128], BF16) for i in range(2)]
    atile = ctx.sbuf(es, "l0_atile", [128, DC, 128], BF16)
    m_max = [ctx.sbuf(es, f"l0_mmax{i}", [128, 1], F32) for i in range(2)]
    m_neg = [ctx.sbuf(es, f"l0_mneg{i}", [128, 1], F32) for i in range(2)]
    m_sum = [ctx.sbuf(es, f"l0_msum{i}", [128, 1], F32) for i in range(2)]
    m_rcp = [ctx.sbuf(es, f"l0_mrcp{i}", [128, 1], F32) for i in range(2)]

    psA, psB, psC, psD = (ctx.ps[n] for n in ("psA", "psB", "psC", "psD"))
    psW0, psW1 = ctx.ps["psW0"], ctx.ps["psW1"]
    psC_bf = psC[:].bitcast(BF16)
    psD_bf = psD[:].bitcast(BF16)

    @ctx.block.gpsimd
    def _(g):
        ctx.inc(g.dma_start(out=qkv[:],
                            in_=w["qkv_kernel"].rearrange("(c p) n -> p c n", p=128)),
                "dl", 16)
        ctx.inc(g.dma_start(out=outk[:],
                            in_=w["out_kernel"].rearrange("(c p) n -> p c n", p=128)),
                "dl", 16)
        ctx.inc(g.dma_start(out=qkbias[:],
                            in_=w["qkv_bias"][0:2048].rearrange("(c p) -> p c", p=128),
                            allow_slow_non_contiguous=True), "dl", 16)
        ctx.inc(g.dma_start(out=vbias_bc[:],
                            in_=bass.AP(w["qkv_bias"].tensor, 2048,
                                        [[0, 128], [1, 1024]])), "dl", 16)
        ctx.inc(g.dma_start(out=obias[:],
                            in_=w["out_bias"].rearrange("(c p) -> p c", p=128),
                            allow_slow_non_contiguous=True), "dl", 16)
    wl = ctx.cnt["dl"]

    last_ln = ctx.cnt["dve"]
    last_pv_ev = ctx.cnt["dve"]
    zgrp_ready = 0

    for tt in range(NTT):
        s, ti = tt // TPS, tt % TPS
        ts0 = s * SEQ
        t0 = tt * 128
        slot = ti % 5
        mm = {}

        @ctx.block.tensor
        def _(t, tt=tt, ti=ti, t0=t0, slot=slot):
            if tt == 0:
                ctx.wait(t, "dl", wl)
            ctx.wait(t, "dve", last_pv_ev)
            for mt in range(DC):
                ps = psC if mt < 4 else psD
                col = (mt % 4) * 128
                for kc in range(DC):
                    ctx.inc(t.matmul(ps[:, col:col + 128],
                                     qkv[:, kc, 1024 + mt * 128:1024 + (mt + 1) * 128],
                                     xT[:, kc, t0:t0 + 128],
                                     start=(kc == 0), stop=(kc == DC - 1)), "pe", 1)
            mm["k"] = ctx.cnt["pe"]
            for nt in range(2):
                for kc in range(DC):
                    ctx.inc(t.matmul(psW0[:, nt * 512:(nt + 1) * 512],
                                     xT[:, kc, t0:t0 + 128],
                                     qkv[:, kc, 2048 + nt * 512:2048 + (nt + 1) * 512],
                                     start=(kc == 0), stop=(kc == DC - 1)), "pe", 1)
            mm["v"] = ctx.cnt["pe"]

        @ctx.block.vector
        def _(v, tt=tt, slot=slot):
            ctx.wait(v, "pe", mm["k"])
            for mt in range(DC):
                ps = psC if mt < 4 else psD
                col = (mt % 4) * 128
                ctx.inc(v.tensor_scalar(out=kring[:, slot, mt, :],
                                        in0=ps[:, col:col + 128],
                                        scalar1=qkbias[:, 8 + mt:9 + mt], scalar2=None,
                                        op0=ALU.add), "dve", 1)
            mm["kev"] = ctx.cnt["dve"]
            ctx.wait(v, "pe", mm["v"])
            for nt in range(2):
                ctx.inc(v.tensor_tensor(out=vring[:, slot, nt * 512:(nt + 1) * 512],
                                        in0=psW0[:, nt * 512:(nt + 1) * 512],
                                        in1=vbias_bc[:, nt * 512:(nt + 1) * 512],
                                        op=ALU.add), "dve", 1)
            mm["vev"] = ctx.cnt["dve"]

        @ctx.block.tensor
        def _(t, tt=tt, t0=t0):
            ctx.wait(t, "dve", mm["kev"])
            for mt in range(DC):
                ps = psC if mt < 4 else psD
                col = (mt % 4) * 128
                for kc in range(DC):
                    ctx.inc(t.matmul(ps[:, col:col + 128],
                                     qkv[:, kc, mt * 128:(mt + 1) * 128],
                                     xT[:, kc, t0:t0 + 128],
                                     start=(kc == 0), stop=(kc == DC - 1)), "pe", 1)
            mm["q"] = ctx.cnt["pe"]

        @ctx.block.vector
        def _(v, tt=tt):
            ctx.wait(v, "pe", mm["q"])
            for mt in range(DC):
                ps = psC if mt < 4 else psD
                col = (mt % 4) * 128
                ctx.inc(v.tensor_scalar(out=qtT[:, mt, :], in0=ps[:, col:col + 128],
                                        scalar1=qkbias[:, mt:mt + 1], scalar2=None,
                                        op0=ALU.add), "dve", 1)
            mm["qev"] = ctx.cnt["dve"]

        if DBG and tt == 17:
            dbg = ctx.nc._dbg
            @ctx.block.gpsimd
            def _(g, slot=slot):
                ctx.wait(g, "dve", mm["qev"])
                ctx.inc(g.dma_start(out=dbg["dbg_q"], in_=qtT[:]), "st", 16)
                ctx.inc(g.dma_start(out=dbg["dbg_k"], in_=kring[:, slot]), "st", 16)
                ctx.inc(g.dma_start(out=dbg["dbg_v"], in_=vring[:, slot]), "st", 16)
            dump_barrier(ctx)

        kt_lo = max(0, ti - 4)
        nk = ti - kt_lo + 1
        mask0 = (4 - (ti - kt_lo)) * 128
        span = nk * 128
        SC = [psW0, psW1]
        PTp = [psC_bf, psD_bf]
        PVp = [psC, psD]
        ev = {}

        def emit_scores(h):
            pi = h % 2

            @ctx.block.tensor
            def _(t, h=h, pi=pi):
                if h < 2:
                    ctx.wait(t, "dve", mm["qev"])
                else:
                    ctx.wait(t, "act", ev[("exp", h - 2)])
                for j in range(nk):
                    kt = kt_lo + j
                    ctx.inc(t.matmul(SC[pi][:, j * 128:(j + 1) * 128],
                                     qtT[hb_of(h):hb_of(h) + 64, h // 2, :],
                                     kring[hb_of(h):hb_of(h) + 64, kt % 5, h // 2, :],
                                     start=True, stop=True), "pe", 1)
            ev[("sc", h)] = ctx.cnt["pe"]

        def hb_of(h):
            return (h % 2) * 64

        def emit_softmax(h):
            pi = h % 2

            @ctx.block.vector
            def _(v, h=h, pi=pi):
                ctx.wait(v, "pe", ev[("sc", h)])
                ctx.inc(v.tensor_tensor(out=SC[pi][:, 0:span], in0=SC[pi][:, 0:span],
                                        in1=ctx.sb["c_mask_l0"][:, mask0:mask0 + span],
                                        op=ALU.add), "dve", 1)
                ctx.wait(v, "dve")
                ctx.inc(v.reduce_max(m_max[pi][:], SC[pi][:, 0:span], axis=AX.X),
                        "dve", 1)
                ctx.wait(v, "dve")
                ctx.inc(v.tensor_scalar(out=m_neg[pi][:], in0=m_max[pi][:],
                                        scalar1=-0.125, scalar2=None, op0=ALU.mult),
                        "dve", 1)
            ev[("msk", h)] = ctx.cnt["dve"]

            @ctx.block.scalar
            def _(s, h=h, pi=pi):
                ctx.wait(s, "dve", ev[("msk", h)])
                ctx.inc(s.activation(out=pbuf[pi][:, 0:span], in_=SC[pi][:, 0:span],
                                     func=AF.Exp, bias=m_neg[pi][:], scale=0.125,
                                     accum_out=m_sum[pi][:]), "act", 1)
            ev[("exp", h)] = ctx.cnt["act"]

            @ctx.block.vector
            def _(v, h=h, pi=pi):
                ctx.wait(v, "act", ev[("exp", h)])
                ctx.inc(v.reciprocal(m_rcp[pi][:], m_sum[pi][:]), "dve", 1)
                ctx.wait(v, "dve")
                ctx.inc(v.tensor_scalar(out=pbuf[pi][:, 0:span], in0=pbuf[pi][:, 0:span],
                                        scalar1=m_rcp[pi][:], scalar2=None,
                                        op0=ALU.mult), "dve", 1)
            ev[("psc", h)] = ctx.cnt["dve"]

        def emit_ptpv(h):
            pi = h % 2

            @ctx.block.tensor
            def _(t, h=h, pi=pi):
                ctx.wait(t, "dve", ev[("psc", h)])
                if h >= 2:
                    ctx.wait(t, "dve", ev[("ptbev", h - 2)])
                    ctx.wait(t, "dve", ev[("ocev", h - 2)])
                for j in range(nk):
                    ctx.inc(t.transpose(PTp[pi][:, j * 128:(j + 1) * 128],
                                        pbuf[pi][:, j * 128:(j + 1) * 128],
                                        ctx.sb["c_id_bf"][:]), "pe", 1)
            ev[("pt", h)] = ctx.cnt["pe"]

            @ctx.block.vector
            def _(v, h=h, pi=pi):
                ctx.wait(v, "pe", ev[("pt", h)])
                ctx.inc(v.tensor_copy(
                    out=ptb[pi][:, 0:nk, :].rearrange("p a b -> p (a b)"),
                    in_=PTp[pi][:, 0:nk * 128]), "dve", 1)
            ev[("ptbev", h)] = ctx.cnt["dve"]

            @ctx.block.tensor
            def _(t, h=h, pi=pi):
                ctx.wait(t, "dve", ev[("ptbev", h)])
                for j in range(nk):
                    kt = kt_lo + j
                    ctx.inc(t.matmul(PVp[pi][0:64, 384:512],
                                     vring[:, kt % 5, h * 64:(h + 1) * 64],
                                     ptb[pi][:, j, :],
                                     start=(j == 0), stop=(j == nk - 1)), "pe", 1)
            ev[("pv", h)] = ctx.cnt["pe"]

            @ctx.block.vector
            def _(v, h=h, pi=pi):
                ctx.wait(v, "pe", ev[("pv", h)])
                ctx.inc(v.tensor_copy(out=atile[hb_of(h):hb_of(h) + 64, h // 2, :],
                                      in_=PVp[pi][0:64, 384:512]), "dve", 1)
            ev[("ocev", h)] = ctx.cnt["dve"]

        emit_scores(0)
        emit_softmax(0)
        for h in range(16):
            if h + 1 < 16:
                emit_scores(h + 1)
                emit_softmax(h + 1)
            emit_ptpv(h)
        last_pv_ev = ev[("ocev", 15)]
        heads_done = ctx.cnt["dve"]
        if DBG and tt == 17:
            dbg = ctx.nc._dbg
            @ctx.block.gpsimd
            def _(g):
                ctx.wait(g, "dve", heads_done)
                ctx.inc(g.dma_start(out=dbg["dbg_at"], in_=atile[:]), "st", 16)
            dump_barrier(ctx)

        @ctx.block.tensor
        def _(t, heads_done=heads_done):
            ctx.wait(t, "dve", heads_done)
            ctx.wait(t, "dve", last_ln)
            for mt in range(DC):
                ps = psA if mt < 4 else psB
                col = (mt % 4) * 128
                for kc in range(DC):
                    ctx.inc(t.matmul(ps[:, col:col + 128],
                                     outk[:, kc, mt * 128:(mt + 1) * 128],
                                     atile[:, kc, :],
                                     start=(kc == 0), stop=(kc == DC - 1)), "pe", 1)
        y_done = ctx.cnt["pe"]

        zc = tt % 4
        @ctx.block.vector
        def _(v, tt=tt, t0=t0, zc=zc, y_done=y_done):
            ctx.wait(v, "pe", y_done)
            z = ctx.sb["ln_z"]
            for c in range(DC):
                ps = psA if c < 4 else psB
                col = (c % 4) * 128
                ctx.inc(v.scalar_tensor_tensor(
                    out=z[:, c, zc * 128:(zc + 1) * 128], in0=ps[:, col:col + 128],
                    scalar=obias[:, c:c + 1], in1=xT[:, c, t0:t0 + 128],
                    op0=ALU.add, op1=ALU.add), "dve", 1)
        last_ln = ctx.cnt["dve"]
        if DBG and tt == 17:
            dbg = ctx.nc._dbg
            @ctx.block.sync
            def _(sy, zc=zc):
                ctx.wait(sy, "dve", last_ln)
                ctx.inc(sy.dma_start(out=dbg["dbg_z"],
                                     in_=ctx.sb["ln_z"][:, :, zc * 128:(zc + 1) * 128]),
                        "st", 16)
            dump_barrier(ctx)

        if zc == 3:
            g0 = tt * 128 - 384
            z = ctx.sb["ln_z"]
            last_ln = emit_ln_group(
                ctx, 512, lambda c: z[:, c, 0:512],
                lambda c: xT[:, c, g0:g0 + 512], last_ln)
            last_pv_ev = last_ln



def emit_l0_ffn(ctx, es, w):
    nc = ctx.nc
    xT = ctx.sb["xT"]
    psA, psB, psC, psD = (ctx.ps[n] for n in ("psA", "psB", "psC", "psD"))
    psW0, psW1 = ctx.ps["psW0"], ctx.ps["psW1"]
    psC_bf = psC[:].bitcast(BF16)

    gu_bf = nc.dram_tensor("gu_bf", [32, 1024, 1024], BF16).ap()
    dn_bf = nc.dram_tensor("dn_bf", [32, 1024, 512], BF16).ap()
    sg_bf = nc.dram_tensor("sg_bf", [1024, 1024], BF16).ap()
    su_bf = nc.dram_tensor("su_bf", [1024, 1024], BF16).ap()
    sd_bf = nc.dram_tensor("sd_bf", [1024, 1024], BF16).ap()
    comb = nc.dram_tensor("comb_dram", [32, NTOK], F32).ap()

    gwbf = ctx.sbuf(es, "f0_gwbf", [32, 1024], BF16)
    gwT = ctx.sbuf(es, "f0_gwT", [128, DC, 32], BF16)
    biasbc = ctx.sbuf(es, "f0_biasbc", [128, 32], F32)
    scores = ctx.sbuf(es, "f0_scores", [128, 32], F32)
    sfc = ctx.sbuf(es, "f0_sfc", [128, 32], F32)
    gs = ctx.sbuf(es, "f0_gs", [128, 8], F32)
    gwork = ctx.sbuf(es, "f0_gwork", [128, 8], F32)
    g01 = ctx.sbuf(es, "f0_g01", [128, 8], F32)
    smask = ctx.sbuf(es, "f0_smask", [128, 8], F32)
    masked = ctx.sbuf(es, "f0_masked", [128, 32], F32)
    mwork = ctx.sbuf(es, "f0_mwork", [128, 32], F32)
    m01 = ctx.sbuf(es, "f0_m01", [128, 32], F32)
    sel = ctx.sbuf(es, "f0_sel", [128, 32], F32)
    selsc = ctx.sbuf(es, "f0_selsc", [128, 32], F32)
    r_mg = ctx.sbuf(es, "f0_rmg", [128, 1], F32)
    r_sum = ctx.sbuf(es, "f0_rsum", [128, 1], F32)
    r_rcp = ctx.sbuf(es, "f0_rrcp", [128, 1], F32)
    r_fac = ctx.sbuf(es, "f0_rfac", [128, 1], F32)
    combT = ctx.sbuf(es, "f0_combT", [32, 2, 128], F32)
    guT = [ctx.sbuf(es, f"f0_guT{i}", [128, DC, 1024], BF16) for i in range(3)]
    dnT = [ctx.sbuf(es, f"f0_dnT{i}", [128, 4, 1024], BF16) for i in range(3)]
    silb = ctx.sbuf(es, "f0_sil", [128, 4, 256], BF16)
    tmpb = ctx.sbuf(es, "f0_tmp", [128, 4, 256], BF16)
    aT = [ctx.sbuf(es, f"f0_aT{i}", [128, 4, 256], BF16) for i in range(2)]
    cbb = [ctx.sbuf(es, f"f0_cb{i}", [128, 256], F32) for i in range(2)]
    acc = ctx.sbuf(es, "f0_acc", [128, DC, 768], F32)

    @ctx.block.gpsimd
    def _(g):
        for e in range(32):
            ctx.inc(g.dma_start(out=gu_bf[e], in_=w["experts_gate_up"][e]), "dl", 16)
            ctx.inc(g.dma_start(out=dn_bf[e], in_=w["experts_down"][e]), "dl", 16)
        ctx.inc(g.dma_start(out=sg_bf, in_=w["shared_gate_w"]), "dl", 16)
        ctx.inc(g.dma_start(out=su_bf, in_=w["shared_up_w"]), "dl", 16)
        ctx.inc(g.dma_start(out=sd_bf, in_=w["shared_down_w"]), "dl", 16)
    d2d_done = ctx.cnt["dl"]

    @ctx.block.gpsimd
    def _(g):
        ctx.inc(g.dma_start(out=gwbf[:], in_=w["gate_weight"]), "dl", 16)
        ctx.inc(g.dma_start(out=biasbc[:],
                            in_=bass.AP(w["e_score_correction_bias"].tensor, 0,
                                        [[0, 128], [1, 32]])), "dl", 16)
    gw_done = ctx.cnt["dl"]

    gw_ev = 0
    for c in range(DC):
        @ctx.block.tensor
        def _(t, c=c, gw_ev=gw_ev):
            if c == 0:
                ctx.wait(t, "dl", gw_done)
            else:
                ctx.wait(t, "dve", gw_ev)
            ctx.inc(t.transpose(psC_bf[:, 0:32], gwbf[0:32, c * 128:(c + 1) * 128],
                                ctx.sb["c_id_bf"][0:32, 0:32]), "pe", 1)
        pe_tr = ctx.cnt["pe"]

        @ctx.block.vector
        def _(v, c=c, pe_tr=pe_tr):
            ctx.wait(v, "pe", pe_tr)
            ctx.inc(v.tensor_copy(out=gwT[:, c, :], in_=psC_bf[:, 0:32]), "dve", 1)
        gw_ev = ctx.cnt["dve"]
    gwT_done = ctx.cnt["dve"]

    def W(v):
        ctx.wait(v, "dve")

    lg_ev = {None: gwT_done}
    st_prev = {0: 0, 1: 0}
    for t in range(NTT):
        par = t % 2
        ps = psA if par == 0 else psB

        @ctx.block.tensor
        def _(tb, t=t, ps=ps, par=par):
            if t == 0:
                ctx.wait(tb, "dve", gwT_done)
                ctx.wait(tb, "dl", gw_done)
            if t >= 2:
                ctx.wait(tb, "act", lg_ev[("sig", t - 2)])
            for kc in range(DC):
                ctx.inc(tb.matmul(ps[:, 0:32], xT[:, kc, t * 128:(t + 1) * 128],
                                  gwT[:, kc, :], start=(kc == 0), stop=(kc == DC - 1)),
                        "pe", 1)
        lg_mm = ctx.cnt["pe"]

        @ctx.block.scalar
        def _(s, t=t, ps=ps, lg_mm=lg_mm):
            ctx.wait(s, "pe", lg_mm)
            if t >= 1:
                ctx.wait(s, "dve", lg_ev[("comb", t - 1)])
            ctx.inc(s.activation(out=scores[:], in_=ps[:, 0:32], func=AF.Tanh,
                                 scale=0.5), "act", 1)
        lg_ev[("sig", t)] = ctx.cnt["act"]

        @ctx.block.vector
        def _(v, t=t):
            ctx.wait(v, "act", lg_ev[("sig", t)])
            ctx.inc(v.tensor_scalar(out=scores[:], in0=scores[:], scalar1=0.5,
                                    scalar2=0.5, op0=ALU.mult, op1=ALU.add), "dve", 1)
            W(v)
            ctx.inc(v.tensor_tensor(out=sfc[:], in0=scores[:], in1=biasbc[:],
                                    op=ALU.add), "dve", 1)
            lg_ev[("sfc", t)] = ctx.cnt["dve"]
            W(v)
            ctx.inc(v.reduce_max(gs[:], sfc.rearrange("p (g i) -> p g i", i=4),
                                 axis=AX.X), "dve", 1)
            W(v)
            ctx.inc(v.tensor_copy(out=gwork[:], in_=gs[:]), "dve", 1)
            for i in range(4):
                W(v)
                ctx.inc(v.reduce_max(r_mg[:], gwork[:], axis=AX.X), "dve", 1)
                if i < 3:
                    W(v)
                    ctx.inc(v.tensor_scalar(out=g01[:], in0=gwork[:], scalar1=r_mg[:],
                                            scalar2=None, op0=ALU.is_ge), "dve", 1)
                    W(v)
                    ctx.inc(v.scalar_tensor_tensor(out=gwork[:], in0=g01[:],
                                                   scalar=-20.0, in1=gwork[:],
                                                   op0=ALU.mult, op1=ALU.add), "dve", 1)
            W(v)
            ctx.inc(v.tensor_scalar(out=smask[:], in0=gs[:], scalar1=r_mg[:],
                                    scalar2=None, op0=ALU.is_ge), "dve", 1)
            W(v)
            ctx.inc(v.tensor_tensor(out=masked.rearrange("p (g i) -> p g i", i=4),
                                    in0=sfc.rearrange("p (g i) -> p g i", i=4),
                                    in1=smask[:].unsqueeze(2).broadcast_to([128, 8, 4]),
                                    op=ALU.mult), "dve", 1)
            W(v)
            ctx.inc(v.tensor_copy(out=mwork[:], in_=masked[:]), "dve", 1)
            for i in range(6):
                W(v)
                ctx.inc(v.reduce_max(r_mg[:], mwork[:], axis=AX.X), "dve", 1)
                if i < 5:
                    W(v)
                    ctx.inc(v.tensor_scalar(out=m01[:], in0=mwork[:], scalar1=r_mg[:],
                                            scalar2=None, op0=ALU.is_ge), "dve", 1)
                    W(v)
                    ctx.inc(v.scalar_tensor_tensor(out=mwork[:], in0=m01[:],
                                                   scalar=-20.0, in1=mwork[:],
                                                   op0=ALU.mult, op1=ALU.add), "dve", 1)
            W(v)
            ctx.inc(v.tensor_scalar(out=sel[:], in0=masked[:], scalar1=r_mg[:],
                                    scalar2=None, op0=ALU.is_ge), "dve", 1)
            W(v)
            ctx.inc(v.tensor_tensor(out=selsc[:], in0=sel[:], in1=scores[:],
                                    op=ALU.mult), "dve", 1)
            W(v)
            ctx.inc(v.reduce_sum(r_sum[:], selsc[:], axis=AX.X), "dve", 1)
            W(v)
            ctx.inc(v.tensor_scalar(out=r_sum[:], in0=r_sum[:], scalar1=1e-20,
                                    scalar2=None, op0=ALU.add), "dve", 1)
            W(v)
            ctx.inc(v.reciprocal(r_rcp[:], r_sum[:]), "dve", 1)
            W(v)
            ctx.inc(v.tensor_scalar(out=r_fac[:], in0=r_rcp[:], scalar1=2.5,
                                    scalar2=None, op0=ALU.mult), "dve", 1)
            W(v)
            ctx.inc(v.tensor_scalar(out=selsc[:], in0=selsc[:], scalar1=r_fac[:],
                                    scalar2=None, op0=ALU.mult), "dve", 1)
        comb_dve = ctx.cnt["dve"]
        lg_ev[("comb", t)] = comb_dve

        @ctx.block.tensor
        def _(tb, t=t, comb_dve=comb_dve, par=par):
            ctx.wait(tb, "dve", comb_dve)
            ctx.wait(tb, "st", st_prev[par])
            ctx.inc(tb.transpose(psD[0:32, 0:128], selsc[:], ctx.sb["c_id_f32"][:]),
                    "pe", 1)
        ct_mm = ctx.cnt["pe"]

        @ctx.block.vector
        def _(v, t=t, par=par, ct_mm=ct_mm):
            ctx.wait(v, "pe", ct_mm)
            ctx.inc(v.tensor_copy(out=combT[:, par, :], in_=psD[0:32, 0:128]), "dve", 1)
        ct_ev = ctx.cnt["dve"]

        @ctx.block.gpsimd
        def _(g, t=t, par=par, ct_ev=ct_ev):
            ctx.wait(g, "dve", ct_ev)
            ctx.inc(g.dma_start(out=comb[:, t * 128:(t + 1) * 128],
                                in_=combT[:, par, :]), "st", 16)
            st_prev[par] = ctx.cnt["st"]
    router_done_dve = ctx.cnt["dve"]
    router_done_st = ctx.cnt["st"]

    NU = 34
    groups = [(0, 768), (768, 768), (1536, 768), (2304, 768), (3072, 768),
              (3840, 256)]
    wload = {}
    slot_luse = {0: 0, 1: 0, 2: 0}

    def load_unit(gi, u):
        slot = u % 3

        @ctx.block.sync
        def _(sy, gi=gi, u=u, slot=slot):
            if gi == 0 and u < 3:
                ctx.wait(sy, "dl", d2d_done)
            ctx.wait(sy, "pe", slot_luse[slot])
            if u < 32:
                for c in range(DC):
                    ctx.inc(sy.dma_start(out=guT[slot][:, c, :],
                                         in_=gu_bf[u][:, c * 128:(c + 1) * 128],
                                         transpose=True), "dh", 16)
                for kc in range(4):
                    ctx.inc(sy.dma_start(out=dnT[slot][:, kc, :],
                                         in_=dn_bf[u][:, kc * 128:(kc + 1) * 128],
                                         transpose=True), "dh", 16)
            else:
                hh = (u - 32) * 512
                for c in range(DC):
                    ctx.inc(sy.dma_start(out=guT[slot][:, c, 0:512],
                                         in_=sg_bf[hh:hh + 512, c * 128:(c + 1) * 128],
                                         transpose=True), "dh", 16)
                    ctx.inc(sy.dma_start(out=guT[slot][:, c, 512:1024],
                                         in_=su_bf[hh:hh + 512, c * 128:(c + 1) * 128],
                                         transpose=True), "dh", 16)
                for kc in range(4):
                    ctx.inc(sy.dma_start(out=dnT[slot][:, kc, :],
                                         in_=sd_bf[:, hh + kc * 128:hh + (kc + 1) * 128],
                                         transpose=True), "dh", 16)
        wload[(gi, u)] = ctx.cnt["dh"]

    cb_prev = {0: router_done_dve, 1: router_done_dve}
    epi_prev = router_done_dve
    epi_prev_it = {it: router_done_dve for it in range(4)}
    evac_prev = router_done_dve
    aT_prev = {0: 0, 1: 0}

    def ps_out(mt):
        ps = (psA, psB, psC, psD)[mt // 2]
        return ps[:, (mt % 2) * 256:(mt % 2) * 256 + 256]

    for gi, (g0, gn) in enumerate(groups):
        ntt = gn // 256
        if gi == 0:
            load_unit(0, 0)
            load_unit(0, 1)
            load_unit(0, 2)
        for p in range(NU // 2):
            u0, u1 = 2 * p, 2 * p + 1
            for tt in range(ntt):
                t0 = g0 + tt * 256
                first_pair = (p == 0)
                epi_ev = {}
                for ei, u in enumerate((u0, u1)):
                    if u < 32:
                        @ctx.block.gpsimd
                        def _(g, u=u, ei=ei, t0=t0):
                            ctx.wait(g, "st", router_done_st)
                            ctx.wait(g, "dve", cb_prev[ei])
                            ctx.inc(g.dma_start(
                                out=cbb[ei][:],
                                in_=bass.AP(comb.tensor, u * NTOK + t0,
                                            [[0, 128], [1, 256]])), "dl", 16)
                        cb_ld = ctx.cnt["dl"]
                    else:
                        cb_ld = 0

                    slot = u % 3
                    sil_act = {}
                    epi_cur = {}
                    for idx, it in enumerate((0, 2, 1, 3)):
                        @ctx.block.tensor
                        def _(tb, u=u, it=it, idx=idx, t0=t0, gi=gi, slot=slot):
                            if idx == 0:
                                ctx.wait(tb, "dh", wload[(gi, u)])
                            ctx.wait(tb, "dve", epi_prev_it[it])
                            if idx < 2:
                                ctx.wait(tb, "dve", epi_prev_it[it ^ 1])
                            else:
                                ctx.wait(tb, "dve", epi_cur[it ^ 1])
                            for kc in range(DC):
                                ctx.inc(tb.matmul(
                                    psW0[:, it * 256:(it + 1) * 256],
                                    guT[slot][:, kc, it * 128:(it + 1) * 128],
                                    xT[:, kc, t0:t0 + 256],
                                    start=(kc == 0), stop=(kc == DC - 1)), "pe", 1)
                            for kc in range(DC):
                                ctx.inc(tb.matmul(
                                    psW1[:, it * 256:(it + 1) * 256],
                                    guT[slot][:, kc, (it + 4) * 128:(it + 5) * 128],
                                    xT[:, kc, t0:t0 + 256],
                                    start=(kc == 0), stop=(kc == DC - 1)), "pe", 1)
                        pe_it = ctx.cnt["pe"]
                        slot_luse[slot] = max(slot_luse[slot], pe_it)

                        @ctx.block.scalar
                        def _(s, it=it, pe_it=pe_it):
                            ctx.wait(s, "pe", pe_it)
                            sil_act[it] = ctx.inc(
                                s.activation(out=silb[:, it, :],
                                             in_=psW0[:, it * 256:(it + 1) * 256],
                                             func=AF.Silu), "act", 1)

                        @ctx.block.vector
                        def _(v, u=u, ei=ei, it=it, cb_ld=cb_ld):
                            if it == 0:
                                ctx.wait(v, "pe", aT_prev[ei])
                                if u < 32:
                                    ctx.wait(v, "dl", cb_ld)
                            ctx.wait(v, "act", sil_act[it])
                            ctx.inc(v.tensor_tensor(
                                out=tmpb[:, it, :], in0=silb[:, it, :],
                                in1=psW1[:, it * 256:(it + 1) * 256], op=ALU.mult),
                                "dve", 1)
                            W(v)
                            if u < 32:
                                ctx.inc(v.tensor_tensor(
                                    out=aT[ei][:, it, :], in0=tmpb[:, it, :],
                                    in1=cbb[ei][:], op=ALU.mult), "dve", 1)
                            else:
                                ctx.inc(v.tensor_copy(out=aT[ei][:, it, :],
                                                      in_=tmpb[:, it, :]), "dve", 1)
                        epi_prev_it[it] = ctx.cnt["dve"]
                        epi_cur[it] = ctx.cnt["dve"]
                    epi_ev[ei] = ctx.cnt["dve"]
                    epi_prev = epi_ev[ei]
                    cb_prev[ei] = ctx.cnt["dve"]

                @ctx.block.tensor
                def _(tb, u0=u0, u1=u1, gi=gi):
                    ctx.wait(tb, "dve", epi_ev[1])
                    ctx.wait(tb, "dve", evac_prev)
                    for mt in range(DC):
                        first = True
                        for ei, u in enumerate((u0, u1)):
                            slot = u % 3
                            for kc in range(4):
                                ctx.inc(tb.matmul(
                                    ps_out(mt),
                                    dnT[slot][:, kc, mt * 128:(mt + 1) * 128],
                                    aT[ei][:, kc, :],
                                    start=first, stop=(ei == 1 and kc == 3)), "pe", 1)
                                first = False
                            slot_luse[slot] = max(slot_luse[slot], ctx.cnt["pe"])
                    aT_prev[0] = ctx.cnt["pe"]
                    aT_prev[1] = ctx.cnt["pe"]
                dn_mm = ctx.cnt["pe"]

                if tt == ntt - 1:
                    for un in (u0 + 2, u1 + 2):
                        if un < NU:
                            load_unit(gi, un)
                        elif gi + 1 < len(groups):
                            load_unit(gi + 1, un - NU)

                @ctx.block.vector
                def _(v, t0=t0, dn_mm=dn_mm, first_pair=first_pair):
                    ctx.wait(v, "pe", dn_mm)
                    for mt in range(DC):
                        if first_pair:
                            ctx.inc(v.tensor_tensor(
                                out=acc[:, mt, (t0 % 768):(t0 % 768) + 256],
                                in0=ps_out(mt), in1=xT[:, mt, t0:t0 + 256],
                                op=ALU.add), "dve", 1)
                        else:
                            ctx.inc(v.tensor_tensor(
                                out=acc[:, mt, (t0 % 768):(t0 % 768) + 256],
                                in0=ps_out(mt),
                                in1=acc[:, mt, (t0 % 768):(t0 % 768) + 256],
                                op=ALU.add), "dve", 1)
                evac_prev = ctx.cnt["dve"]

        z_ready = evac_prev
        off = 0
        while off < gn:
            nw = min(512, gn - off)
            ln_done = emit_ln_group(
                ctx, nw,
                (lambda c, off=off, nw=nw: acc[:, c, off:off + nw]),
                (lambda c, g0=g0, off=off, nw=nw: xT[:, c, g0 + off:g0 + off + nw]),
                z_ready)
            off += nw
        epi_prev = ctx.cnt["dve"]
        evac_prev = ctx.cnt["dve"]
        for itx in range(4):
            epi_prev_it[itx] = ctx.cnt["dve"]



def emit_l1_attn(ctx, es, w):
    nc = ctx.nc
    xT = ctx.sb["xT"]
    psA, psB, psC, psD = (ctx.ps[n] for n in ("psA", "psB", "psC", "psD"))
    psW0, psW1 = ctx.ps["psW0"], ctx.ps["psW1"]
    psC_bf = psC[:].bitcast(BF16)
    psD_bf = psD[:].bitcast(BF16)

    wq = ctx.sbuf(es, "a1_wq", [128, DC, 1024], BF16)
    wo = ctx.sbuf(es, "a1_wo", [128, DC, 1024], BF16)
    wkv = ctx.sbuf(es, "a1_wkv", [128, DC, 128], BF16)
    wkp = ctx.sbuf(es, "a1_wkp", [128, 1024], BF16)
    wv = ctx.sbuf(es, "a1_wv", [128, 512], BF16)
    qg = ctx.sbuf(es, "a1_qg", [128, 1], F32)
    kg = ctx.sbuf(es, "a1_kg", [128, 1], F32)
    u0 = ctx.sbuf(es, "a1_u0", [128, 2048], BF16)
    kn = ctx.sbuf(es, "a1_kn", [128, DC, SEQ], BF16)
    vtok = ctx.sbuf(es, "a1_vtok", [128, TPS, 512], BF16)
    sqb = ctx.sbuf(es, "a1_sqb", [128, 2, 512], BF16)
    rtmp = ctx.sbuf(es, "a1_rtmp", [16, 512], F32)
    rstd = ctx.sbuf(es, "a1_rstd", [16, 512], BF16)
    scm = ctx.sbuf(es, "a1_scm", [128, SEQ], F32)
    pbuf = [ctx.sbuf(es, f"a1_p{i}", [128, SEQ], BF16) for i in range(2)]
    ptb = ctx.sbuf(es, "a1_pt", [128, 8, 128], BF16)
    atile = ctx.sbuf(es, "a1_atile", [128, DC, 128], BF16)
    ln_z = ctx.sbuf(es, "a1_lnz", [128, DC, 128], F32)
    m_max = ctx.sbuf(es, "a1_mmax", [128, 1], F32)
    m_neg = ctx.sbuf(es, "a1_mneg", [128, 1], F32)
    m_sum = ctx.sbuf(es, "a1_msum", [128, 1], F32)
    m_rcp = ctx.sbuf(es, "a1_mrcp", [128, 1], F32)

    def qn_ap(mt):
        return u0[:, mt * 128:(mt + 1) * 128]

    def qb_ap(mt):
        return u0[:, 1024 + mt * 128:1024 + (mt + 1) * 128]

    @ctx.block.gpsimd
    def _(g):
        ctx.inc(g.dma_start(out=wq[:],
                            in_=w["Wq"].rearrange("(c p) n -> p c n", p=128)), "dl", 16)
        ctx.inc(g.dma_start(out=wo[:],
                            in_=w["Wo"].rearrange("(c p) n -> p c n", p=128)), "dl", 16)
        ctx.inc(g.dma_start(out=wkv[:],
                            in_=w["Wkv"].rearrange("(c p) n -> p c n", p=128)), "dl", 16)
        ctx.inc(g.dma_start(out=wkp[:], in_=w["wk_pad"]), "dl", 16)
        ctx.inc(g.dma_start(out=wv[:], in_=w["Wv"]), "dl", 16)
        ctx.inc(g.dma_start(out=qg[:], in_=w["qg_col"]), "dl", 16)
        ctx.inc(g.dma_start(out=kg[:], in_=w["kg_col"]), "dl", 16)
    wl = ctx.cnt["dl"]

    @ctx.block.vector
    def _(v):
        ctx.inc(v.memset(atile[:], 0.0), "dve", 1)
    cd_free = ctx.cnt["dve"]
    last_ln = ctx.cnt["dve"]
    w01_free = ctx.cnt["dve"]

    for s in range(2):
        ts0 = s * SEQ
        kv_ev = cd_free
        for nt in range(4):
            @ctx.block.tensor
            def _(t, nt=nt, ts0=ts0, kv_ev=kv_ev):
                if nt == 0 and ts0 == 0:
                    ctx.wait(t, "dl", wl)
                ctx.wait(t, "dve", kv_ev)
                for kc in range(DC):
                    ctx.inc(t.matmul(psC[:, 0:512], wkv[:, kc, :],
                                     xT[:, kc, ts0 + nt * 512:ts0 + (nt + 1) * 512],
                                     start=(kc == 0), stop=(kc == DC - 1)), "pe", 1)
            mmv = ctx.cnt["pe"]

            @ctx.block.vector
            def _(v, nt=nt, mmv=mmv):
                ctx.wait(v, "pe", mmv)
                ctx.inc(v.tensor_copy(out=u0[:, nt * 512:(nt + 1) * 512],
                                      in_=psC[:, 0:512]), "dve", 1)
            kv_ev = ctx.cnt["dve"]
        kv_done = kv_ev

        for nt in range(4):
            sq_act = {}
            smm = {}
            krev = {}
            for mt in range(DC):
                ps = psC if mt % 2 == 0 else psD

                @ctx.block.tensor
                def _(t, nt=nt, mt=mt, ps=ps):
                    if nt == 0 and mt == 0:
                        ctx.wait(t, "dve", kv_done)
                    if mt >= 2:
                        ctx.wait(t, "act", sq_act[mt - 2])
                        ctx.wait(t, "dve", krev[mt - 2])
                    ctx.inc(t.matmul(ps[:, 0:512], wkp[:, mt * 128:(mt + 1) * 128],
                                     u0[:, nt * 512:(nt + 1) * 512],
                                     start=True, stop=True), "pe", 1)
                kr_mm = ctx.cnt["pe"]

                @ctx.block.scalar
                def _(sc_, mt=mt, ps=ps, kr_mm=kr_mm):
                    ctx.wait(sc_, "pe", kr_mm)
                    if mt >= 2:
                        ctx.wait(sc_, "pe", smm[mt - 2])
                    ctx.inc(sc_.activation(out=sqb[:, mt % 2, :], in_=ps[:, 0:512],
                                           func=AF.Square), "act", 1)
                sq_act[mt] = ctx.cnt["act"]

                @ctx.block.tensor
                def _(t, mt=mt):
                    ctx.wait(t, "act", sq_act[mt])
                    ctx.inc(t.matmul(psW0[0:16, 0:512], ctx.sb["c_sel64"][:, mt, :],
                                     sqb[:, mt % 2, :],
                                     start=(mt == 0), stop=(mt == DC - 1)), "pe", 1)
                smm[mt] = ctx.cnt["pe"]

                @ctx.block.vector
                def _(v, nt=nt, mt=mt, ps=ps, kr_mm=kr_mm):
                    ctx.wait(v, "pe", kr_mm)
                    ctx.inc(v.tensor_copy(out=kn[:, mt, nt * 512:(nt + 1) * 512],
                                          in_=ps[:, 0:512]), "dve", 1)
                krev[mt] = ctx.cnt["dve"]
            cd_free = ctx.cnt["dve"]

            @ctx.block.vector
            def _(v, nt=nt):
                ctx.wait(v, "pe", smm[DC - 1])
                ctx.inc(v.tensor_scalar(out=rtmp[:, 0:512], in0=psW0[0:16, 0:512],
                                        scalar1=1.0 / 32, scalar2=1e-6,
                                        op0=ALU.mult, op1=ALU.add), "dve", 1)
            r1 = ctx.cnt["dve"]

            @ctx.block.scalar
            def _(sc_, r1=r1):
                ctx.wait(sc_, "dve", r1)
                ctx.inc(sc_.activation(out=rtmp[:, 0:512], in_=rtmp[:, 0:512],
                                       func=AF.Ln), "act", 1)
                ctx.wait(sc_, "act")
                with nc.allow_low_precision(reason="rms scale in bf16 like other operands"):
                    ctx.inc(sc_.activation(out=rstd[:, 0:512], in_=rtmp[:, 0:512],
                                           func=AF.Exp, scale=-0.5), "act", 1)
            r2 = ctx.cnt["act"]

            @ctx.block.vector
            def _(v, r2=r2):
                ctx.wait(v, "act", r2)
            r3 = ctx.cnt["dve"]

            nrm_ev = {}
            for c in range(DC):
                half = c % 2

                @ctx.block.tensor
                def _(t, c=c, half=half, r2=r2):
                    ctx.wait(t, "act", r2)
                    if c >= 2:
                        ctx.wait(t, "dve", nrm_ev[c - 2])
                    ctx.inc(t.matmul(psW1[:, half * 512:half * 512 + 512],
                                     ctx.sb["c_sel64b"][:, c, :], rstd[:, 0:512],
                                     start=True, stop=True), "pe", 1)
                bc_mm = ctx.cnt["pe"]

                @ctx.block.vector
                def _(v, c=c, nt=nt, half=half, bc_mm=bc_mm):
                    ctx.wait(v, "pe", bc_mm)
                    sl = kn[:, c, nt * 512:(nt + 1) * 512]
                    ctx.inc(v.scalar_tensor_tensor(
                        out=sl, in0=sl, scalar=kg[:],
                        in1=psW1[:, half * 512:half * 512 + 512],
                        op0=ALU.mult, op1=ALU.mult), "dve", 1)
                nrm_ev[c] = ctx.cnt["dve"]
            w01_free = ctx.cnt["dve"]

        vt_ev = {}
        for tk in range(TPS):
            ps = psC if tk % 2 == 0 else psD

            @ctx.block.tensor
            def _(t, tk=tk, ps=ps):
                if tk == 0:
                    ctx.wait(t, "dve", cd_free)
                if tk >= 2:
                    ctx.wait(t, "dve", vt_ev[tk - 2])
                ctx.inc(t.matmul(ps[:, 0:512], u0[:, tk * 128:(tk + 1) * 128],
                                 wv[:], start=True, stop=True), "pe", 1)
            vmm = ctx.cnt["pe"]

            @ctx.block.vector
            def _(v, tk=tk, ps=ps, vmm=vmm):
                ctx.wait(v, "pe", vmm)
                ctx.inc(v.tensor_copy(out=vtok[:, tk, :], in_=ps[:, 0:512]), "dve", 1)
            vt_ev[tk] = ctx.cnt["dve"]
        cd_free = ctx.cnt["dve"]

        for ti in range(TPS):
            t0 = ts0 + ti * 128
            @ctx.block.tensor
            def _(t, t0=t0, ti=ti):
                ctx.wait(t, "dve", cd_free)
                for mt in range(DC):
                    ps = psC if mt < 4 else psD
                    col = (mt % 4) * 128
                    for kc in range(DC):
                        ctx.inc(t.matmul(ps[:, col:col + 128],
                                         wq[:, kc, mt * 128:(mt + 1) * 128],
                                         xT[:, kc, t0:t0 + 128],
                                         start=(kc == 0), stop=(kc == DC - 1)), "pe", 1)
            q_mm = ctx.cnt["pe"]

            sq_act = {}
            smm = {}
            for mt in range(DC):
                ps = psC if mt < 4 else psD
                col = (mt % 4) * 128

                @ctx.block.scalar
                def _(sc_, mt=mt, ps=ps, col=col, q_mm=q_mm):
                    if mt == 0:
                        ctx.wait(sc_, "pe", q_mm)
                        ctx.wait(sc_, "dve", w01_free)
                    if mt >= 2:
                        ctx.wait(sc_, "pe", smm[mt - 2])
                    ctx.inc(sc_.activation(out=sqb[:, mt % 2, 0:128],
                                           in_=ps[:, col:col + 128],
                                           func=AF.Square), "act", 1)
                sq_act[mt] = ctx.cnt["act"]

                @ctx.block.tensor
                def _(t, mt=mt):
                    ctx.wait(t, "act", sq_act[mt])
                    if mt == 0:
                        ctx.wait(t, "dve", w01_free)
                    ctx.inc(t.matmul(psW0[0:16, 0:128], ctx.sb["c_sel64"][:, mt, :],
                                     sqb[:, mt % 2, 0:128],
                                     start=(mt == 0), stop=(mt == DC - 1)), "pe", 1)
                smm[mt] = ctx.cnt["pe"]

            qev = {}
            for mt in range(DC):
                ps = psC if mt < 4 else psD
                col = (mt % 4) * 128

                @ctx.block.vector
                def _(v, mt=mt, ps=ps, col=col, q_mm=q_mm):
                    if mt == 0:
                        ctx.wait(v, "pe", q_mm)
                    ctx.inc(v.tensor_copy(out=qb_ap(mt), in_=ps[:, col:col + 128]),
                            "dve", 1)
                qev[mt] = ctx.cnt["dve"]
            cd_free_q = ctx.cnt["dve"]

            @ctx.block.vector
            def _(v):
                ctx.wait(v, "pe", smm[DC - 1])
                ctx.inc(v.tensor_scalar(out=rtmp[:, 0:128], in0=psW0[0:16, 0:128],
                                        scalar1=1.0 / 64, scalar2=1e-6,
                                        op0=ALU.mult, op1=ALU.add), "dve", 1)
            r1 = ctx.cnt["dve"]

            @ctx.block.scalar
            def _(sc_, r1=r1):
                ctx.wait(sc_, "dve", r1)
                ctx.inc(sc_.activation(out=rtmp[:, 0:128], in_=rtmp[:, 0:128],
                                       func=AF.Ln), "act", 1)
                ctx.wait(sc_, "act")
                with nc.allow_low_precision(reason="rms scale in bf16 like other operands"):
                    ctx.inc(sc_.activation(out=rstd[:, 0:128], in_=rtmp[:, 0:128],
                                           func=AF.Exp, scale=-0.5), "act", 1)
            r2 = ctx.cnt["act"]

            @ctx.block.vector
            def _(v, r2=r2):
                ctx.wait(v, "act", r2)
            r3 = ctx.cnt["dve"]

            @ctx.block.tensor
            def _(t, r2=r2):
                ctx.wait(t, "act", r2)
                for c in range(DC):
                    ctx.inc(t.matmul(psW1[:, c * 128:(c + 1) * 128],
                                     ctx.sb["c_sel64b"][:, c, :], rstd[:, 0:128],
                                     start=True, stop=True), "pe", 1)
            bc_mm = ctx.cnt["pe"]

            @ctx.block.vector
            def _(v, bc_mm=bc_mm):
                ctx.wait(v, "pe", bc_mm)
                for mt in range(DC):
                    ctx.wait(v, "dve")
                    ctx.inc(v.scalar_tensor_tensor(
                        out=qn_ap(mt), in0=qb_ap(mt), scalar=qg[:],
                        in1=psW1[:, mt * 128:(mt + 1) * 128],
                        op0=ALU.mult, op1=ALU.mult), "dve", 1)
            qn_done = ctx.cnt["dve"]
            w01_free = qn_done

            nk = ti + 1
            kspan = nk * 128
            npieces = (kspan + 511) // 512
            slots = [psW0[:, 0:512], psW0[:, 512:1024],
                     psW1[:, 0:512], psW1[:, 512:1024]]
            ev = {"cpall": {-1: qn_done}, "exp": {}, "psc": {}, "ptbev": {},
                  "ocev": {-1: qn_done}, "scp": {}}

            def emit_scores_l1(h):
                hb = (h % 2) * 64
                hc = h // 2

                @ctx.block.tensor
                def _(t, h=h, hb=hb, hc=hc):
                    if h == 0:
                        ctx.wait(t, "dve", qn_done)
                    else:
                        ctx.wait(t, "dve", ev["cpall"][h - 1])
                    for j in range(npieces):
                        n0 = j * 512
                        nn = min(512, kspan - n0)
                        ctx.inc(t.matmul(slots[j % 4][:, 0:nn],
                                         u0[hb:hb + 32, hc * 128:(hc + 1) * 128],
                                         kn[hb:hb + 32, hc, n0:n0 + nn],
                                         start=True, stop=True), "pe", 1)
                        ev["scp"][(h, j)] = ctx.cnt["pe"]

            def emit_softmax_l1(h):
                pi = h % 2

                @ctx.block.vector
                def _(v, h=h):
                    if h >= 1:
                        ctx.wait(v, "act", ev["exp"][h - 1])
                    for j in range(npieces):
                        n0 = j * 512
                        nn = min(512, kspan - n0)
                        ctx.wait(v, "pe", ev["scp"][(h, j)])
                        if n0 + nn == kspan:
                            if nn > 128:
                                ctx.inc(v.tensor_scalar(
                                    out=scm[:, n0:n0 + nn - 128],
                                    in0=slots[j % 4][:, 0:nn - 128],
                                    scalar1=0.125, scalar2=None, op0=ALU.mult),
                                    "dve", 1)
                            ctx.inc(v.scalar_tensor_tensor(
                                out=scm[:, kspan - 128:kspan],
                                in0=slots[j % 4][:, nn - 128:nn], scalar=0.125,
                                in1=ctx.sb["c_causal"][:],
                                op0=ALU.mult, op1=ALU.add), "dve", 1)
                        else:
                            ctx.inc(v.tensor_scalar(
                                out=scm[:, n0:n0 + nn], in0=slots[j % 4][:, 0:nn],
                                scalar1=0.125, scalar2=None, op0=ALU.mult), "dve", 1)
                    ev["cpall"][h] = ctx.cnt["dve"]
                    ctx.wait(v, "dve")
                    ctx.inc(v.reduce_max(m_max[:], scm[:, 0:kspan], axis=AX.X),
                            "dve", 1)
                    ctx.wait(v, "dve")
                    ctx.inc(v.tensor_scalar(out=m_neg[:], in0=m_max[:], scalar1=-1.0,
                                            scalar2=None, op0=ALU.mult), "dve", 1)
                msk = ctx.cnt["dve"]

                @ctx.block.scalar
                def _(s, h=h, pi=pi, msk=msk):
                    ctx.wait(s, "dve", msk)
                    ctx.inc(s.activation(out=pbuf[pi][:, 0:kspan], in_=scm[:, 0:kspan],
                                         func=AF.Exp, bias=m_neg[:], scale=1.0,
                                         accum_out=m_sum[:]), "act", 1)
                ev["exp"][h] = ctx.cnt["act"]

                @ctx.block.vector
                def _(v, h=h, pi=pi):
                    ctx.wait(v, "act", ev["exp"][h])
                    ctx.inc(v.reciprocal(m_rcp[:], m_sum[:]), "dve", 1)
                    ctx.wait(v, "dve")
                    ctx.inc(v.tensor_scalar(out=pbuf[pi][:, 0:kspan],
                                            in0=pbuf[pi][:, 0:kspan],
                                            scalar1=m_rcp[:], scalar2=None,
                                            op0=ALU.mult), "dve", 1)
                ev["psc"][h] = ctx.cnt["dve"]

            def emit_ptpv_l1(h):
                pi = h % 2
                nhalf = (nk + 7) // 8
                for half in range(nhalf):
                    j0 = half * 8
                    jn = min(8, nk - j0)

                    @ctx.block.tensor
                    def _(t, h=h, pi=pi, half=half, j0=j0, jn=jn):
                        if half == 0:
                            ctx.wait(t, "dve", ev["psc"][h])
                            ctx.wait(t, "dve", ev["ptbev"].get(h - 1, qn_done))
                        else:
                            ctx.wait(t, "dve", ev["ptbev"][h])
                        for j in range(jn):
                            ctx.inc(t.transpose(psC_bf[:, j * 128:(j + 1) * 128],
                                                pbuf[pi][:, (j0 + j) * 128:
                                                          (j0 + j + 1) * 128],
                                                ctx.sb["c_id_bf"][:]), "pe", 1)
                    ptm = ctx.cnt["pe"]

                    @ctx.block.vector
                    def _(v, ptm=ptm, jn=jn):
                        ctx.wait(v, "pe", ptm)
                        ctx.inc(v.tensor_copy(
                            out=ptb[:, 0:jn, :].rearrange("p a b -> p (a b)"),
                            in_=psC_bf[:, 0:jn * 128]), "dve", 1)
                    ev["ptbev"][h] = ctx.cnt["dve"]

                    @ctx.block.tensor
                    def _(t, h=h, half=half, j0=j0, jn=jn):
                        ctx.wait(t, "dve", ev["ptbev"][h])
                        if half == 0:
                            ctx.wait(t, "dve", ev["ocev"][h - 1])
                        for j in range(jn):
                            ctx.inc(t.matmul(psD[0:32, 384:512],
                                             vtok[:, j0 + j, h * 32:(h + 1) * 32],
                                             ptb[:, j, :],
                                             start=(half == 0 and j == 0),
                                             stop=(half == nhalf - 1 and j == jn - 1)),
                                    "pe", 1)
                pv = ctx.cnt["pe"]

                @ctx.block.vector
                def _(v, h=h, pv=pv):
                    ctx.wait(v, "pe", pv)
                    ctx.inc(v.tensor_copy(out=atile[(h % 2) * 64:(h % 2) * 64 + 32,
                                                    h // 2, :],
                                          in_=psD[0:32, 384:512]), "dve", 1)
                ev["ocev"][h] = ctx.cnt["dve"]

            emit_scores_l1(0)
            emit_softmax_l1(0)
            for h in range(16):
                if h + 1 < 16:
                    emit_scores_l1(h + 1)
                    emit_softmax_l1(h + 1)
                emit_ptpv_l1(h)
            scm_prev = ev["ocev"][15]
            heads_done = ctx.cnt["dve"]
            w01_free = heads_done
            cd_free = heads_done

            @ctx.block.tensor
            def _(t, heads_done=heads_done):
                ctx.wait(t, "dve", heads_done)
                ctx.wait(t, "dve", last_ln)
                for mt in range(DC):
                    ps = psA if mt < 4 else psB
                    col = (mt % 4) * 128
                    for kc in range(DC):
                        ctx.inc(t.matmul(ps[:, col:col + 128],
                                         wo[:, kc, mt * 128:(mt + 1) * 128],
                                         atile[:, kc, :],
                                         start=(kc == 0), stop=(kc == DC - 1)), "pe", 1)
            y_done = ctx.cnt["pe"]

            @ctx.block.vector
            def _(v, t0=t0, y_done=y_done):
                ctx.wait(v, "pe", y_done)
                for c in range(DC):
                    ps = psA if c < 4 else psB
                    col = (c % 4) * 128
                    ctx.inc(v.tensor_tensor(
                        out=ln_z[:, c, 0:128],
                        in0=ps[:, col:col + 128], in1=xT[:, c, t0:t0 + 128],
                        op=ALU.add), "dve", 1)
            last_ln = ctx.cnt["dve"]

            ln_done = emit_ln_group(
                ctx, 128, (lambda c: ln_z[:, c, 0:128]),
                (lambda c, t0=t0: xT[:, c, t0:t0 + 128]), last_ln)
            last_ln = ln_done
            cd_free = ln_done
            w01_free = ln_done



CAP = 1536
GCAP = 10240
NEXP = 8


def emit_l1_ffn_a(ctx, es, w, dram, g_mode="consume", g_ap=None):
    nc = ctx.nc
    xT = ctx.sb["xT"]
    psA, psB, psC, psD = (ctx.ps[n] for n in ("psA", "psB", "psC", "psD"))
    psC_bf = psC[:].bitcast(BF16)
    slot0 = ctx.sb["slot0"]
    slot1 = ctx.sb["slot1"]

    g_locT = g_ap if g_mode == "produce" else None
    g_all = g_ap if g_mode == "consume" else None
    LCAP = GCAP // NCORES

    rg = ctx.sbuf(es, "f1_rg", [128, DC, 8], BF16)
    probs = ctx.sbuf(es, "f1_probs", [128, 8], F32)
    mk2 = ctx.sbuf(es, "f1_mk2", [128, 8], F32)
    oh1 = ctx.sbuf(es, "f1_oh1", [128, 8], F32)
    oh2 = ctx.sbuf(es, "f1_oh2", [128, 8], F32)
    tmp8 = ctx.sbuf(es, "f1_tmp8", [128, 8], F32)
    gt8 = ctx.sbuf(es, "f1_gt8", [128, 8], F32)
    m1 = ctx.sbuf(es, "f1_m1", [128, 1], F32)
    m2 = ctx.sbuf(es, "f1_m2", [128, 1], F32)
    mneg = ctx.sbuf(es, "f1_mneg", [128, 1], F32)
    msum = ctx.sbuf(es, "f1_msum", [128, 1], F32)
    mrcp = ctx.sbuf(es, "f1_mrcp", [128, 1], F32)
    g1 = ctx.sbuf(es, "f1_g1", [128, 1], F32)
    g2 = ctx.sbuf(es, "f1_g2", [128, 1], F32)
    ek = ctx.sbuf(es, "f1_ek", [128, 1], F32)
    rank = ctx.sbuf(es, "f1_rank", [128, 1], F32)
    ovf = ctx.sbuf(es, "f1_ovf", [128, 1], F32)
    keep = ctx.sbuf(es, "f1_keep", [128, 1], F32)
    mk = ctx.sbuf(es, "f1_mk", [128, 1], F32)
    mkn = ctx.sbuf(es, "f1_mkn", [128, 1], F32)
    thtok = ctx.sbuf(es, "f1_thtok", [128, 1], F32)
    slotf = ctx.sbuf(es, "f1_slotf", [128, 1], F32)
    base0 = ctx.sbuf(es, "f1_base0", [1, 8], F32)
    base1 = ctx.sbuf(es, "f1_base1", [1, 8], F32)
    n0row = ctx.sbuf(es, "f1_n0row", [1, 8], F32)
    n0bc = ctx.sbuf(es, "f1_n0bc", [128, 8], F32)
    thbc = ctx.sbuf(es, "f1_thbc", [128, 8], F32)
    onescol = ctx.sbuf(es, "f1_onescol", [128, 1], F32)
    oh1_all = ctx.sbuf(es, "f1_oh1a", [128, NTT, 8], F32)
    oh2_all = ctx.sbuf(es, "f1_oh2a", [128, NTT, 8], F32)
    g1_all = ctx.sbuf(es, "f1_g1a", [128, NTT], F32)
    g2_all = ctx.sbuf(es, "f1_g2a", [128, NTT], F32)
    grow = ctx.sbuf(es, "f1_grow", [8, 2, 128], F32)
    if g_mode == "consume":
        gsb = ctx.sbuf(es, "f1_gsb", [64, NTOK], F32)
        cmp_bf = ctx.sbuf(es, "f1_cmp", [64, NTOK], BF16)
    elif g_mode == "local":
        gsb = ctx.sbuf(es, "f1_gsb", [8, NTOK], F32)
        cmp_bf = ctx.sbuf(es, "f1_cmp", [8, NTOK], BF16)
        th8 = ctx.sbuf(es, "f1_th8", [8, 1], F32)
        lo8 = ctx.sbuf(es, "f1_lo8", [8, 1], F32)
        hi8 = ctx.sbuf(es, "f1_hi8", [8, 1], F32)
        cnt8 = ctx.sbuf(es, "f1_cnt8", [8, 1], F32)
        ov8 = ctx.sbuf(es, "f1_ov8", [8, 1], F32)
        ov8n = ctx.sbuf(es, "f1_ov8n", [8, 1], F32)
        sc8 = ctx.sbuf(es, "f1_sc8", [8, 1], F32)
        throw_ = ctx.sbuf(es, "f1_throw", [1, 8], F32)
    cnt_p = ctx.sbuf(es, "f1_cntp", [64, 1], F32)
    th = ctx.sbuf(es, "f1_th", [1, 8], F32)
    lo = ctx.sbuf(es, "f1_lo", [1, 8], F32)
    hi = ctx.sbuf(es, "f1_hi", [1, 8], F32)
    ovr = ctx.sbuf(es, "f1_ovr", [1, 8], F32)
    ovrn = ctx.sbuf(es, "f1_ovrn", [1, 8], F32)
    trow = ctx.sbuf(es, "f1_trow", [1, 8], F32)
    tokb = ctx.sbuf(es, "f1_tokb", [128, 2, D], BF16)
    xsc = ctx.sbuf(es, "f1_xsc", [128, 2, D], BF16)

    @ctx.block.gpsimd
    def _(g):
        ctx.inc(g.dma_start(out=rg[:],
                            in_=w["router_gate"].rearrange("(c p) n -> p c n", p=128)),
                "dl", 16)
    wl = ctx.cnt["dl"]

    @ctx.block.vector
    def _(v):
        ctx.inc(v.memset(base0[:], 0.0), "dve", 1)
        ctx.inc(v.memset(base1[:], 0.0), "dve", 1)
        ctx.inc(v.memset(tokb[:, 0, :], 0.0), "dve", 1)
        if g_mode == "consume":
            ctx.inc(v.memset(lo[:], -1.0), "dve", 1)
            ctx.inc(v.memset(hi[:], 1.0), "dve", 1)
            ctx.inc(v.memset(th[:], 0.0), "dve", 1)
        elif g_mode == "local":
            ctx.inc(v.memset(lo8[:], -1.0), "dve", 1)
            ctx.inc(v.memset(hi8[:], 1.0), "dve", 1)
        ctx.inc(v.tensor_copy(out=onescol[:], in_=ctx.sb["c_ones_col_bf"][:]), "dve", 1)
    init_dve = ctx.cnt["dve"]

    if g_mode in ("consume", "local"):
        @ctx.block.gpsimd
        def _(g):
            ctx.wait(g, "dve", init_dve)
            ctx.inc(g.dma_start(out=dram["ye"][NEXP * CAP:NEXP * CAP + 1, :],
                                in_=tokb[0:1, 0, :]), "st", 16)

    def W(v):
        ctx.wait(v, "dve")

    lg_free = {0: init_dve, 1: init_dve}
    gr_free = {0: init_dve, 1: init_dve}
    gst = {}
    for t in range(NTT):
        par = t % 2
        ps = psA if par == 0 else psB

        @ctx.block.tensor
        def _(tb, t=t, ps=ps, par=par):
            if t == 0:
                ctx.wait(tb, "dl", wl)
            ctx.wait(tb, "dve", lg_free[par])
            for kc in range(DC):
                ctx.inc(tb.matmul(ps[:, 0:8], xT[:, kc, t * 128:(t + 1) * 128],
                                  rg[:, kc, :], start=(kc == 0), stop=(kc == DC - 1)),
                        "pe", 1)
        lg_mm = ctx.cnt["pe"]

        @ctx.block.vector
        def _(v, t=t, ps=ps, lg_mm=lg_mm):
            ctx.wait(v, "pe", lg_mm)
            ctx.inc(v.reduce_max(m1[:], ps[:, 0:8], axis=AX.X), "dve", 1)
            W(v)
            ctx.inc(v.tensor_scalar(out=mneg[:], in0=m1[:], scalar1=-1.0,
                                    scalar2=None, op0=ALU.mult), "dve", 1)
        sm1 = ctx.cnt["dve"]

        @ctx.block.scalar
        def _(s, ps=ps, sm1=sm1):
            ctx.wait(s, "dve", sm1)
            ctx.inc(s.activation(out=probs[:], in_=ps[:, 0:8], func=AF.Exp,
                                 bias=mneg[:], scale=1.0, accum_out=msum[:]), "act", 1)
        sm2 = ctx.cnt["act"]

        @ctx.block.vector
        def _(v, t=t, sm2=sm2):
            ctx.wait(v, "act", sm2)
            ctx.inc(v.reciprocal(mrcp[:], msum[:]), "dve", 1)
            W(v)
            ctx.inc(v.tensor_scalar(out=probs[:], in0=probs[:], scalar1=mrcp[:],
                                    scalar2=None, op0=ALU.mult), "dve", 1)
            W(v)
            ctx.inc(v.reduce_max(m1[:], probs[:], axis=AX.X), "dve", 1)
            W(v)
            ctx.inc(v.tensor_scalar(out=oh1[:], in0=probs[:], scalar1=m1[:],
                                    scalar2=None, op0=ALU.is_ge), "dve", 1)
            W(v)
            ctx.inc(v.scalar_tensor_tensor(out=mk2[:], in0=oh1[:], scalar=-10.0,
                                           in1=probs[:], op0=ALU.mult, op1=ALU.add),
                    "dve", 1)
            W(v)
            ctx.inc(v.reduce_max(m2[:], mk2[:], axis=AX.X), "dve", 1)
            W(v)
            ctx.inc(v.tensor_scalar(out=oh2[:], in0=mk2[:], scalar1=m2[:],
                                    scalar2=None, op0=ALU.is_ge), "dve", 1)
            W(v)
            ctx.inc(v.tensor_copy(out=oh1_all[:, t, :], in_=oh1[:]), "dve", 1)
            ctx.inc(v.tensor_copy(out=oh2_all[:, t, :], in_=oh2[:]), "dve", 1)
            ctx.inc(v.tensor_tensor(out=g1[:], in0=m1[:], in1=m2[:], op=ALU.add),
                    "dve", 1)
            W(v)
            ctx.inc(v.tensor_scalar(out=g1[:], in0=g1[:], scalar1=1e-9, scalar2=None,
                                    op0=ALU.add), "dve", 1)
            W(v)
            ctx.inc(v.reciprocal(g2[:], g1[:]), "dve", 1)
            W(v)
            ctx.inc(v.tensor_tensor(out=g1[:], in0=m1[:], in1=g2[:], op=ALU.mult),
                    "dve", 1)
            W(v)
            ctx.inc(v.tensor_tensor(out=g2[:], in0=m2[:], in1=g2[:], op=ALU.mult),
                    "dve", 1)
            W(v)
            ctx.inc(v.tensor_copy(out=g1_all[:, t:t + 1], in_=g1[:]), "dve", 1)
            ctx.inc(v.tensor_copy(out=g2_all[:, t:t + 1], in_=g2[:]), "dve", 1)
            ctx.inc(v.tensor_scalar(out=tmp8[:], in0=oh1[:], scalar1=g1[:],
                                    scalar2=None, op0=ALU.mult), "dve", 1)
            ctx.inc(v.tensor_scalar(out=gt8[:], in0=oh2[:], scalar1=g2[:],
                                    scalar2=None, op0=ALU.mult), "dve", 1)
            W(v)
            ctx.inc(v.tensor_tensor(out=gt8[:], in0=gt8[:], in1=tmp8[:], op=ALU.add),
                    "dve", 1)
        gt_dve = ctx.cnt["dve"]
        lg_free[par] = gt_dve

        if g_mode == "produce":
            @ctx.block.tensor
            def _(tb, t=t, par=par, gt_dve=gt_dve):
                ctx.wait(tb, "dve", gt_dve)
                ctx.wait(tb, "st", gst.get(t - 2, 0))
                ctx.inc(tb.transpose(psD[0:8, 0:128], gt8[:], ctx.sb["c_id_f32"][:]),
                        "pe", 1)
            gt_mm = ctx.cnt["pe"]

            @ctx.block.vector
            def _(v, t=t, par=par, gt_mm=gt_mm):
                ctx.wait(v, "pe", gt_mm)
                ctx.inc(v.tensor_copy(out=grow[:, par, :], in_=psD[0:8, 0:128]),
                        "dve", 1)
            gr_ev = ctx.cnt["dve"]

            @ctx.block.gpsimd
            def _(g, t=t, par=par, gr_ev=gr_ev):
                ctx.wait(g, "dve", gr_ev)
                ctx.inc(g.dma_start(out=g_locT[:, t * 128:(t + 1) * 128],
                                    in_=grow[:, par, :]), "st", 16)
                gst[t] = ctx.cnt["st"]
        elif g_mode == "local":
            @ctx.block.tensor
            def _(tb, t=t, gt_dve=gt_dve):
                ctx.wait(tb, "dve", gt_dve)
                ctx.wait(tb, "dve", gst.get(t - 1, 0))
                ctx.inc(tb.transpose(psD[0:8, 0:128], gt8[:], ctx.sb["c_id_f32"][:]),
                        "pe", 1)
            gt_mm = ctx.cnt["pe"]

            @ctx.block.vector
            def _(v, t=t, gt_mm=gt_mm):
                ctx.wait(v, "pe", gt_mm)
                ctx.inc(v.tensor_copy(out=gsb[:, t * 128:(t + 1) * 128],
                                      in_=psD[0:8, 0:128]), "dve", 1)
                gst[t] = ctx.cnt["dve"]
    pass1_st = ctx.cnt["st"]
    if g_mode == "produce":
        @ctx.block.gpsimd
        def _(g):
            ctx.wait(g, "st", pass1_st)
        return

    if g_mode == "local":
        gates_ready = ctx.cnt["dve"]
        for it in range(30):
            @ctx.block.vector
            def _(v, it=it):
                if it == 0:
                    ctx.wait(v, "dve", gates_ready)
                ctx.inc(v.tensor_tensor(out=th8[:], in0=lo8[:], in1=hi8[:],
                                        op=ALU.add), "dve", 1)
                W(v)
                ctx.inc(v.tensor_scalar(out=th8[:], in0=th8[:], scalar1=0.5,
                                        scalar2=None, op0=ALU.mult), "dve", 1)
                W(v)
                ctx.inc(v.tensor_scalar(out=cmp_bf[:], in0=gsb[:], scalar1=th8[:],
                                        scalar2=None, op0=ALU.is_gt), "dve", 1)
                W(v)
                ctx.inc(v.reduce_sum(cnt8[:], cmp_bf[:], axis=AX.X), "dve", 1)
                W(v)
                ctx.inc(v.tensor_scalar(out=ov8[:], in0=cnt8[:],
                                        scalar1=float(LCAP) + 0.5, scalar2=None,
                                        op0=ALU.is_ge), "dve", 1)
                W(v)
                ctx.inc(v.tensor_scalar(out=ov8n[:], in0=ov8[:], scalar1=-1.0,
                                        scalar2=1.0, op0=ALU.mult, op1=ALU.add),
                        "dve", 1)
                W(v)
                ctx.inc(v.tensor_tensor(out=lo8[:], in0=lo8[:], in1=ov8n[:],
                                        op=ALU.mult), "dve", 1)
                ctx.inc(v.tensor_tensor(out=sc8[:], in0=th8[:], in1=ov8[:],
                                        op=ALU.mult), "dve", 1)
                W(v)
                ctx.inc(v.tensor_tensor(out=lo8[:], in0=lo8[:], in1=sc8[:],
                                        op=ALU.add), "dve", 1)
                ctx.inc(v.tensor_tensor(out=hi8[:], in0=hi8[:], in1=ov8[:],
                                        op=ALU.mult), "dve", 1)
                W(v)
                ctx.inc(v.tensor_tensor(out=sc8[:], in0=th8[:], in1=ov8n[:],
                                        op=ALU.mult), "dve", 1)
                W(v)
                ctx.inc(v.tensor_tensor(out=hi8[:], in0=hi8[:], in1=sc8[:],
                                        op=ALU.add), "dve", 1)
        bis8 = ctx.cnt["dve"]

        @ctx.block.tensor
        def _(tb, bis8=bis8):
            ctx.wait(tb, "dve", bis8)
            ctx.inc(tb.transpose(psD[0:1, 0:8], hi8[:], ctx.sb["c_id_f32"][0:8, 0:8]),
                    "pe", 1)
        tr8 = ctx.cnt["pe"]

        @ctx.block.vector
        def _(v, tr8=tr8):
            ctx.wait(v, "pe", tr8)
            ctx.inc(v.tensor_copy(out=throw_[:], in_=psD[0:1, 0:8]), "dve", 1)
        th_dve2 = ctx.cnt["dve"]

        @ctx.block.tensor
        def _(tb, th_dve2=th_dve2):
            ctx.wait(tb, "dve", th_dve2)
            ctx.inc(tb.matmul(psD[:, 0:8], ctx.sb["c_ones_row"][:], throw_[:],
                              start=True, stop=True), "pe", 1)
        thbc_mm2 = ctx.cnt["pe"]

        @ctx.block.vector
        def _(v, thbc_mm2=thbc_mm2):
            ctx.wait(v, "pe", thbc_mm2)
            ctx.inc(v.tensor_copy(out=thbc[:], in_=psD[:, 0:8]), "dve", 1)
    else:
        @ctx.block.gpsimd
        def _(g):
            ctx.inc(g.dma_start(out=gsb[:], in_=g_all), "dl", 16)
        ga_ld = ctx.cnt["dl"]

    if g_mode == "consume":
        NIT_BIS = 30
        for it in range(NIT_BIS):
            @ctx.block.vector
            def _(v, it=it):
                if it == 0:
                    ctx.wait(v, "dl", ga_ld)
                ctx.inc(v.tensor_tensor(out=trow[:], in0=lo[:], in1=hi[:], op=ALU.add),
                        "dve", 1)
                W(v)
                ctx.inc(v.tensor_scalar(out=trow[:], in0=trow[:], scalar1=0.5,
                                        scalar2=None, op0=ALU.mult), "dve", 1)
            th_dve = ctx.cnt["dve"]

            @ctx.block.tensor
            def _(tb, it=it, th_dve=th_dve):
                ctx.wait(tb, "dve", th_dve)
                ctx.inc(tb.matmul(psD[0:64, 0:8], ctx.sb["c_ones_row"][:, 0:64], trow[:],
                                  start=True, stop=True), "pe", 1)
            bc_mm = ctx.cnt["pe"]

            @ctx.block.vector
            def _(v, it=it, bc_mm=bc_mm):
                ctx.wait(v, "pe", bc_mm)
                ctx.inc(v.tensor_tensor(out=tmp8[0:64, :],
                                        in0=ctx.sb["c_selE"][:], in1=psD[0:64, 0:8],
                                        op=ALU.mult), "dve", 1)
                W(v)
                ctx.inc(v.reduce_sum(cnt_p[:], tmp8[0:64, :], axis=AX.X), "dve", 1)
                W(v)
                ctx.inc(v.tensor_scalar(out=cmp_bf[:], in0=gsb[:], scalar1=cnt_p[:],
                                        scalar2=None, op0=ALU.is_gt), "dve", 1)
                W(v)
                ctx.inc(v.reduce_sum(cnt_p[:], cmp_bf[:], axis=AX.X), "dve", 1)
            cnt_dve = ctx.cnt["dve"]

            @ctx.block.tensor
            def _(tb, it=it, cnt_dve=cnt_dve):
                ctx.wait(tb, "dve", cnt_dve)
                ctx.inc(tb.matmul(psD[0:1, 8:16], cnt_p[:], ctx.sb["c_selE"][:],
                                  start=True, stop=True), "pe", 1)
            cc_mm = ctx.cnt["pe"]

            @ctx.block.vector
            def _(v, it=it, cc_mm=cc_mm):
                ctx.wait(v, "pe", cc_mm)
                ctx.inc(v.tensor_scalar(out=ovr[:], in0=psD[0:1, 8:16],
                                        scalar1=float(GCAP) + 0.5, scalar2=None,
                                        op0=ALU.is_ge), "dve", 1)
                W(v)
                ctx.inc(v.tensor_scalar(out=ovrn[:], in0=ovr[:], scalar1=-1.0,
                                        scalar2=1.0, op0=ALU.mult, op1=ALU.add), "dve", 1)
                W(v)
                ctx.inc(v.tensor_tensor(out=lo[:], in0=lo[:], in1=ovrn[:], op=ALU.mult),
                        "dve", 1)
                W(v)
                ctx.inc(v.tensor_tensor(out=tmp8[0:1, :], in0=trow[:], in1=ovr[:],
                                        op=ALU.mult), "dve", 1)
                W(v)
                ctx.inc(v.tensor_tensor(out=lo[:], in0=lo[:], in1=tmp8[0:1, :],
                                        op=ALU.add), "dve", 1)
                ctx.inc(v.tensor_tensor(out=hi[:], in0=hi[:], in1=ovr[:], op=ALU.mult),
                        "dve", 1)
                W(v)
                ctx.inc(v.tensor_tensor(out=tmp8[0:1, :], in0=trow[:], in1=ovrn[:],
                                        op=ALU.mult), "dve", 1)
                W(v)
                ctx.inc(v.tensor_tensor(out=hi[:], in0=hi[:], in1=tmp8[0:1, :],
                                        op=ALU.add), "dve", 1)
        bis_dve = ctx.cnt["dve"]

        @ctx.block.tensor
        def _(tb, bis_dve=bis_dve):
            ctx.wait(tb, "dve", bis_dve)
            ctx.inc(tb.matmul(psD[:, 0:8], ctx.sb["c_ones_row"][:], hi[:],
                              start=True, stop=True), "pe", 1)
        thbc_mm = ctx.cnt["pe"]

        @ctx.block.vector
        def _(v, thbc_mm=thbc_mm):
            ctx.wait(v, "pe", thbc_mm)
            ctx.inc(v.tensor_copy(out=thbc[:], in_=psD[:, 0:8]), "dve", 1)
    thbc_dve = ctx.cnt["dve"]

    pfx_free = thbc_dve
    tr_ev_prev = [thbc_dve]
    st_done = {}
    n0_dve = 0

    for k in (0, 1):
        oh_all = oh1_all if k == 0 else oh2_all
        gk_all = g1_all if k == 0 else g2_all
        basek = base0 if k == 0 else base1
        slotk = slot0 if k == 0 else slot1
        for t in range(NTT):
            par = t % 2

            @ctx.block.vector
            def _(v, t=t, k=k, oh_all=oh_all, gk_all=gk_all):
                if t == 0 and k == 0:
                    ctx.wait(v, "dve", thbc_dve)
                if k == 1 and t == 0:
                    ctx.wait(v, "dve", n0_dve)
                ctx.inc(v.tensor_tensor(out=tmp8[:], in0=oh_all[:, t, :], in1=thbc[:],
                                        op=ALU.mult), "dve", 1)
                W(v)
                ctx.inc(v.reduce_sum(thtok[:], tmp8[:], axis=AX.X), "dve", 1)
                W(v)
                ctx.inc(v.tensor_tensor(out=keep[:], in0=gk_all[:, t:t + 1],
                                        in1=thtok[:], op=ALU.is_gt), "dve", 1)
                ctx.inc(v.tensor_tensor(out=tmp8[:], in0=oh_all[:, t, :],
                                        in1=ctx.sb["c_iota8"][:], op=ALU.mult),
                        "dve", 1)
                W(v)
                ctx.inc(v.reduce_sum(ek[:], tmp8[:], axis=AX.X), "dve", 1)
                if k == 1:
                    ctx.inc(v.tensor_tensor(out=tmp8[:], in0=oh_all[:, t, :],
                                            in1=n0bc[:], op=ALU.mult), "dve", 1)
                    W(v)
                    ctx.inc(v.reduce_sum(thtok[:], tmp8[:], axis=AX.X), "dve", 1)
                W(v)
                ctx.inc(v.tensor_scalar(out=gt8[:], in0=oh_all[:, t, :],
                                        scalar1=keep[:], scalar2=None,
                                        op0=ALU.mult), "dve", 1)
            oh_dve = ctx.cnt["dve"]

            @ctx.block.tensor
            def _(tb, t=t, k=k, basek=basek, oh_dve=oh_dve):
                ctx.wait(tb, "dve", oh_dve)
                ctx.wait(tb, "dve", pfx_free)
                ctx.inc(tb.matmul(psD[:, 16:24], ctx.sb["c_ut_strict"][:], gt8[:],
                                  start=True, stop=False), "pe", 1)
                ctx.inc(tb.matmul(psD[:, 16:24], ctx.sb["c_ones_row"][:], basek[:],
                                  start=False, stop=True), "pe", 1)
                ctx.inc(tb.matmul(psD[0:1, 24:32], onescol[:], gt8[:],
                                  start=True, stop=True), "pe", 1)
            pfx_mm = ctx.cnt["pe"]

            @ctx.block.vector
            def _(v, t=t, k=k, basek=basek, slotk=slotk, pfx_mm=pfx_mm):
                ctx.wait(v, "pe", pfx_mm)
                ctx.inc(v.tensor_tensor(out=tmp8[:], in0=psD[:, 16:24], in1=gt8[:],
                                        op=ALU.mult), "dve", 1)
                W(v)
                ctx.inc(v.reduce_sum(rank[:], tmp8[:], axis=AX.X), "dve", 1)
                if k == 1:
                    W(v)
                    ctx.inc(v.tensor_tensor(out=rank[:], in0=rank[:], in1=thtok[:],
                                            op=ALU.add), "dve", 1)
                ctx.inc(v.tensor_tensor(out=basek[:], in0=basek[:],
                                        in1=psD[0:1, 24:32], op=ALU.add), "dve", 1)
                W(v)
                ctx.inc(v.scalar_tensor_tensor(out=slotf[:], in0=ek[:],
                                               scalar=float(CAP), in1=rank[:],
                                               op0=ALU.mult, op1=ALU.add), "dve", 1)
                ctx.inc(v.tensor_scalar(out=ovf[:], in0=rank[:],
                                        scalar1=float(CAP) - 0.5, scalar2=None,
                                        op0=ALU.is_ge), "dve", 1)
                W(v)
                ctx.inc(v.tensor_scalar(out=ovf[:], in0=ovf[:], scalar1=-1.0,
                                        scalar2=1.0, op0=ALU.mult, op1=ALU.add),
                        "dve", 1)
                W(v)
                ctx.inc(v.tensor_tensor(out=mk[:], in0=keep[:], in1=ovf[:],
                                        op=ALU.mult), "dve", 1)
                W(v)
                ctx.inc(v.tensor_scalar(out=mkn[:], in0=mk[:], scalar1=-1.0,
                                        scalar2=1.0, op0=ALU.mult, op1=ALU.add),
                        "dve", 1)
                W(v)
                ctx.inc(v.tensor_tensor(out=slotf[:], in0=slotf[:], in1=mk[:],
                                        op=ALU.mult), "dve", 1)
                W(v)
                ctx.inc(v.scalar_tensor_tensor(out=slotf[:], in0=mkn[:],
                                               scalar=float(NEXP * CAP), in1=slotf[:],
                                               op0=ALU.mult, op1=ALU.add), "dve", 1)
                W(v)
                ctx.inc(v.tensor_copy(out=slotk[:, t:t + 1], in_=slotf[:]), "dve", 1)
            pfx_free = ctx.cnt["dve"]

            @ctx.block.tensor
            def _(tb, t=t, k=k, par=par):
                idx = k * NTT + t
                if idx >= 2:
                    ctx.wait(tb, "st", st_done[idx - 2])
                ctx.wait(tb, "dve", tr_ev_prev[0])
                for c in range(DC):
                    ctx.inc(tb.transpose(psC_bf[:, c * 128:(c + 1) * 128],
                                         xT[:, c, t * 128:(t + 1) * 128],
                                         ctx.sb["c_id_bf"][:]), "pe", 1)
            tr_mm = ctx.cnt["pe"]

            @ctx.block.vector
            def _(v, t=t, k=k, par=par, tr_mm=tr_mm, gk_all=gk_all):
                ctx.wait(v, "pe", tr_mm)
                ctx.inc(v.tensor_copy(out=tokb[:, par, :], in_=psC_bf[:, 0:1024]),
                        "dve", 1)
                tr_ev_prev[0] = ctx.cnt["dve"]
                W(v)
                ctx.inc(v.tensor_scalar(out=xsc[:, par, :], in0=tokb[:, par, :],
                                        scalar1=gk_all[:, t:t + 1], scalar2=None,
                                        op0=ALU.mult), "dve", 1)
            tr_ev = ctx.cnt["dve"]

            @ctx.block.gpsimd
            def _(g, t=t, k=k, par=par, tr_ev=tr_ev, slotk=slotk):
                ctx.wait(g, "dve", tr_ev)
                if k == 0:
                    ctx.inc(g.dma_start(out=dram["xstage"][t * 128:(t + 1) * 128, :],
                                        in_=tokb[:, par, :]), "st", 16)
                ctx.inc(g.indirect_dma_start(
                    out=dram["xe"],
                    out_offset=bass.IndirectOffsetOnAxis(slotk[:, t:t + 1], 0),
                    in_=tokb[:, par, :], in_offset=None), "st", 16)
                ctx.inc(g.indirect_dma_start(
                    out=dram["xte"],
                    out_offset=bass.IndirectOffsetOnAxis(slotk[:, t:t + 1], 0),
                    in_=xsc[:, par, :], in_offset=None), "st", 16)
                st_done[k * NTT + t] = ctx.cnt["st"]

        if k == 0:
            @ctx.block.vector
            def _(v):
                ctx.inc(v.tensor_copy(out=n0row[:], in_=base0[:]), "dve", 1)
            nr = ctx.cnt["dve"]

            @ctx.block.tensor
            def _(tb, nr=nr):
                ctx.wait(tb, "dve", nr)
                ctx.wait(tb, "dve", pfx_free)
                ctx.inc(tb.matmul(psD[:, 0:8], ctx.sb["c_ones_row"][:], n0row[:],
                                  start=True, stop=True), "pe", 1)
            nm = ctx.cnt["pe"]

            @ctx.block.vector
            def _(v, nm=nm):
                ctx.wait(v, "pe", nm)
                ctx.inc(v.tensor_copy(out=n0bc[:], in_=psD[:, 0:8]), "dve", 1)
            n0_dve = ctx.cnt["dve"]
            pfx_free = n0_dve

    @ctx.block.gpsimd
    def _(g):
        ctx.wait(g, "st")


def emit_l1_ffn_b(ctx, es, w, dram):
    nc = ctx.nc
    psA, psB, psC, psD = (ctx.ps[n] for n in ("psA", "psB", "psC", "psD"))
    psW0, psW1 = ctx.ps["psW0"], ctx.ps["psW1"]
    psA_bf = psA[:].bitcast(BF16)
    NCT = CAP // 128

    hT = ctx.sbuf(es, "f1_hT", [128, 32, CAP], BF16)
    xeT = ctx.sbuf(es, "f1_xeT", [128, DC, CAP], BF16)
    xteT = ctx.sbuf(es, "f1_xteT", [128, DC, CAP], BF16)
    tokl = ctx.sbuf(es, "f1_tokl", [128, 2, D], BF16)
    wg = ctx.sbuf(es, "f1_wg", [128, 2, DC, 128], BF16)
    wu = ctx.sbuf(es, "f1_wu", [128, 2, DC, 128], BF16)
    dw = ctx.sbuf(es, "f1_dw", [128, 2, DC, 128], BF16)
    silb = ctx.sbuf(es, "f1_silb", [128, 2, 512], BF16)
    yebf = ctx.sbuf(es, "f1_yebf", [128, DC, 512], BF16)
    ytok = ctx.sbuf(es, "f1_ytok", [128, 2, D], BF16)

    Cbs = [(0, 512), (512, 512), (1024, CAP - 1024)]

    ld_prev = {0: 0, 1: 0}
    ps_free = 0
    cd_free = 0
    st_prev = {0: 0, 1: 0}
    hT_free = 0

    for e in range(min(NEXP, B_NEXP)):
        if B_LEVEL < 1:
            break
        last_ev = cd_free
        for which, dst in (("xe", xeT), ("xte", xteT)):
            for ct in range(NCT):
                par = ct % 2

                @ctx.block.gpsimd
                def _(g, which=which, e=e, ct=ct, par=par):
                    ctx.wait(g, "pe", ld_prev[par])
                    ctx.inc(g.dma_start(
                        out=tokl[:, par, :],
                        in_=dram[which][e * CAP + ct * 128:e * CAP + (ct + 1) * 128, :]),
                        "dl", 16)
                ldd = ctx.cnt["dl"]

                @ctx.block.tensor
                def _(tb, which=which, ct=ct, par=par, ldd=ldd, e=e, last_ev=last_ev):
                    ctx.wait(tb, "dl", ldd)
                    ctx.wait(tb, "dve", last_ev)
                    for c in range(DC):
                        ctx.inc(tb.transpose(psA_bf[:, c * 128:(c + 1) * 128],
                                             tokl[:, par, c * 128:(c + 1) * 128],
                                             ctx.sb["c_id_bf"][:]), "pe", 1)
                ld_prev[par] = ctx.cnt["pe"]
                tr_mm = ctx.cnt["pe"]

                @ctx.block.vector
                def _(v, which=which, dst=dst, ct=ct, tr_mm=tr_mm):
                    ctx.wait(v, "pe", tr_mm)
                    for c in range(DC):
                        ctx.inc(v.tensor_copy(
                            out=dst[:, c, ct * 128:(ct + 1) * 128],
                            in_=psA_bf[:, c * 128:(c + 1) * 128]), "dve", 1)
                last_ev = ctx.cnt["dve"]
        xe_done = ctx.cnt["dve"]
        if B_LEVEL < 2:
            continue

        epi_ev = {}
        for It in range(32):
            par = It % 2

            @ctx.block.gpsimd
            def _(g, e=e, It=It, par=par):
                ctx.wait(g, "pe", ld_prev[("w", par)] if ("w", par) in ld_prev else 0)
                ctx.inc(g.dma_start(
                    out=wg[:, par], in_=w["gate_w"][e][:, It * 128:(It + 1) * 128]
                    .rearrange("(c p) n -> p c n", p=128)), "dl", 16)
                ctx.inc(g.dma_start(
                    out=wu[:, par], in_=w["up_w"][e][:, It * 128:(It + 1) * 128]
                    .rearrange("(c p) n -> p c n", p=128)), "dl", 16)
            wld = ctx.cnt["dl"]

            for cb, (c0, cn) in enumerate(Cbs):
                @ctx.block.tensor
                def _(tb, It=It, par=par, cb=cb, c0=c0, cn=cn, wld=wld, e=e):
                    if It == 0 and cb == 0:
                        ctx.wait(tb, "dve", xe_done)
                    ctx.wait(tb, "dl", wld)
                    if It >= 1 or cb >= 1:
                        ctx.wait(tb, "act", epi_ev.get(("sil", It, cb - 1),
                                                       epi_ev.get(("sil", It - 1, 2), 0)))
                        ctx.wait(tb, "dve", epi_ev.get(("ev", It, cb - 1),
                                                       epi_ev.get(("ev", It - 1, 2), 0)))
                    for kc in range(DC):
                        ctx.inc(tb.matmul(psC[:, 0:cn],
                                          wg[:, par, kc, :], xeT[:, kc, c0:c0 + cn],
                                          start=(kc == 0), stop=(kc == DC - 1)), "pe", 1)
                    for kc in range(DC):
                        ctx.inc(tb.matmul(psD[:, 0:cn],
                                          wu[:, par, kc, :], xteT[:, kc, c0:c0 + cn],
                                          start=(kc == 0), stop=(kc == DC - 1)), "pe", 1)
                    ld_prev[("w", par)] = ctx.cnt["pe"]
                gu_mm = ctx.cnt["pe"]

                @ctx.block.scalar
                def _(s, It=It, cb=cb, cn=cn, gu_mm=gu_mm):
                    ctx.wait(s, "pe", gu_mm)
                    ctx.inc(s.activation(out=silb[:, cb % 2, 0:cn], in_=psC[:, 0:cn],
                                         func=AF.Silu), "act", 1)
                epi_ev[("sil", It, cb)] = ctx.cnt["act"]

                @ctx.block.vector
                def _(v, It=It, cb=cb, c0=c0, cn=cn):
                    ctx.wait(v, "act", epi_ev[("sil", It, cb)])
                    ctx.inc(v.tensor_tensor(out=hT[:, It, c0:c0 + cn],
                                            in0=silb[:, cb % 2, 0:cn],
                                            in1=psD[:, 0:cn], op=ALU.mult), "dve", 1)
                epi_ev[("ev", It, cb)] = ctx.cnt["dve"]
        h_done = ctx.cnt["dve"]
        if B_LEVEL < 3:
            continue

        yslices = [psA[:, 0:512], psB[:, 0:512], psC[:, 0:512], psD[:, 0:512],
                   psW0[:, 0:512], psW0[:, 512:1024], psW1[:, 0:512],
                   psW1[:, 512:1024]]
        for cb, (c0, cn) in enumerate(Cbs):
            dn_wld = {}
            for It in range(32):
                par = It % 2

                @ctx.block.gpsimd
                def _(g, e=e, It=It, par=par, cb=cb):
                    ctx.wait(g, "pe", ld_prev.get(("d", par), 0))
                    ctx.inc(g.dma_start(
                        out=dw[:, par], in_=w["down_w"][e][It * 128:(It + 1) * 128, :]
                        .rearrange("p (c n) -> p c n", n=128)), "dl", 16)
                dn_wld[It] = ctx.cnt["dl"]

                @ctx.block.tensor
                def _(tb, It=It, par=par, cb=cb, c0=c0, cn=cn, e=e):
                    ctx.wait(tb, "dl", dn_wld[It])
                    if It == 0 and cb == 0:
                        ctx.wait(tb, "dve", h_done)
                        ctx.wait(tb, "dve", ps_free)
                    elif It == 0:
                        ctx.wait(tb, "dve", ps_free)
                    for mt in range(DC):
                        ctx.inc(tb.matmul(yslices[mt][:, 0:cn], dw[:, par, mt, :],
                                          hT[:, It, c0:c0 + cn],
                                          start=(It == 0), stop=(It == 31)), "pe", 1)
                    ld_prev[("d", par)] = ctx.cnt["pe"]
            dn_mm = ctx.cnt["pe"]
            hT_free = dn_mm

            @ctx.block.vector
            def _(v, cb=cb, cn=cn, dn_mm=dn_mm):
                ctx.wait(v, "pe", dn_mm)
                for mt in range(DC):
                    ctx.inc(v.tensor_copy(out=yebf[:, mt, 0:cn], in_=yslices[mt]
                                          if cn == 512 else yslices[mt][:, 0:cn]),
                            "dve", 1)
            ye_ev = ctx.cnt["dve"]

            ytt = cn // 128
            yev_prev = ye_ev
            for yt in range(ytt):
                par = yt % 2

                @ctx.block.tensor
                def _(tb, yt=yt, cb=cb, yev_prev=yev_prev, par=par):
                    ctx.wait(tb, "dve", yev_prev)
                    ctx.wait(tb, "st", st_prev[par])
                    for c in range(DC):
                        ctx.inc(tb.transpose(psA_bf[:, c * 128:(c + 1) * 128],
                                             yebf[:, c, yt * 128:(yt + 1) * 128],
                                             ctx.sb["c_id_bf"][:]), "pe", 1)
                ytr = ctx.cnt["pe"]

                @ctx.block.vector
                def _(v, yt=yt, par=par, ytr=ytr):
                    ctx.wait(v, "pe", ytr)
                    ctx.inc(v.tensor_copy(out=ytok[:, par, :], in_=psA_bf[:, 0:1024]),
                            "dve", 1)
                yev = ctx.cnt["dve"]
                yev_prev = yev

                @ctx.block.gpsimd
                def _(g, e=e, cb=cb, yt=yt, par=par, yev=yev):
                    ctx.wait(g, "dve", yev)
                    r0 = e * CAP + cb * 512 + yt * 128
                    ctx.inc(g.dma_start(out=dram["ye"][r0:r0 + 128, :],
                                        in_=ytok[:, par, :]), "st", 16)
                    st_prev[par] = ctx.cnt["st"]
            ps_free = ctx.cnt["dve"]
        cd_free = ctx.cnt["dve"]

    @ctx.block.gpsimd
    def _(g):
        ctx.wait(g, "st")


def emit_l1_ffn_c(ctx, es, dram, x_res_ap, out_ap, res_f32):
    nc = ctx.nc
    slot0 = ctx.sb["slot0"]
    slot1 = ctx.sb["slot1"]
    y0 = ctx.sbuf(es, "f1_y0", [128, 2, D], BF16)
    y1 = ctx.sbuf(es, "f1_y1", [128, 2, D], BF16)
    xres = ctx.sbuf(es, "f1_xres", [128, 2, D], F32 if res_f32 else BF16)
    zb = ctx.sbuf(es, "f1_zb", [128, 2, D], F32)
    ob = ctx.sbuf(es, "f1_ob", [128, 2, D], F32)
    zsq = ctx.sbuf(es, "f1_zsq", [128, 2, D], F32)
    smean = ctx.sbuf(es, "f1_smean", [128, 1], F32)
    svar = ctx.sbuf(es, "f1_svar", [128, 1], F32)
    srstd = ctx.sbuf(es, "f1_srstd", [128, 1], F32)

    gather_done = {}
    z_done = {}
    o_done = {}
    st_done = {}
    for t in range(NTT):
        par = t % 2

        @ctx.block.gpsimd
        def _(g, t=t, par=par):
            if t >= 2:
                ctx.wait(g, "dve", z_done[t - 2])
            ctx.inc(g.memset(y0[:, par, :], 0.0), "po", 1)
            ctx.inc(g.memset(y1[:, par, :], 0.0), "po", 1)
            g.wait_ge(ctx.sem["po"], ctx.cnt["po"])
            ctx.inc(g.indirect_dma_start(
                out=y0[:, par, :], out_offset=None, in_=dram["ye"],
                in_offset=bass.IndirectOffsetOnAxis(slot0[:, t:t + 1], 0)), "dl", 16)
            ctx.inc(g.indirect_dma_start(
                out=y1[:, par, :], out_offset=None, in_=dram["ye"],
                in_offset=bass.IndirectOffsetOnAxis(slot1[:, t:t + 1], 0)), "dl", 16)
            ctx.inc(g.dma_start(out=xres[:, par, :],
                                in_=x_res_ap[t * 128:(t + 1) * 128, :]), "dl", 16)
        g_ld = ctx.cnt["dl"]

        @ctx.block.vector
        def _(v, t=t, par=par, g_ld=g_ld):
            ctx.wait(v, "dl", g_ld)
            ctx.inc(v.tensor_tensor(out=zb[:, par, :], in0=y0[:, par, :],
                                    in1=y1[:, par, :], op=ALU.add), "dve", 1)
            ctx.wait(v, "dve")
            ctx.inc(v.tensor_tensor(out=zb[:, par, :], in0=zb[:, par, :],
                                    in1=xres[:, par, :], op=ALU.add), "dve", 1)
            ctx.wait(v, "dve")
            ctx.inc(v.reduce_sum(smean[:], zb[:, par, :], axis=AX.X), "dve", 1)
            ctx.wait(v, "dve")
            ctx.inc(v.tensor_scalar(out=smean[:], in0=smean[:], scalar1=1.0 / D,
                                    scalar2=None, op0=ALU.mult), "dve", 1)
        z1 = ctx.cnt["dve"]
        z_done[t] = z1

        @ctx.block.scalar
        def _(s, t=t, par=par, z1=z1):
            ctx.wait(s, "dve", z1)
            ctx.inc(s.activation(out=zsq[:, par, :], in_=zb[:, par, :],
                                 func=AF.Square), "act", 1)
        sq_a = ctx.cnt["act"]

        @ctx.block.vector
        def _(v, t=t, par=par, sq_a=sq_a):
            ctx.wait(v, "act", sq_a)
            ctx.inc(v.reduce_sum(svar[:], zsq[:, par, :], axis=AX.X), "dve", 1)
            ctx.wait(v, "dve")
            ctx.inc(v.tensor_scalar(out=svar[:], in0=svar[:], scalar1=1.0 / D,
                                    scalar2=None, op0=ALU.mult), "dve", 1)
            ctx.wait(v, "dve")
            ctx.inc(v.scalar_tensor_tensor(out=srstd[:], in0=smean[:], scalar=-1.0,
                                           in1=smean[:], op0=ALU.mult, op1=ALU.mult),
                    "dve", 1)
            ctx.wait(v, "dve")
            ctx.inc(v.tensor_tensor(out=svar[:], in0=svar[:], in1=srstd[:],
                                    op=ALU.add), "dve", 1)
            ctx.wait(v, "dve")
            ctx.inc(v.tensor_scalar(out=svar[:], in0=svar[:], scalar1=1e-5,
                                    scalar2=None, op0=ALU.add), "dve", 1)
        v1 = ctx.cnt["dve"]

        @ctx.block.scalar
        def _(s, v1=v1):
            ctx.wait(s, "dve", v1)
            ctx.inc(s.activation(out=svar[:], in_=svar[:], func=AF.Ln), "act", 1)
            ctx.wait(s, "act")
            ctx.inc(s.activation(out=srstd[:], in_=svar[:], func=AF.Exp,
                                 scale=-0.5), "act", 1)
        sq_b = ctx.cnt["act"]

        @ctx.block.vector
        def _(v, t=t, par=par, sq_b=sq_b):
            ctx.wait(v, "act", sq_b)
            ctx.inc(v.tensor_scalar(out=ob[:, par, :], in0=zb[:, par, :],
                                    scalar1=smean[:], scalar2=srstd[:],
                                    op0=ALU.subtract, op1=ALU.mult), "dve", 1)
        o_done[t] = ctx.cnt["dve"]

        @ctx.block.sync
        def _(sy, t=t, par=par):
            ctx.wait(sy, "dve", o_done[t])
            ctx.inc(sy.dma_start(out=out_ap[t * 128:(t + 1) * 128, :],
                                 in_=ob[:, par, :]), "st", 16)
            st_done[t] = ctx.cnt["st"]

    @ctx.block.sync
    def _(sy):
        ctx.wait(sy, "st")



def build_program(phases, weight_shapes):
    nc = bass.Bass("TRN2", target_bir_lowering=False)
    caps = declare_consts(nc)
    waps = {}
    for name, shape in weight_shapes.items():
        waps[name] = nc.dram_tensor(name, list(shape), F32, kind="ExternalInput").ap()
    x_in = nc.dram_tensor("x_in", [NTOK, D], F32, kind="ExternalInput").ap()
    x_out = nc.dram_tensor("x_out", [NTOK, D], F32, kind="ExternalOutput").ap()
    dbg = {}
    if DBG:
        for nm, shape in (("dbg_xT", [128, DC, 256]), ("dbg_q", [128, DC, 128]),
                          ("dbg_k", [128, DC, 128]), ("dbg_v", [128, 1024]),
                          ("dbg_at", [128, DC, 128]), ("dbg_scm", [128, 640]),
                          ("dbg_p", [128, 640]), ("dbg_z", [128, DC, 128]),
                          ("dbg_pt", [128, 5, 128]), ("dbg_st", [128, 4])):
            dbg[nm] = nc.dram_tensor(nm, shape, F32, kind="ExternalOutput").ap()
    nc._dbg = dbg

    root = ExitStack()
    with root:
        with nc.Block() as block:
            ctx = Ctx(nc, root, block)
            ctx.ps = {}
            for nm, shape in (("psA", [128, 512]), ("psB", [128, 512]),
                              ("psC", [128, 512]), ("psD", [128, 512]),
                              ("psW0", [128, 1024]), ("psW1", [128, 1024])):
                ctx.ps[nm] = root.enter_context(nc.psum_tensor(nm, shape, F32))
            ctx.sbuf(root, "ln_zbf", [128, DC, 512], BF16)
            ctx.sbuf(root, "ln_zsq", [128, 2, 512], BF16)
            ctx.sbuf(root, "ln_stats", [1, 3, 512], F32)
            ctx.sbuf(root, "slot0", [128, NTT], U32)
            ctx.sbuf(root, "slot1", [128, NTT], U32)
            load_consts(ctx, root, caps)

            l1f_any = ("l1_ffn" in phases or "l1_ffn_local" in phases
                       or "l1_ffn_a_only" in phases)
            if l1f_any or "l1_ffn_g" in phases:
                dram = {
                    "xe": nc.dram_tensor("xe_buf", [NEXP * CAP + 1, D], BF16).ap(),
                    "xte": nc.dram_tensor("xte_buf", [NEXP * CAP + 1, D], BF16).ap(),
                    "ye": nc.dram_tensor("ye_buf", [NEXP * CAP + 1, D], BF16).ap(),
                    "xstage": nc.dram_tensor("xstage", [NTOK, D], BF16).ap(),
                }

            with ExitStack() as xes:
                ctx.sbuf(xes, "xT", [128, DC, NTOK], BF16)

                with ExitStack() as pes:
                    emit_stream_load(ctx, pes, x_in)
                    ctx.fence()

                if "l0_attn" in phases:
                    with ExitStack() as pes:
                        emit_l0_attn(ctx, pes, {
                            "qkv_kernel": waps["L0_attn_qkv_kernel"],
                            "qkv_bias": waps["L0_attn_qkv_bias"],
                            "out_kernel": waps["L0_attn_out_kernel"],
                            "out_bias": waps["L0_attn_out_bias"]})
                        ctx.fence()

                if "l0_ffn" in phases:
                    with ExitStack() as pes:
                        emit_l0_ffn(ctx, pes, {
                            "gate_weight": waps["L0_ffn_gate_weight"],
                            "e_score_correction_bias": waps["L0_ffn_e_score_correction_bias"],
                            "experts_gate_up": waps["L0_ffn_experts_gate_up"],
                            "experts_down": waps["L0_ffn_experts_down"],
                            "shared_gate_w": waps["L0_ffn_shared_gate_w"],
                            "shared_up_w": waps["L0_ffn_shared_up_w"],
                            "shared_down_w": waps["L0_ffn_shared_down_w"]})
                        ctx.fence()

                if "l1_attn" in phases:
                    with ExitStack() as pes:
                        emit_l1_attn(ctx, pes, {
                            "Wq": waps["L1_attn_Wq"], "Wkv": waps["L1_attn_Wkv"],
                            "Wv": waps["L1_attn_Wv"], "Wo": waps["L1_attn_Wo"],
                            "wk_pad": waps["wk_pad"], "qg_col": waps["qg_col"],
                            "kg_col": waps["kg_col"]})
                        ctx.fence()

                if "l1_ffn_a_only" in phases:
                    g_all_in2 = nc.dram_tensor("g_all_in", [64, NTOK], F32,
                                               kind="ExternalInput").ap()
                    with ExitStack() as pes:
                        emit_l1_ffn_a(ctx, pes, {
                            "router_gate": waps["L1_ffn_router_gate"]}, dram,
                            g_mode="consume", g_ap=g_all_in2)
                        ctx.fence()

                if "l1_ffn_g" in phases:
                    g_out = nc.dram_tensor("g_loc_out", [8, NTOK], F32,
                                           kind="ExternalOutput").ap()
                    with ExitStack() as pes:
                        emit_l1_ffn_a(ctx, pes, {
                            "router_gate": waps["L1_ffn_router_gate"]}, None,
                            g_mode="produce", g_ap=g_out)
                        ctx.fence()

                if "l1_ffn_local" in phases:
                    with ExitStack() as pes:
                        emit_l1_ffn_a(ctx, pes, {
                            "router_gate": waps["L1_ffn_router_gate"]}, dram,
                            g_mode="local", g_ap=None)
                        ctx.fence()
                elif "l1_ffn" in phases:
                    g_all_in = nc.dram_tensor("g_all_in", [64, NTOK], F32,
                                              kind="ExternalInput").ap()
                    with ExitStack() as pes:
                        emit_l1_ffn_a(ctx, pes, {
                            "router_gate": waps["L1_ffn_router_gate"]}, dram,
                            g_mode="consume", g_ap=g_all_in)
                        ctx.fence()
                else:
                    with ExitStack() as pes:
                        emit_stream_store(ctx, pes, x_out)
                        ctx.fence()
            if "l1_ffn_a_only" in phases:
                s0d = nc.dram_tensor("slots0_d", [128, NTT], F32, kind="ExternalOutput").ap()
                s1d = nc.dram_tensor("slots1_d", [128, NTT], F32, kind="ExternalOutput").ap()
                sf0 = ctx.sbuf(root, "dbg_sf0", [128, NTT], F32)
                sf1 = ctx.sbuf(root, "dbg_sf1", [128, NTT], F32)

                @ctx.block.vector
                def _(v):
                    ctx.inc(v.tensor_copy(out=sf0[:], in_=ctx.sb["slot0"][:]), "dve", 1)
                    ctx.inc(v.tensor_copy(out=sf1[:], in_=ctx.sb["slot1"][:]), "dve", 1)
                dbg_dve = ctx.cnt["dve"]

                @ctx.block.sync
                def _(sy):
                    ctx.wait(sy, "dve", dbg_dve)
                    ctx.inc(sy.dma_start(out=s0d, in_=sf0[:]), "st", 16)
                    ctx.inc(sy.dma_start(out=s1d, in_=sf1[:]), "st", 16)
                    ctx.wait(sy, "st")

                @ctx.block.gpsimd
                def _(g):
                    ctx.wait(g, "dve", dbg_dve)
                    ctx.inc(g.dma_start(out=x_out[0:NTOK, :], in_=dram["xstage"]), "st", 16)
                    ctx.wait(g, "st")

            if "l1_ffn" in phases or "l1_ffn_local" in phases:
                if not SKIP_B:
                    with ExitStack() as pes:
                        emit_l1_ffn_b(ctx, pes, {
                            "gate_w": waps["L1_ffn_gate_w"],
                            "up_w": waps["L1_ffn_up_w"],
                            "down_w": waps["L1_ffn_down_w"]}, dram)
                        ctx.fence()
                if not SKIP_C:
                    with ExitStack() as pes:
                        if "fused" in phases:
                            emit_l1_ffn_c(ctx, pes, dram, dram["xstage"], x_out, False)
                        else:
                            emit_l1_ffn_c(ctx, pes, dram, x_in, x_out, True)
    return nc


WS_L0A = {
    "L0_attn_qkv_kernel": (1024, 3072),
    "L0_attn_qkv_bias": (3072,),
    "L0_attn_out_kernel": (1024, 1024),
    "L0_attn_out_bias": (1024,),
}
WS_L0F = {
    "L0_ffn_gate_weight": (32, 1024),
    "L0_ffn_e_score_correction_bias": (32,),
    "L0_ffn_experts_gate_up": (32, 1024, 1024),
    "L0_ffn_experts_down": (32, 1024, 512),
    "L0_ffn_shared_gate_w": (1024, 1024),
    "L0_ffn_shared_up_w": (1024, 1024),
    "L0_ffn_shared_down_w": (1024, 1024),
}
WS_L1A = {
    "L1_attn_Wq": (1024, 1024),
    "L1_attn_Wkv": (1024, 128),
    "L1_attn_Wv": (128, 512),
    "L1_attn_Wo": (1024, 1024),
    "wk_pad": (128, 1024),
    "qg_col": (128, 1),
    "kg_col": (128, 1),
    "L1_ffn_router_gate": (1024, 8),
}
WS_L1F = {
    "L1_ffn_router_gate": (1024, 8),
    "L1_ffn_gate_w": (8, 1024, 4096),
    "L1_ffn_up_w": (8, 1024, 4096),
    "L1_ffn_down_w": (8, 4096, 1024),
}

_PROG_CACHE = {}


def _get_prog(key, phases, wshapes):
    if key not in _PROG_CACHE:
        _PROG_CACHE[key] = build_program(phases, wshapes)
    return _PROG_CACHE[key]


def _derived_l1a(inputs):
    Wk = np.ascontiguousarray(inputs["L1_attn_Wk"], dtype=np.float32)
    wk_pad = np.zeros((128, 1024), np.float32)
    for h in range(16):
        wk_pad[:, h * 64:h * 64 + 32] = Wk[:, h * 32:(h + 1) * 32]
    qg_col = np.tile(np.ascontiguousarray(inputs["L1_attn_q_g"], np.float32),
                     2).reshape(128, 1)
    kg_col = np.tile(np.ascontiguousarray(inputs["L1_attn_k_g"], np.float32),
                     4).reshape(128, 1)
    return wk_pad, qg_col, kg_col


def _launch(nc, base_map, xs):
    in_maps = []
    for c in range(NCORES):
        m = dict(base_map)
        m["x_in"] = xs[c]
        in_maps.append(m)
    return run_bass_kernel_spmd(nc, in_maps, core_ids=list(range(NCORES)))


FUSED = True

WS_ALL = {}
WS_ALL.update(WS_L0A)
WS_ALL.update(WS_L0F)
WS_ALL.update(WS_L1A)
WS_ALL.update(WS_L1F)


def kernel(**inputs):
    if FUSED:
        return kernel_fused(**inputs)
    return kernel_4launch(**inputs)


def kernel_fused(**inputs):
    f32c = lambda a: np.ascontiguousarray(a, dtype=np.float32)
    x = f32c(inputs["L0_attn_x"])
    nc = _get_prog("fused", ["l0_attn", "l0_ffn", "l1_attn", "l1_ffn_local",
                             "fused"], WS_ALL)
    wk_pad, qg_col, kg_col = _derived_l1a(inputs)
    m = {k: f32c(inputs[k]) for k in WS_ALL
         if k not in ("wk_pad", "qg_col", "kg_col")}
    m["wk_pad"] = wk_pad
    m["qg_col"] = qg_col
    m["kg_col"] = kg_col
    m.update(CONSTS)
    xs = [np.ascontiguousarray(x[2 * c:2 * c + 2].reshape(NTOK, D))
          for c in range(NCORES)]
    r = _launch(nc, m, xs)
    out = np.empty((16, SEQ, D), np.float32)
    for c in range(NCORES):
        out[2 * c:2 * c + 2] = r.results[c]["x_out"].reshape(2, SEQ, D)
    return out


def kernel_4launch(**inputs):
    f32c = lambda a: np.ascontiguousarray(a, dtype=np.float32)
    x = f32c(inputs["L0_attn_x"])
    xs = [np.ascontiguousarray(x[2 * c:2 * c + 2].reshape(NTOK, D))
          for c in range(NCORES)]

    nc1 = _get_prog("l0a", ["l0_attn"], WS_L0A)
    m = {k: f32c(inputs[k]) for k in WS_L0A}
    m.update(CONSTS)
    r = _launch(nc1, m, xs)
    xs = [r.results[c]["x_out"] for c in range(NCORES)]

    nc2 = _get_prog("l0f", ["l0_ffn"], WS_L0F)
    m = {k: f32c(inputs[k]) for k in WS_L0F}
    m.update(CONSTS)
    r = _launch(nc2, m, xs)
    xs = [r.results[c]["x_out"] for c in range(NCORES)]

    nc3 = _get_prog("l1a", ["l1_attn", "l1_ffn_g"], WS_L1A)
    wk_pad, qg_col, kg_col = _derived_l1a(inputs)
    m = {"L1_attn_Wq": f32c(inputs["L1_attn_Wq"]),
         "L1_attn_Wkv": f32c(inputs["L1_attn_Wkv"]),
         "L1_attn_Wv": f32c(inputs["L1_attn_Wv"]),
         "L1_attn_Wo": f32c(inputs["L1_attn_Wo"]),
         "wk_pad": wk_pad, "qg_col": qg_col, "kg_col": kg_col,
         "L1_ffn_router_gate": f32c(inputs["L1_ffn_router_gate"])}
    m.update(CONSTS)
    r = _launch(nc3, m, xs)
    g_all = np.concatenate([r.results[c]["g_loc_out"] for c in range(NCORES)],
                           axis=0)
    xs = [r.results[c]["x_out"] for c in range(NCORES)]

    nc4 = _get_prog("l1f", ["l1_ffn"], WS_L1F)
    m = {k: f32c(inputs[k]) for k in WS_L1F}
    m["g_all_in"] = np.ascontiguousarray(g_all)
    m.update(CONSTS)
    r = _launch(nc4, m, xs)

    out = np.empty((16, SEQ, D), np.float32)
    for c in range(NCORES):
        out[2 * c:2 * c + 2] = r.results[c]["x_out"].reshape(2, SEQ, D)
    return out
